# version 05
# baseline (speedup 1.0000x reference)
import contextlib
import numpy as np
import concourse.bass as bass
import concourse.mybir as mybir
from concourse.bass_utils import run_bass_kernel_spmd
from concourse.aot_env import allow_ici_all_to_all

F32 = mybir.dt.float32
BF16 = mybir.dt.bfloat16
AF = mybir.ActivationFunctionType
ALU = mybir.AluOpType
AX = mybir.AxisListType

D = 2048
NF = D // 128
T = 1024
CH = 512
NB = 2
NCHK = 8
NH = 16
DI = 4096
NFB = DI // 128
NST = 16
NSA = 8
FF = 8192
NFF = FF // 128
EPS_LN = 1e-5
EPS_RMS = 1e-6
BIG_NEG = -1.0e30
BIAS_OFF = 60.0
QSCALE = float(1.0 / np.sqrt(128.0))

N_PHASES = 99


def _chunk_owner(c):
    return (c, 0) if c <= 3 else (7 - c, 1)


def _rank_of(b, c):
    r, s = _chunk_owner(c)
    return 4 * b + r, s


class Sem:
    def __init__(self, nc, es, name):
        self.h = es.enter_context(nc.semaphore(name))
        self.n = 0


class Queue:

    def __init__(self, bd, eng):
        self.bd = bd
        self.eng = eng
        self.sems = [bd.sem(f"q_{eng}_a"), bd.sem(f"q_{eng}_b")]
        self.epoch = 0
        self.guarded = False
        self.totals = []

    def cur(self):
        return self.sems[self.epoch % 2]

    def dma(self, out, in_, transpose=False):
        self.pending = getattr(self, "pending", 0) + 1
        if not self.guarded:
            if self.epoch >= 2:
                si, tot = self.totals[self.epoch - 2]
                self.bd.wait(self.eng, self.sems[si], tot)
            self.guarded = True
        s = self.cur()
        s.n += 16

        def fn(e, out=out, in_=in_, h=s.h, tr=transpose):
            kw = {"transpose": True} if tr else {}
            e.dma_start(out=out, in_=in_, **kw).then_inc(h, 16)

        self.bd.emit(self.eng, fn)

    def flip(self):
        if getattr(self, "pending", 0) == 0:
            return self.epoch - 1
        s = self.cur()
        self.totals.append((self.epoch % 2, s.n))
        tok = self.epoch
        self.epoch += 1
        self.guarded = False
        self.pending = 0
        return tok

    def wait(self, eng, tok=None):
        if tok is None:
            if getattr(self, "pending", 0):
                self.flip()
            last = [0, 0]
            for si, tot in self.totals:
                last[si] = max(last[si], tot)
            for si in (0, 1):
                if last[si] > 0:
                    self.bd.wait(eng, self.sems[si], last[si])
            return
        if tok < 0 or tok >= len(self.totals):
            return
        si, tot = self.totals[tok]
        self.bd.wait(eng, self.sems[si], tot)


class Builder:
    ENGS = ("sync", "tensor", "vector", "scalar", "gpsimd")

    def __init__(self, nc, es):
        self.nc = nc
        self.es = es
        self.ops = {e: [] for e in self.ENGS}
        self.sems = {}

    def sem(self, name):
        if name not in self.sems:
            self.sems[name] = Sem(self.nc, self.es, name)
        return self.sems[name]

    def emit(self, eng, fn):
        self.ops[eng].append(fn)

    def wait(self, eng, sem, val=None):
        v = sem.n if val is None else val
        if v <= 0:
            return
        self.emit(eng, lambda e, h=sem.h, v=v: e.wait_ge(h, v))

    CHAIN = {"vector": "ev", "scalar": "ea", "gpsimd": "eg"}

    def op(self, eng, meth, *args, sem=None, seq=True, **kwargs):
        if eng in self.CHAIN:
            own = self.sem(self.CHAIN[eng])
            assert sem is None or sem is own, "use the engine chain sem"
            if seq and own.n > 0:
                self.wait(eng, own, own.n)
            own.n += 1

            def fn(e, a=args, k=kwargs, h=own.h):
                getattr(e, meth)(*a, **k).then_inc(h, 1)
        elif sem is not None:
            sem.n += 1

            def fn(e, a=args, k=kwargs, h=sem.h):
                getattr(e, meth)(*a, **k).then_inc(h, 1)
        else:
            def fn(e, a=args, k=kwargs):
                getattr(e, meth)(*a, **k)

        self.emit(eng, fn)

    def collective(self, name, kind, op_, groups, ins, outs):
        s = self.sem(name)
        s.n += 1

        def fn(e, kind=kind, op_=op_, groups=groups, ins=ins, outs=outs, h=s.h):
            with allow_ici_all_to_all():
                e.collective_compute(kind, op_, replica_groups=groups,
                                     ins=ins, outs=outs).then_inc(h)

        self.emit("gpsimd", fn)
        return s


class Carver:
    def __init__(self, tensor, nbytes):
        self.t = tensor
        self.nbytes = nbytes
        self.off = 0
        self.marks = []
        self.peak = 0

    def take(self, rows, cols, dtype, align=64):
        isz = 4 if dtype == F32 else 2
        self.off = (self.off + align - 1) // align * align
        start = self.off
        self.off += cols * isz
        self.peak = max(self.peak, self.off)
        assert self.off <= self.nbytes, f"scratch overflow: {self.off} > {self.nbytes}"
        base = self.t.ap().bitcast(dtype)
        return base[0:rows, start // isz: start // isz + cols]

    def push(self):
        self.marks.append(self.off)

    def pop(self):
        self.off = self.marks.pop()


def build_program(n_phases=99):
    nc = bass.Bass("TRN2", target_bir_lowering=False)

    def inp(name, shape):
        return nc.dram_tensor(name, list(shape), F32, kind="ExternalInput").ap()

    x0 = inp("x0", (D, T))
    a_xprojT = inp("a_xprojT", (D, 272))
    a_dtT = inp("a_dtT", (256, D))
    a_bdt = inp("a_bdt", (NF, 128, 1))
    a_alog = inp("a_alog", (NF, 128, NSA))
    a_Dp = inp("a_Dp", (NF, 128, 1))
    b_inprojT = inp("b_inprojT", (D, 10240))
    b_dtinT = inp("b_dtinT", (D, 128))
    b_dtprojT = inp("b_dtprojT", (128, DI))
    b_dtb = inp("b_dtb", (NFB, 128, 1))
    b_convw = inp("b_convw", (NFB, 128, 4))
    b_convb = inp("b_convb", (NFB, 128, 1))
    b_alog = inp("b_alog", (NFB, 128, NST))
    b_Dp = inp("b_Dp", (NFB, 128, 1))
    b_outprojT = inp("b_outprojT", (DI, D))
    f_WqT = inp("f_WqT", (D, D))
    f_WkT = inp("f_WkT", (D, D))
    f_WvT = inp("f_WvT", (D, D))
    f_Wo = inp("f_Wo", (D, D))
    f_A = inp("f_A", (D, NST))
    f_B = inp("f_B", (NST, FF))
    f_ln1 = inp("f_ln1", (NF, 128, 1))
    f_ln2 = inp("f_ln2", (NF, 128, 1))
    f_WinT = inp("f_WinT", (D, FF))
    f_WupT = inp("f_WupT", (D, FF))
    f_WdownT = inp("f_WdownT", (FF, D))
    selx_d = inp("selx", (32, 128))
    selb_d = inp("selb", (NST, 32, 128))
    selc_d = inp("selc", (NST, 128, 128))
    selone_d = inp("selone", (16, 16, 128))
    trimask_d = inp("trimask", (4, 128, CH))

    y0 = nc.dram_tensor("y0", [D, T], F32, kind="ExternalOutput").ap()

    kvq_send = nc.dram_tensor("kvq_send", [8, 6, 128, T], BF16)
    kvq_recv = nc.dram_tensor("kvq_recv", [8, 6, 128, T], BF16)
    o_send = nc.dram_tensor("o_send", [8, 2, 128, T], BF16)
    o_recv = nc.dram_tensor("o_recv", [8, 2, 128, T], BF16)
    a_pub = nc.dram_tensor("a_pub", [2, D + D * NSA], F32)
    a_gath = nc.dram_tensor("a_gath", [16, D + D * NSA], F32, addr_space="Shared")
    a_sel_send = nc.dram_tensor("a_sel_send", [8, 2, D * NSA], F32)
    a_sel_recv = nc.dram_tensor("a_sel_recv", [8, 2, D * NSA], F32)
    bnd_pub = nc.dram_tensor("bnd_pub", [2, 1024, 4], F32)
    bnd_gath = nc.dram_tensor("bnd_gath", [16, 1024, 4], F32, addr_space="Shared")
    bnd_send = nc.dram_tensor("bnd_send", [8, 2, 1024, 4], F32)
    bnd_recv = nc.dram_tensor("bnd_recv", [8, 2, 1024, 4], F32)
    b_pub = [nc.dram_tensor(f"b_pub{h}", [2, 2048 + 2048 * NST], F32) for h in range(2)]
    b_gathb = [nc.dram_tensor(f"b_gath{h}", [16, 2048 + 2048 * NST], F32,
                              addr_space="Shared") for h in range(2)]
    b_sel_send = [nc.dram_tensor(f"b_sel_send{h}", [8, 2, 2048 * NST], F32)
                  for h in range(2)]
    b_sel_recv = [nc.dram_tensor(f"b_sel_recv{h}", [8, 2, 2048 * NST], F32)
                  for h in range(2)]
    xrb_d = nc.dram_tensor("xrb_d", [2048, T], BF16)
    b01_d = nc.dram_tensor("b01_d", [NST, FF], BF16)

    es = contextlib.ExitStack()
    bd = Builder(nc, es)

    x_sb = nc.alloc_sbuf_tensor("x_sb", [128, NF * T], F32)
    big1 = nc.alloc_sbuf_tensor("big1", [128, NF * T], BF16)
    big2 = nc.alloc_sbuf_tensor("big2", [128, NF * T], BF16)
    SCR_BYTES = 81600
    scr_t = nc.alloc_sbuf_tensor("scr", [128, SCR_BYTES], mybir.dt.uint8)
    scr = Carver(scr_t, SCR_BYTES)
    bigc = Carver(big2, NF * T * 2)

    def xt(f):
        return x_sb.ap()[:, T * f: T * (f + 1)]

    def b1(f):
        return big1.ap()[:, T * f: T * (f + 1)]

    def b2(f):
        return big2.ap()[:, T * f: T * (f + 1)]

    selx_s = scr.take(32, 128, BF16)
    selb_s = [scr.take(32, 128, BF16) for _ in range(NST)]
    selc_s = [scr.take(128, 128, BF16) for _ in range(NST)]
    ones_bf = scr.take(128, 128, BF16)
    ones_f1 = scr.take(1, 128, F32)
    ones_f = scr.take(128, CH, F32)
    CONSTVALS = [0.0, 1.0, -1.0, -0.5, EPS_LN, EPS_RMS, BIAS_OFF, -BIAS_OFF, 0.5]
    const_tile = scr.take(128, len(CONSTVALS), F32)
    for ci_, cv_ in enumerate(CONSTVALS):
        nc.const_aps.register(F32, cv_, const_tile[:, ci_:ci_ + 1])
    wc_s = [scr.take(128, NST, BF16) for _ in range(NH)]
    a_bdt_s = scr.take(128, NF, F32)
    a_alog_s = scr.take(128, NF * NSA, F32)
    a_Dp_s = scr.take(128, NF, F32)
    b_dtb_s = scr.take(128, NFB, F32)
    b_convw_s = scr.take(128, NFB * 4, F32)
    b_convb_s = scr.take(128, NFB, F32)
    b_alog_s = scr.take(128, NFB * NST, F32)
    b_Dp_s = scr.take(128, NFB, F32)
    f_ln1_s = scr.take(128, NF, F32)
    f_ln2_s = scr.take(128, NF, F32)
    f_A_s = [scr.take(128, NST, BF16) for _ in range(NF)]
    rows = scr.take(1, 2 * T, F32)

    def row(i, tt=None):
        if tt is None:
            return rows[:, i * T:(i + 1) * T]
        return rows[:, i * T + tt * CH: i * T + (tt + 1) * CH]

    pb = [nc.alloc_psum_tensor(f"pb{i}", [128, CH], F32) for i in range(8)]

    def pbank(i, rows_=128, cols=CH):
        return pb[i].ap()[0:rows_, 0:cols]

    def pbank_bf(i, sl=None):
        v = pb[i].ap().bitcast(BF16)
        if sl is None:
            return v[0:128, 0:2 * CH]
        return v[0:128, sl * CH:(sl + 1) * CH]

    qs = Queue(bd, "sync")
    qg = Queue(bd, "gpsimd")
    qa = Queue(bd, "scalar")
    qv = Queue(bd, "vector")

    G8 = [list(range(8))]

    ev = bd.sem("ev")
    ea = bd.sem("ea")
    ep = bd.sem("ep")
    eg = bd.sem("eg")
    esy = bd.sem("esy")

    def barrier():
        qg.wait("gpsimd")
        qa.wait("scalar")
        qs.wait("sync")
        bd.wait("tensor", ep, ep.n)
        bd.wait("sync", esy, esy.n)
        bd.op("gpsimd", "nop", sem=eg)
        bd.op("scalar", "nop", sem=ea)
        bd.op("sync", "nop", sem=esy)
        bd.op("vector", "nop", sem=ev)
        bd.op("tensor", "nop", sem=ep)
        marks = [(ev, "vector"), (ea, "scalar"), (ep, "tensor"),
                 (eg, "gpsimd"), (esy, "sync")]
        for s, _ in marks:
            for _, eng in marks:
                bd.wait(eng, s, s.n)

    def _w(eng, sems):
        for s in sems:
            if isinstance(s, Queue):
                s.wait(eng)
            else:
                bd.wait(eng, s)

    def vwait(*sems):
        _w("vector", sems)

    def awaitx(*sems):
        _w("scalar", sems)

    def pwait(*sems):
        _w("tensor", sems)

    def gwait(*sems):
        _w("gpsimd", sems)

    def swait(*sems):
        _w("sync", sems)

    def phase0():
        qg.dma(selx_s, selx_d)
        for n in range(NST):
            qg.dma(selb_s[n], selb_d[n])
            qg.dma(selc_s[n], selc_d[n])
        for f in range(NF):
            qs.dma(a_bdt_s[:, f:f + 1], a_bdt[f])
            qs.dma(a_alog_s[:, NSA * f:NSA * (f + 1)], a_alog[f])
            qs.dma(a_Dp_s[:, f:f + 1], a_Dp[f])
            qs.dma(f_ln1_s[:, f:f + 1], f_ln1[f])
            qs.dma(f_ln2_s[:, f:f + 1], f_ln2[f])
            qg.dma(f_A_s[f], f_A[128 * f:128 * (f + 1), :])
        for f in range(NFB):
            qs.dma(b_dtb_s[:, f:f + 1], b_dtb[f])
            qs.dma(b_convw_s[:, 4 * f:4 * (f + 1)], b_convw[f])
            qs.dma(b_convb_s[:, f:f + 1], b_convb[f])
            qs.dma(b_alog_s[:, NST * f:NST * (f + 1)], b_alog[f])
            qs.dma(b_Dp_s[:, f:f + 1], b_Dp[f])
        for f in range(NF):
            qs.dma(xt(f), x0[128 * f:128 * (f + 1), :])
        scr.push()
        FQ = FF // 4
        tmpB = scr.take(NST, FQ, F32)
        tmpB2 = scr.take(NST, FQ, BF16)
        vwait(qs, qg)
        bd.op("vector", "memset", ones_bf, 1.0)
        bd.op("vector", "memset", ones_f1, 1.0)
        bd.op("vector", "memset", ones_f, 1.0)
        for ci_, cv_ in enumerate(CONSTVALS):
            bd.op("vector", "memset", const_tile[:, ci_:ci_ + 1], cv_)
        bd.op("vector", "engine_nop", sem=ev)
        awaitx(ev)
        for q4 in range(4):
            swait(ev)
            qs.dma(tmpB, f_B[:, FQ * q4:FQ * (q4 + 1)])
            vwait(qs)
            bd.op("vector", "tensor_scalar_mul", tmpB2, tmpB, 0.1, sem=ev)
            swait(ev)
            qs.dma(b01_d.ap()[:, FQ * q4:FQ * (q4 + 1)], tmpB2)
        scr.push()
        atmp = scr.take(128, NFB * NST, F32)
        awaitx(qs)
        bd.op("scalar", "activation", atmp[:, 0:NF * NSA], a_alog_s, AF.Exp, sem=ea)
        vwait(ea)
        bd.op("vector", "tensor_scalar_mul", a_alog_s, atmp[:, 0:NF * NSA], -1.0, sem=ev)
        awaitx(ev)
        bd.op("scalar", "activation", atmp, b_alog_s, AF.Exp, sem=ea)
        vwait(ea)
        bd.op("vector", "tensor_scalar_mul", b_alog_s, atmp, -1.0, sem=ev)
        scr.pop()
        barrier()
        woT = [scr.take(128, 128, BF16) for _ in range(2)]
        nmm = 0
        for h in range(NH):
            for k in range(NF):
                buf = woT[(h * NF + k) % 2]
                if h * NF + k >= 2:
                    gwait(ep)
                qg.dma(buf, f_Wo[128 * k:128 * (k + 1),
                                           128 * h:128 * (h + 1)])
                pwait(qg)
                bd.op("tensor", "matmul", pbank(0, 128, NST), buf, f_A_s[k],
                      start=(k == 0), stop=(k == NF - 1), sem=ep)
                nmm += 1
            awaitx(ep)
            bd.op("scalar", "activation", wc_s[h], pbank(0, 128, NST), AF.Copy,
                  sem=ea)
            pwait(ea)
        scr.pop()
        barrier()

    def ln_stats(tag, with_rms, s1seg=None):
        scr.push()
        xb = [scr.take(128, T, BF16) for _ in range(2)]
        xb2 = [scr.take(128, T, BF16) for _ in range(2)]
        for f in range(NF):
            b = f % 2
            if f >= 2:
                vwait(ep)
            bd.op("vector", "tensor_copy", xb[b], xt(f), sem=ev)
            awaitx(ev)
            bd.op("scalar", "activation", xb2[b], xb[b], AF.Square, sem=ea)
            pwait(ev, ea)
            for tt in range(2):
                bd.op("tensor", "matmul", pbank(0 if tt == 0 else 6, 1, CH),
                      ones_bf[:, 0:1], xb[b][:, tt * CH:(tt + 1) * CH],
                      start=(f == 0), stop=(f == NF - 1), sem=None)
                bd.op("tensor", "matmul", pbank(1 if tt == 0 else 7, 1, CH),
                      ones_bf[:, 0:1], xb2[b][:, tt * CH:(tt + 1) * CH],
                      start=(f == 0), stop=(f == NF - 1), sem=ep)
        scr.pop()
        scr.push()
        tmp = scr.take(1, T, F32)
        t2 = scr.take(1, T, F32)
        vwait(ep)
        for tt in range(2):
            sb, qb = (0, 1) if tt == 0 else (6, 7)
            bd.op("vector", "tensor_scalar_mul", row(0, tt), pbank(sb, 1, CH), 1.0 / D)
            bd.op("vector", "tensor_scalar_mul", row(1, tt), pbank(qb, 1, CH), 1.0 / D)
        bd.op("vector", "tensor_mul", tmp, row(0), row(0))
        bd.op("vector", "tensor_sub", row(1), row(1), tmp, sem=ev)
        awaitx(ev)
        bd.op("scalar", "activation", tmp, row(1), AF.Ln, bias=EPS_LN, sem=ea)
        if with_rms:
            vwait(ea)
            bd.op("vector", "tensor_scalar_add", t2, row(1), EPS_LN)
            bd.op("vector", "reciprocal", t2, t2)
            bd.op("vector", "tensor_mul", t2, row(1), t2, sem=ev)
            awaitx(ev)
        bd.op("scalar", "activation", row(1), tmp, AF.Exp, scale=-0.5, sem=ea)
        if with_rms:
            awaitx(ev)
            bd.op("scalar", "activation", t2, t2, AF.Ln, bias=EPS_RMS)
            bd.op("scalar", "activation", t2, t2, AF.Exp, scale=-0.5, sem=ea)
            vwait(ea)
            bd.op("vector", "tensor_mul", s1seg, row(1), t2, sem=ev)
        scr.pop()

    def bcast_row(bank, rseg):
        bd.op("tensor", "matmul", pbank(bank), ones_f1[:, :], rseg,
              start=True, stop=True, sem=ep)

    def phase1():
        scr.push()
        bigc.push()
        selone_s = [scr.take(16, 128, BF16) for _ in range(16)]
        aw_xproj = [bigc.take(128, 272, BF16) for _ in range(NF)]
        aw_dt = [[bigc.take(128, 128, BF16) for _ in range(NF)] for _ in range(2)]
        for n in range(16):
            qg.dma(selone_s[n], selone_d[n])
        for f in range(NF):
            qg.dma(aw_xproj[f], a_xprojT[128 * f:128 * (f + 1), :])
            qg.dma(aw_dt[0][f], a_dtT[0:128, 128 * f:128 * (f + 1)])
            qg.dma(aw_dt[1][f], a_dtT[128:256, 128 * f:128 * (f + 1)])
        pwait(qg)
        vwait(qg)
        ln_stats("p1", with_rms=False)
        pwait(ev, ea)
        bcast_row(2, row(0, 0))
        bcast_row(3, row(1, 0))
        bcast_row(4, row(0, 1))
        bcast_row(5, row(1, 1))
        MU, RS = [2, 4], [3, 5]
        xc = scr.take(128, CH, F32)
        xa32 = scr.take(128, CH, F32)
        vwait(ep)
        for f in range(NF):
            for tt in range(2):
                sl = slice(tt * CH, (tt + 1) * CH)
                bd.op("vector", "tensor_sub", xc, xt(f)[:, sl], pbank(MU[tt]))
                bd.op("vector", "tensor_mul", xa32, xc, pbank(RS[tt]))
                bd.op("vector", "tensor_copy", b1(f)[:, sl], xa32)
                bd.op("vector", "scalar_tensor_tensor", xt(f)[:, sl], xa32,
                      a_Dp_s[:, f:f + 1], xt(f)[:, sl], ALU.mult, ALU.add, sem=ev)
        dtr = [bigc.take(128, T, BF16) for _ in range(2)]
        bca = scr.take(16, T, BF16)
        for tt in range(2):
            for mb in range(3):
                mrows = 128 if mb < 2 else 16
                pwait(ev, ea)
                for k in range(NF):
                    bd.op("tensor", "matmul", pbank(6, mrows, CH),
                          aw_xproj[k][:, 128 * mb:128 * mb + mrows],
                          b1(k)[:, tt * CH:(tt + 1) * CH],
                          start=(k == 0), stop=(k == NF - 1),
                          sem=(ep if k == NF - 1 else None))
                awaitx(ep)
                dst = (dtr[mb][:, tt * CH:(tt + 1) * CH] if mb < 2
                      else bca[:, tt * CH:(tt + 1) * CH])
                bd.op("scalar", "activation", dst, pbank(6, mrows, CH), AF.Copy,
                      sem=ea)
        delta = bigc.take(128, T, F32)
        sp1 = scr.take(128, CH, F32)
        sp2 = scr.take(128, CH, F32)
        fstage = scr.take(128, 2 * NF * NSA, F32)
        dstage = scr.take(128, 2 * NF, F32)
        bigc.push()
        scr.push()
        ub = bigc.take(128, T, BF16)
        at_ = scr.take(128, T, F32)
        bt_ = scr.take(128, T, F32)
        Ht = bigc.take(128, T, F32)
        at2 = [at_, scr.t.ap().bitcast(F32)[0:128, sp1.offset:sp1.offset + T]]
        for f in range(NF):
            for tt in range(2):
                pwait(ev, ea)
                for k in range(2):
                    bd.op("tensor", "matmul", pbank(0),
                          aw_dt[k][f], dtr[k][:, tt * CH:(tt + 1) * CH],
                          start=(k == 0), stop=(k == 1),
                          sem=(ep if k == 1 else None))
                awaitx(ep, ev)
                bd.op("scalar", "activation", sp1, pbank(0), AF.Abs,
                      bias=a_bdt_s[:, f:f + 1])
                bd.op("scalar", "activation", sp1, sp1, AF.Exp, scale=-1.0)
                bd.op("scalar", "activation", sp1, sp1, AF.Ln, bias=1.0)
                bd.op("scalar", "activation", sp2, pbank(0), AF.Relu,
                      bias=a_bdt_s[:, f:f + 1], sem=ea)
                vwait(ea)
                bd.op("vector", "scalar_tensor_tensor",
                      delta[:, tt * CH:(tt + 1) * CH], sp1, 1.0, sp2,
                      ALU.mult, ALU.add,
                      accum_out=dstage[:, NF * tt + f: NF * tt + f + 1], sem=ev)
                bd.op("vector", "tensor_mul", ub[:, tt * CH:(tt + 1) * CH],
                      delta[:, tt * CH:(tt + 1) * CH],
                      b1(f)[:, tt * CH:(tt + 1) * CH], sem=ev)
            mka, mkB, mkC = {}, {}, {}
            _pv = bd.sems["ev"].n
            _pg = bd.sems["eg"].n
            mkscan = {-2: _pv, -1: _pv}
            mkb = {-2: _pv, -1: _pv}
            mkym = {-2: _pv, -1: _pv}
            mkg = {-2: _pg, -1: _pg}
            pwait(ea)

            def prep1B(n, f=f):
                BP = (1, 2) if n % 2 == 0 else (3, 4)
                bd.wait("scalar", ev, mkscan[n - 2])
                bd.op("scalar", "activation", at2[n % 2], delta, AF.Exp,
                      scale=a_alog_s[:, NSA * f + n: NSA * f + n + 1], sem=ea)
                mka[n] = ea.n
                bd.wait("tensor", ev, mkb[n - 2])
                for tt in range(2):
                    bd.op("tensor", "matmul", pbank(BP[tt]),
                          selone_s[n][0:16, :], bca[0:16, tt * CH:(tt + 1) * CH],
                          start=True, stop=True, sem=ep)
                mkB[n] = ep.n

            def prep1C(n, f=f):
                bd.wait("tensor", ev, mkym[n - 1])
                for tt in range(2):
                    bd.op("tensor", "matmul", pbank(5 + tt),
                          selone_s[8 + n][0:16, :], bca[0:16, tt * CH:(tt + 1) * CH],
                          start=True, stop=True, sem=ep)
                mkC[n] = ep.n

            prep1B(0)
            prep1C(0)
            for n in range(NSA):
                if n + 1 < NSA:
                    prep1B(n + 1)
                BP = (1, 2) if n % 2 == 0 else (3, 4)
                bd.wait("vector", ea, mka[n])
                bd.wait("vector", ep, mkB[n])
                for tt in range(2):
                    bd.op("vector", "tensor_mul", bt_[:, tt * CH:(tt + 1) * CH],
                          ub[:, tt * CH:(tt + 1) * CH], pbank(BP[tt]))
                mkb[n] = bd.sems["ev"].n
                bd.wait("vector", eg, mkg[n - 1])
                for tt in range(2):
                    bd.op("vector", "tensor_tensor_scan",
                          Ht[:, tt * CH:(tt + 1) * CH],
                          at2[n % 2][:, tt * CH:(tt + 1) * CH],
                          bt_[:, tt * CH:(tt + 1) * CH],
                          0.0, ALU.mult, ALU.add)
                    bd.op("vector", "tensor_copy",
                          fstage[:, 128 * tt + NSA * f + n: 128 * tt + NSA * f + n + 1],
                          Ht[:, (tt + 1) * CH - 1:(tt + 1) * CH])
                mkscan[n] = bd.sems["ev"].n
                bd.wait("vector", ep, mkC[n])
                bd.wait("vector", eg, mkg[n - 1])
                for tt in range(2):
                    bd.op("vector", "tensor_mul", Ht[:, tt * CH:(tt + 1) * CH],
                          Ht[:, tt * CH:(tt + 1) * CH], pbank(5 + tt))
                mkym[n] = bd.sems["ev"].n
                bd.wait("gpsimd", ev, mkym[n])
                bd.op("gpsimd", "tensor_add", xt(f), xt(f), Ht)
                mkg[n] = bd.sems["eg"].n
                if n + 1 < NSA:
                    prep1C(n + 1)
        bigc.pop()
        scr.pop()
        barrier()
        awaitx(ev)
        for c in range(2):
            qa.dma(
                   a_pub[c, 0:D].rearrange("(p f) -> p f", p=128),
                   dstage[:, NF * c:NF * (c + 1)])
            qa.dma(
                   a_pub[c, D:].rearrange("(p k) -> p k", p=128),
                   fstage[:, 128 * c:128 * (c + 1)])
        gwait(qa)
        cc1 = bd.collective("cc_a_g", "AllGather", ALU.bypass, G8,
                            [a_pub.ap().opt()], [a_gath.ap().opt()])
        gwait(cc1)
        bigc.push()
        dts_roll = [bigc.take(128, NF, F32) for _ in range(2)]
        F_roll = [bigc.take(128, NF * NSA, F32) for _ in range(2)]
        cprev = bigc.take(128, NF * NSA, F32)
        ccur = bigc.take(128, NF * NSA, F32)
        Ptmp = bigc.take(128, NSA, F32)

        def s16_of(b2, c):
            r, s = _rank_of(b2, c)
            return 2 * r + s

        for b2 in range(2):
            for c in range(NCHK):
                cseg = ccur if c % 2 else cprev
                psg = cprev if c % 2 else ccur
                vwait(qa)
                if c == 0:
                    bd.op("vector", "memset", cseg, 0.0, sem=ev)
                else:
                    ps16 = s16_of(b2, c - 1)
                    gwait(ev)
                    qg.dma(dts_roll[c % 2],
                           a_gath[ps16, 0:D].rearrange("(p f) -> p f", p=128),
                           )
                    qg.dma(F_roll[c % 2],
                           a_gath[ps16, D:].rearrange("(p k) -> p k", p=128),
                           )
                    vwait(qg)
                    awaitx(qg, ev)
                    for f in range(NF):
                        awaitx(ev)
                        bd.op("scalar", "activation", Ptmp,
                              a_alog_s[:, NSA * f:NSA * (f + 1)], AF.Exp,
                              scale=dts_roll[c % 2][:, f:f + 1], sem=ea)
                        vwait(ea)
                        bd.op("vector", "tensor_mul",
                              cseg[:, NSA * f:NSA * (f + 1)],
                              psg[:, NSA * f:NSA * (f + 1)], Ptmp)
                        bd.op("vector", "tensor_add",
                              cseg[:, NSA * f:NSA * (f + 1)],
                              cseg[:, NSA * f:NSA * (f + 1)],
                              F_roll[c % 2][:, NSA * f:NSA * (f + 1)], sem=ev)
                if c <= 3:
                    j, ci = 4 * b2 + c, 0
                else:
                    j, ci = 4 * b2 + 7 - c, 1
                awaitx(ev)
                qa.dma(
                       a_sel_send[j, ci, :].rearrange("(p k) -> p k", p=128),
                       cseg)
                gwait(qa)
        gwait(qa)
        cc2 = bd.collective("cc_a_s", "AllToAll", ALU.bypass, G8,
                            [a_sel_send.ap().opt()], [a_sel_recv.ap().opt()])
        gwait(cc2)
        myc = [bigc.take(128, NF * NSA, F32) for _ in range(2)]
        for ci in range(2):
            qg.dma(myc[ci],
                   a_sel_recv[0, ci, :].rearrange("(p k) -> p k", p=128))
        DT = scr.take(128, T, F32)
        cpA = scr.take(128, T, F32)
        fxt = scr.take(128, T, F32)
        at2 = [cpA, scr.t.ap().bitcast(F32)[0:128, sp1.offset:sp1.offset + T]]
        vwait(qg)
        awaitx(qg)
        for f in range(NF):
            for tt in range(2):
                pwait(ev, ea)
                for k in range(2):
                    bd.op("tensor", "matmul", pbank(0),
                          aw_dt[k][f], dtr[k][:, tt * CH:(tt + 1) * CH],
                          start=(k == 0), stop=(k == 1),
                          sem=(ep if k == 1 else None))
                awaitx(ep, ev)
                bd.op("scalar", "activation", sp1, pbank(0), AF.Abs,
                      bias=a_bdt_s[:, f:f + 1])
                bd.op("scalar", "activation", sp1, sp1, AF.Exp, scale=-1.0)
                bd.op("scalar", "activation", sp1, sp1, AF.Ln, bias=1.0)
                bd.op("scalar", "activation", sp2, pbank(0), AF.Relu,
                      bias=a_bdt_s[:, f:f + 1], sem=ea)
                vwait(ea)
                bd.op("vector", "scalar_tensor_tensor",
                      delta[:, tt * CH:(tt + 1) * CH], sp1, 1.0, sp2,
                      ALU.mult, ALU.add, sem=ev)
                bd.op("vector", "tensor_tensor_scan",
                      DT[:, tt * CH:(tt + 1) * CH], ones_f,
                      delta[:, tt * CH:(tt + 1) * CH], 0.0,
                      ALU.mult, ALU.add, sem=ev)
            mka, mkC = {}, {}
            _pv = bd.sems["ev"].n
            _pg = bd.sems["eg"].n
            mkstt = {-2: _pv, -1: _pv}
            mkg1f = {-2: _pg, -1: _pg}
            pwait(ea)

            def prep1f(n, f=f):
                CP = (1, 2) if n % 2 == 0 else (3, 4)
                bd.wait("scalar", ev, mkstt[n - 2])
                bd.op("scalar", "activation", at2[n % 2], DT, AF.Exp,
                      scale=a_alog_s[:, NSA * f + n: NSA * f + n + 1], sem=ea)
                mka[n] = ea.n
                bd.wait("tensor", ev, mkstt[n - 2])
                for tt in range(2):
                    bd.op("tensor", "matmul", pbank(CP[tt]),
                          selone_s[8 + n][0:16, :], bca[0:16, tt * CH:(tt + 1) * CH],
                          start=True, stop=True, sem=ep)
                mkC[n] = ep.n

            prep1f(0)
            for n in range(NSA):
                if n + 1 < NSA:
                    prep1f(n + 1)
                CP = (1, 2) if n % 2 == 0 else (3, 4)
                bd.wait("vector", ea, mka[n])
                bd.wait("vector", ep, mkC[n])
                bd.wait("vector", eg, mkg1f[n - 1])
                for tt in range(2):
                    bd.op("vector", "scalar_tensor_tensor",
                          fxt[:, tt * CH:(tt + 1) * CH],
                          at2[n % 2][:, tt * CH:(tt + 1) * CH],
                          myc[tt][:, NSA * f + n: NSA * f + n + 1],
                          pbank(CP[tt]), ALU.mult, ALU.mult, sem=ev)
                mkstt[n] = bd.sems["ev"].n
                bd.wait("gpsimd", ev, mkstt[n])
                bd.op("gpsimd", "tensor_add", xt(f), xt(f), fxt)
                mkg1f[n] = bd.sems["eg"].n
        bigc.pop()
        bigc.pop()
        scr.pop()
        barrier()


    def phase_ffn(tag):
        scr.push()
        trimask_s = [scr.take(128, CH, BF16) for _ in range(4)]
        for s in range(4):
            qg.dma(trimask_s[s], trimask_d[s])
        qg.flip()
        vwait(qg)
        s1row = scr.take(1, T, F32)
        brow = scr.take(1, CH, F32)
        lrow = scr.take(1, CH, F32)
        ln_stats(tag, with_rms=True, s1seg=s1row)
        pwait(ev, ea)
        bcast_row(2, row(0, 0))
        bcast_row(3, s1row[:, 0:CH])
        bcast_row(4, row(0, 1))
        bcast_row(5, s1row[:, CH:2 * CH])
        MU, S1 = [2, 4], [3, 5]
        xc = scr.take(128, CH, F32)
        vwait(ep)
        for f in range(NF):
            for tt in range(2):
                sl = slice(tt * CH, (tt + 1) * CH)
                bd.op("vector", "tensor_sub", xc, xt(f)[:, sl], pbank(MU[tt]))
                bd.op("vector", "scalar_tensor_tensor", b1(f)[:, sl], xc,
                      f_ln1_s[:, f:f + 1], pbank(S1[tt]), ALU.mult, ALU.mult,
                      sem=ev)
        wbuf = [scr.take(128, NF * 128, BF16) for _ in range(2)]
        wbuf2 = [scr.take(128, NF * 128, BF16) for _ in range(2)]
        qkvt = [scr.take(128, CH, BF16) for _ in range(4)]
        WT = [f_WqT, f_WkT, f_WvT]
        gwait(ep, ev, ea)
        nblk = 0
        for qkv in range(3):
            for mb in range(NH):
                wb = wbuf[nblk % 2]
                if nblk >= 2:
                    gwait(ep)
                qg.dma(wb.rearrange("p (k m) -> p k m", k=NF),
                       WT[qkv][:, 128 * mb:128 * (mb + 1)]
                       .rearrange("(k p) m -> p k m", p=128))
                qg.flip()
                pwait(qg)
                if nblk == 0:
                    pwait(ev)
                for tt in range(2):
                    bank = tt
                    pwait(ea)
                    for k in range(NF):
                        bd.op("tensor", "matmul", pbank(bank),
                              wb[:, 128 * k:128 * (k + 1)],
                              b1(k)[:, tt * CH:(tt + 1) * CH],
                              start=(k == 0), stop=(k == NF - 1),
                              sem=(ep if k == NF - 1 else None))
                    awaitx(ep)
                    if nblk >= 2:
                        qa.wait("scalar", qa.epoch - 2)
                    qt = qkvt[2 * (nblk % 2) + tt]
                    sc = QSCALE if qkv == 0 else 1.0
                    bd.op("scalar", "activation", qt, pbank(bank), AF.Copy,
                          scale=sc, sem=ea)
                    awaitx(ea)
                    s6 = 2 * qkv + (mb % 2)
                    qa.dma(kvq_send.ap()[mb // 2, s6, :, tt * CH:(tt + 1) * CH],
                           qt)
                qa.flip()
                nblk += 1
        gwait(qa)
        gwait(ev, ea)
        cc_kvq = bd.collective("cc_kvq_" + tag, "AllToAll", ALU.bypass, G8,
                               [kvq_send.ap().opt()], [kvq_recv.ap().opt()])
        gwait(cc_kvq)
        bd.wait("sync", cc_kvq, 1)
        bigc.push()
        qtile = [bigc.take(128, CH, BF16) for _ in range(2)]
        ktile = [bigc.take(128, CH, BF16) for _ in range(3)]
        vtile = [bigc.take(128, CH, BF16) for _ in range(3)]
        ptile = [bigc.take(128, CH, BF16) for _ in range(2)]
        tmpbf = bigc.take(128, CH, BF16)
        linv_sb = bigc.take(128, CH, F32)
        o_sb = [bigc.take(128, CH, BF16) for _ in range(2)]
        SB = [0, 1]
        nq = 0

        def load_kv(buf_i, b2, kc, hh):
            rk2, sl2 = _rank_of(b2, kc)
            qs.dma(ktile[buf_i],
                   kvq_recv.ap()[rk2, 2 + hh, :, sl2 * CH:(sl2 + 1) * CH])
            for s4 in range(4):
                qs.dma(vtile[buf_i][:, 128 * s4:128 * (s4 + 1)],
                       kvq_recv.ap()[rk2, 4 + hh, :,
                                     sl2 * CH + 128 * s4: sl2 * CH + 128 * (s4 + 1)],
                       transpose=True)
            return qs.flip()

        for b2 in range(2):
            for hh in range(2):
                for qrow in range(NCHK):
                    rk, sl_ = _rank_of(b2, qrow)
                    qb = qtile[nq % 2]
                    bd.wait("sync", ev, ev.n)
                    bd.wait("sync", ep, ep.n)
                    qs.dma(qb, kvq_recv.ap()[rk, hh, :, sl_ * CH:(sl_ + 1) * CH])
                    tok = load_kv(qrow % 3, b2, qrow, hh)
                    vwait(qs)
                    bd.op("vector", "tensor_mul", tmpbf, qb,
                          ktile[qrow % 3], sem=ev)
                    pwait(ev, ea)
                    bd.op("tensor", "matmul", pbank(4, 1, CH), ones_bf[:, 0:1],
                          tmpbf, start=True, stop=True, sem=ep)
                    vwait(ep)
                    bd.op("vector", "tensor_scalar", brow, pbank(4, 1, CH),
                          -1.0, -BIAS_OFF, ALU.mult, ALU.add, sem=ev)
                    nsubs = 4 * (qrow + 1)
                    act_mark = {}
                    pv_mark = {-2: 0, -1: 0}
                    toks = None
                    subq = []
                    for kc_ in range(qrow, -1, -1):
                        for s4_ in range(4):
                            subq.append((kc_, s4_))

                    def emit_qk(i):
                        kc, s4 = subq[i]
                        sbank = SB[i % 2]
                        if i >= 2:
                            bd.wait("tensor", ea, act_mark[i - 2])
                        bd.op("tensor", "matmul", pbank(sbank),
                              ones_f1[:, :], brow,
                              start=True, stop=False)
                        bd.op("tensor", "matmul", pbank(sbank),
                              ktile[kc % 3][:, 128 * s4:128 * (s4 + 1)], qb,
                              start=False, stop=True, sem=ep)
                        return ep.n

                    def emit_exp(i, qkm):
                        kc, s4 = subq[i]
                        sbank = SB[i % 2]
                        if kc == qrow:
                            bd.wait("vector", ep, qkm)
                            bd.op("vector", "tensor_add", pbank(sbank),
                                  pbank(sbank), trimask_s[s4], sem=ev)
                            awaitx(ev)
                        else:
                            bd.wait("scalar", ep, qkm)
                        bd.wait("scalar", ep, pv_mark[i - 2])
                        bd.op("scalar", "activation", ptile[i % 2], pbank(sbank),
                              AF.Exp, sem=ea)
                        act_mark[i] = ea.n

                    def emit_lpv(i):
                        kc, s4 = subq[i]
                        first = (i == 0)
                        last = (i == nsubs - 1)
                        bd.wait("tensor", ea, act_mark[i])
                        bd.op("tensor", "matmul", pbank(3, 1, CH),
                              ones_bf[:, 0:1], ptile[i % 2],
                              start=first, stop=last, sem=None)
                        bd.op("tensor", "matmul", pbank(2),
                              vtile[kc % 3][:, 128 * s4:128 * (s4 + 1)],
                              ptile[i % 2],
                              start=first, stop=last, sem=ep)
                        pv_mark[i] = ep.n

                    pwait(ev)
                    prev_kc = None
                    for i in range(nsubs):
                        kc, s4 = subq[i]
                        if i >= 1:
                            emit_lpv(i - 1)
                        if kc != prev_kc:
                            if kc != qrow:
                                qs.wait("tensor", toks)
                                qs.wait("scalar", toks)
                            if kc > 0:
                                bd.wait("sync", ep, ep.n)
                                toks = load_kv((kc - 1) % 3, b2, kc - 1, hh)
                            prev_kc = kc
                        qkm = emit_qk(i)
                        emit_exp(i, qkm)
                    emit_lpv(nsubs - 1)
                    vwait(ep)
                    bd.op("vector", "reciprocal", lrow, pbank(3, 1, CH),
                          sem=ev)
                    pwait(ev)
                    bd.op("tensor", "matmul", pbank(4), ones_f1[:, :],
                          lrow, start=True, stop=True, sem=ep)
                    awaitx(ep)
                    bd.op("scalar", "activation", linv_sb, pbank(4), AF.Copy,
                          sem=ea)
                    vwait(ea)
                    if nq >= 2:
                        qa.wait("vector", qa.epoch - 2)
                    osb = o_sb[nq % 2]
                    bd.op("vector", "tensor_mul", osb, pbank(2), linv_sb,
                          sem=ev)
                    awaitx(ev)
                    qa.dma(o_send.ap()[rk, hh, :, sl_ * CH:(sl_ + 1) * CH], osb)
                    qa.flip()
                    nq += 1
        gwait(qa)
        gwait(ev, ea)
        cc_o = bd.collective("cc_o_" + tag, "AllToAll", ALU.bypass, G8,
                             [o_send.ap().opt()], [o_recv.ap().opt()])
        gwait(cc_o)
        bd.wait("sync", cc_o, 1)
        o2 = [bigc.take(128, CH, BF16) for _ in range(2)]
        tanh_sb = bigc.take(16, T, F32)
        modh_bf = scr.take(16, T, BF16)
        aA_mark = {}
        nh2 = 0
        for tt in range(2):
            for h in range(NH):
                ob = o2[nh2 % 2]
                if nh2 >= 2:
                    bd.wait("sync", ep, aA_mark[nh2 - 2])
                qs.dma(ob, o_recv.ap()[h // 2, h % 2, :, tt * CH:(tt + 1) * CH])
                qs.flip()
                pwait(qs)
                if nh2 == 0:
                    pwait(ea, ev)
                bd.op("tensor", "matmul", pbank(5 + tt, 16, CH), wc_s[h], ob,
                      start=(h == 0), stop=(h == NH - 1), sem=ep)
                aA_mark[nh2] = ep.n
                nh2 += 1
        awaitx(ep)
        for tt in range(2):
            bd.op("scalar", "activation", tanh_sb[:, tt * CH:(tt + 1) * CH],
                  pbank(5 + tt, 16, CH), AF.Tanh, sem=ea)
        pwait(ev, ea)
        bcast_row(2, row(0, 0))
        bcast_row(3, s1row[:, 0:CH])
        bcast_row(4, row(0, 1))
        bcast_row(5, s1row[:, CH:2 * CH])
        vwait(ep)
        for f in range(NF):
            for tt in range(2):
                sl = slice(tt * CH, (tt + 1) * CH)
                bd.op("vector", "tensor_sub", xc, xt(f)[:, sl], pbank(MU[tt]))
                bd.op("vector", "scalar_tensor_tensor", b1(f)[:, sl], xc,
                      f_ln2_s[:, f:f + 1], pbank(S1[tt]), ALU.mult, ALU.mult,
                      sem=ev)
        for tt in range(2):
            pwait(ev)
            pwait(ea)
            for k in range(NF):
                bd.op("tensor", "matmul", pbank(5 + tt, 16, CH), f_A_s[k],
                      b1(k)[:, tt * CH:(tt + 1) * CH],
                      start=(k == 0), stop=(k == NF - 1),
                      sem=(ep if k == NF - 1 else None))
            vwait(ep)
            bd.op("vector", "tensor_mul", modh_bf[:, tt * CH:(tt + 1) * CH],
                  tanh_sb[:, tt * CH:(tt + 1) * CH], pbank(5 + tt, 16, CH),
                  sem=ev)
        barrier()
        bigc.pop()
        bigc.push()
        gu = bigc.take(128, 16 * 2 * CH, BF16)

        def gusl(fb, tt):
            o = (fb * 2 + tt) * CH
            return gu[:, o:o + CH]

        wbufd = scr.take(128, 16 * 128, BF16)
        b01t = [scr.take(16, 128, BF16) for _ in range(2)]
        gate_bf = scr.take(128, CH, BF16)
        nfbq = 0

        def mlp_load(i):
            gwait(ep)
            qg.dma(wbuf[i % 2].rearrange("p (k m) -> p k m", k=NF),
                   f_WinT[:, 128 * i:128 * (i + 1)]
                   .rearrange("(k p) m -> p k m", p=128))
            qg.dma(b01t[i % 2], b01_d.ap()[:, 128 * i:128 * (i + 1)])
            qg.dma(wbuf2[i % 2].rearrange("p (k m) -> p k m", k=NF),
                   f_WupT[:, 128 * i:128 * (i + 1)]
                   .rearrange("(k p) m -> p k m", p=128))
            return qg.flip()

        pend2 = mlp_load(0)
        for quarter in range(4):
            for fbq in range(16):
                fb = 16 * quarter + fbq
                wbi = wbuf[nfbq % 2]
                wbu2 = wbuf2[nfbq % 2]
                tok = pend2
                if fb + 1 < NFF:
                    pend2 = mlp_load(fb + 1)
                qg.wait("tensor", tok)
                for tt in range(2):
                    gb = (2 * nfbq + tt) % 2
                    ub = 2 + (2 * nfbq + tt) % 2
                    pwait(ea, ev)
                    for k in range(NF):
                        bd.op("tensor", "matmul", pbank(gb),
                              wbi[:, 128 * k:128 * (k + 1)],
                              b1(k)[:, tt * CH:(tt + 1) * CH],
                              start=(k == 0), stop=False)
                    bd.op("tensor", "matmul", pbank(gb),
                          b01t[nfbq % 2][0:16, :],
                          modh_bf[0:16, tt * CH:(tt + 1) * CH],
                          start=False, stop=True, sem=ep)
                    for k in range(NF):
                        bd.op("tensor", "matmul", pbank(ub),
                              wbu2[:, 128 * k:128 * (k + 1)],
                              b1(k)[:, tt * CH:(tt + 1) * CH],
                              start=(k == 0), stop=(k == NF - 1),
                              sem=(ep if k == NF - 1 else None))
                    awaitx(ep)
                    bd.op("scalar", "activation", gate_bf, pbank(gb), AF.Silu,
                          sem=ea)
                    vwait(ea)
                    bd.op("vector", "tensor_mul", gusl(fbq, tt), gate_bf,
                          pbank(ub), sem=ev)
                nfbq += 1
            for m in range(NF):
                gwait(ep)
                qg.dma(wbufd.rearrange("p (k mm) -> p k mm", k=16),
                       f_WdownT[2048 * quarter:2048 * (quarter + 1),
                                128 * m:128 * (m + 1)]
                       .rearrange("(k p) mm -> p k mm", p=128))
                qg.flip()
                pwait(qg)
                for tt in range(2):
                    dbank = 4 + tt
                    sl = slice(tt * CH, (tt + 1) * CH)
                    pwait(ev)
                    for k in range(16):
                        bd.op("tensor", "matmul", pbank(dbank),
                              wbufd[:, 128 * k:128 * (k + 1)], gusl(k, tt),
                              start=(k == 0), stop=(k == 15),
                              sem=(ep if k == 15 else None))
                    if quarter == 0:
                        bcast_row(6, row(0, tt))
                        bcast_row(7, row(1, tt))
                    vwait(ep)
                    if quarter == 0:
                        bd.op("vector", "tensor_sub", xc, xt(m)[:, sl],
                              pbank(6))
                        bd.op("vector", "tensor_mul", xc, xc, pbank(7))
                        bd.op("vector", "tensor_add", xc, xc, pbank(dbank))
                        bd.op("vector", "tensor_add", xt(m)[:, sl],
                              xt(m)[:, sl], xc, sem=ev)
                    else:
                        bd.op("vector", "tensor_add", xt(m)[:, sl],
                              xt(m)[:, sl], pbank(dbank), sem=ev)
        bigc.pop()
        scr.pop()
        barrier()


    def phase_b():
        scr.push()
        ln_stats("pb", with_rms=False)
        pwait(ev, ea)
        bcast_row(2, row(0, 0))
        bcast_row(3, row(1, 0))
        bcast_row(4, row(0, 1))
        bcast_row(5, row(1, 1))
        MU, RS = [2, 4], [3, 5]
        xc = scr.take(128, CH, F32)
        vwait(ep)
        for f in range(NF):
            for tt in range(2):
                sl = slice(tt * CH, (tt + 1) * CH)
                bd.op("vector", "tensor_sub", xc, xt(f)[:, sl], pbank(MU[tt]))
                bd.op("vector", "tensor_mul", b1(f)[:, sl], xc, pbank(RS[tt]),
                      sem=ev)
        wbuf = [scr.take(128, NF * 128, BF16) for _ in range(2)]
        evt = [scr.take(128, CH, BF16) for _ in range(2)]
        dtin_sb = scr.take(128, T, BF16)
        gwait(ep, ev, ea)
        qg.dma(wbuf[0].rearrange("p (k m) -> p k m", k=NF),
               b_dtinT.rearrange("(k p) m -> p k m", p=128))
        qg.flip()
        pwait(qg)
        pwait(ev)
        for tt in range(2):
            pwait(ea)
            for k in range(NF):
                bd.op("tensor", "matmul", pbank(0),
                      wbuf[0][:, 128 * k:128 * (k + 1)],
                      b1(k)[:, tt * CH:(tt + 1) * CH],
                      start=(k == 0), stop=(k == NF - 1),
                      sem=(ep if k == NF - 1 else None))
            awaitx(ep)
            bd.op("scalar", "activation", dtin_sb[:, tt * CH:(tt + 1) * CH],
                  pbank(0), AF.Copy, sem=ea)
        nblk = 0
        for mb in range(16):
            wb = wbuf[nblk % 2]
            gwait(ep)
            qg.dma(wb.rearrange("p (k m) -> p k m", k=NF),
                   b_inprojT[:, 128 * (32 + mb):128 * (33 + mb)]
                   .rearrange("(k p) m -> p k m", p=128))
            qg.flip()
            pwait(qg)
            for tt in range(2):
                bank = tt
                pwait(ea)
                for k in range(NF):
                    bd.op("tensor", "matmul", pbank(bank),
                          wb[:, 128 * k:128 * (k + 1)],
                          b1(k)[:, tt * CH:(tt + 1) * CH],
                          start=(k == 0), stop=(k == NF - 1),
                          sem=(ep if k == NF - 1 else None))
                awaitx(ep)
                if nblk >= 1:
                    qa.wait("scalar", qa.epoch - 1)
                et = evt[tt]
                bd.op("scalar", "activation", et, pbank(bank), AF.Copy, sem=ea)
                awaitx(ea)
                qa.dma(xrb_d.ap()[128 * mb:128 * (mb + 1),
                                  tt * CH:(tt + 1) * CH], et)
            qa.flip()
            nblk += 1
        zz_sb = scr.take(128, 32, BF16)
        bd.op("vector", "memset", zz_sb, 0.0, sem=ev)
        zz_d = nc.dram_tensor("zz_" + "b", [1024, 4], BF16)
        vwait(ev)
        gwait(qa)
        gwait(ev)
        qg.dma(zz_d.ap().rearrange("(a p) c -> p a c", p=128), 
               zz_sb.rearrange("p (a c) -> p a c", c=4))
        for c in range(2):
            qg.dma(bnd_pub.ap()[c], xrb_d.ap()[0:1024, c * CH + 508:c * CH + 512])
        qg.flip()
        gwait(qg)
        ccb1 = bd.collective("cc_bnd_g", "AllGather", ALU.bypass, G8,
                             [bnd_pub.ap().opt()], [bnd_gath.ap().opt()])
        gwait(ccb1)
        for j in range(8):
            b2 = j // 4
            for ci, c in enumerate((j % 4, 7 - j % 4)):
                if c == 0:
                    qg.dma(bnd_send.ap()[j, ci], zz_d.ap())
                else:
                    rk, sl2 = _rank_of(b2, c - 1)
                    qg.dma(bnd_send.ap()[j, ci], bnd_gath.ap()[2 * rk + sl2])
        qg.flip()
        gwait(qg)
        ccb2 = bd.collective("cc_bnd_s", "AllToAll", ALU.bypass, G8,
                             [bnd_send.ap().opt()], [bnd_recv.ap().opt()])
        gwait(ccb2)
        bnd_sb = [scr.take(32, 32 * 4, BF16) for _ in range(2)]
        for ci in range(2):
            qg.dma(bnd_sb[ci].rearrange("p (w c) -> p w c", c=4),
                   bnd_recv.ap()[0, ci].rearrange("(w p) c -> p w c", p=32))
        qg.flip()
        bdtp_sb = scr.take(128, 16 * 128, BF16)
        dt_fb = scr.take(128, T, F32)
        xact_bf = scr.take(128, T, BF16)
        u_bf = scr.take(128, T, BF16)
        _xoff = xact_bf.offset
        fixsum = scr.t.ap().bitcast(F32)[0:128, _xoff // 2:_xoff // 2 + T]
        at_ = scr.take(128, T, F32)
        bt_ = scr.take(128, T, F32)
        Ht = scr.take(128, T, F32)
        C_sb = scr.take(128, T, BF16)
        z_bf = scr.take(128, T, BF16)
        xrw = [scr.take(32, T, BF16)]
        Bw = [scr.take(32, T, BF16)]
        dstage_b = scr.take(128, 2 * 16, F32)
        fstage_b = scr.take(128, 2 * 256, F32)
        F_roll = [scr.take(128, 256, F32) for _ in range(2)]
        myc = F_roll
        cprev = bt_[:, 0:256]
        ccur = bt_[:, 256:512]
        Ptmp = bt_[:, 512:528]
        dts_roll = [bt_[:, 528 + 16 * i:544 + 16 * i] for i in range(2)]

        def zc_mms(fb, dst_bf, rowbase, silu):
            nonlocal_wb = wbuf[fb % 2]
            gwait(ep)
            qg.dma(nonlocal_wb.rearrange("p (k m) -> p k m", k=NF),
                   b_inprojT[:, rowbase + 128 * fb: rowbase + 128 * (fb + 1)]
                   .rearrange("(k p) m -> p k m", p=128))
            qg.flip()
            pwait(qg)
            pwait(ev)
            for tt in range(2):
                pwait(ea)
                for k in range(NF):
                    bd.op("tensor", "matmul", pbank(3),
                          nonlocal_wb[:, 128 * k:128 * (k + 1)],
                          b1(k)[:, tt * CH:(tt + 1) * CH],
                          start=(k == 0), stop=(k == NF - 1),
                          sem=(ep if k == NF - 1 else None))
                awaitx(ep)
                bd.op("scalar", "activation", dst_bf[:, tt * CH:(tt + 1) * CH],
                      pbank(3), AF.Silu if silu else AF.Copy, sem=ea)

        bdtp_tok = [None]

        def dt_softplus(fb, want_dtsum):
            fbq = fb % 16
            qs_ = qg.wait("tensor", bdtp_tok[0])
            for tt in range(2):
                pwait(ea, ev)
                bd.op("tensor", "matmul", pbank(0),
                      bdtp_sb[:, 128 * (fb % 16):128 * (fb % 16 + 1)],
                      dtin_sb[:, tt * CH:(tt + 1) * CH],
                      start=True, stop=True, sem=ep)
                awaitx(ep, ev)
                bd.op("scalar", "activation", sp1b, pbank(0), AF.Abs,
                      bias=b_dtb_s[:, fb:fb + 1])
                bd.op("scalar", "activation", sp1b, sp1b, AF.Exp, scale=-1.0)
                bd.op("scalar", "activation", sp1b, sp1b, AF.Ln, bias=1.0)
                bd.op("scalar", "activation", sp2b, pbank(0), AF.Relu,
                      bias=b_dtb_s[:, fb:fb + 1], sem=ea)
                vwait(ea)
                kw = {}
                if want_dtsum:
                    kw["accum_out"] = dstage_b[:, 16 * tt + fbq: 16 * tt + fbq + 1]
                bd.op("vector", "scalar_tensor_tensor",
                      dt_fb[:, tt * CH:(tt + 1) * CH], sp1b, 1.0, sp2b,
                      ALU.mult, ALU.add, sem=ev, **kw)

        _reg = scr.take(128, 1040, F32)
        sp1b = _reg[:, 0:512]
        sp2b = _reg[:, 512:1024]
        xpad = [_reg[:, 0:520]]
        cacc = _reg[:, 520:1032]
        at2 = [at_, _reg[:, 0:T]]
        def ysl_ap(k, tt):
            o = (2 * k + tt) * CH
            return big2.ap()[:, o:o + CH]

        nfb3 = 0
        qa.wait("sync")
        for half in range(2):
            gwait(ep)
            qg.dma(bdtp_sb, b_dtprojT[:, 2048 * half:2048 * (half + 1)])
            bdtp_tok[0] = qg.flip()
            vwait(qa)
            for fbq in range(16):
                fb = 16 * half + fbq
                xw = xrw[0]
                bw = Bw[0]
                bd.wait("sync", ep, ep.n)
                bd.wait("sync", ev, ev.n)
                qs.dma(xw, xrb_d.ap()[32 * fb:32 * (fb + 1), :])
                qs.dma(bw, xrb_d.ap()[1024 + 32 * fb:1024 + 32 * (fb + 1), :])
                qs.flip()
                dt_softplus(fb, True)
                pwait(qs, qg)
                for c in range(2):
                    pwait(ea)
                    bd.op("tensor", "matmul", pbank(1),
                          selx_s[0:32, :], xw[0:32, c * CH:(c + 1) * CH],
                          start=True, stop=True)
                    bd.op("tensor", "matmul", pbank(2, 128, 4),
                          selx_s[0:32, :], bnd_sb[c][0:32, 4 * fb:4 * (fb + 1)],
                          start=True, stop=True, sem=ep)
                    awaitx(ep, ev)
                    awaitx(ev)
                    bd.op("scalar", "activation", xpad[0][:, 4:516], pbank(1),
                          AF.Copy)
                    bd.op("scalar", "activation", xpad[0][:, 0:4],
                          pbank(2, 128, 4), AF.Copy, sem=ea)
                    vwait(ea)
                    bd.op("vector", "tensor_scalar",
                          cacc, xpad[0][:, 1:513],
                          b_convw_s[:, 4 * fb:4 * fb + 1], None, ALU.mult)
                    for kk in range(1, 4):
                        bd.op("vector", "scalar_tensor_tensor", cacc,
                              xpad[0][:, 1 + kk:513 + kk],
                              b_convw_s[:, 4 * fb + kk:4 * fb + kk + 1],
                              cacc, ALU.mult, ALU.add)
                    bd.op("vector", "engine_nop", sem=ev)
                    awaitx(ev)
                    bd.op("scalar", "activation",
                          xact_bf[:, c * CH:(c + 1) * CH], cacc, AF.Silu,
                          bias=b_convb_s[:, fb:fb + 1], sem=ea)
                vwait(ea, ev)
                bd.op("vector", "tensor_mul", u_bf, dt_fb, xact_bf, sem=ev)
                zc_mms(fb, C_sb, 128 * 48, False)
                mka, mkB, mkC = {}, {}, {}
                _pv = bd.sems["ev"].n
                mkscan = {-2: _pv, -1: _pv}
                mkb = {-2: _pv, -1: _pv}
                mkym = {-2: _pv, -1: _pv}
                pwait(ea)

                def prep_nB(n, fb=fb):
                    BP = (1, 2) if n % 2 == 0 else (3, 4)
                    bd.wait("scalar", ev, mkscan[n - 2])
                    bd.op("scalar", "activation", at2[n % 2], dt_fb, AF.Exp,
                          scale=b_alog_s[:, NST * fb + n: NST * fb + n + 1],
                          sem=ea)
                    mka[n] = ea.n
                    bd.wait("tensor", ev, mkb[n - 2])
                    for tt in range(2):
                        bd.op("tensor", "matmul", pbank(BP[tt]),
                              selb_s[n][0:32, :],
                              bw[0:32, tt * CH:(tt + 1) * CH],
                              start=True, stop=True, sem=ep)
                    mkB[n] = ep.n

                def prep_nC(n, fb=fb):
                    bd.wait("tensor", ev, mkym[n - 1])
                    for tt in range(2):
                        bd.op("tensor", "matmul", pbank(5 if tt == 0 else 0),
                              selc_s[n], C_sb[:, tt * CH:(tt + 1) * CH],
                              start=True, stop=True, sem=ep)
                    mkC[n] = ep.n

                prep_nB(0)
                prep_nC(0)
                for n in range(NST):
                    if n + 1 < NST:
                        prep_nB(n + 1)
                    BP = (1, 2) if n % 2 == 0 else (3, 4)
                    bd.wait("vector", ea, mka[n])
                    bd.wait("vector", ep, mkB[n])
                    for tt in range(2):
                        bd.op("vector", "tensor_mul", bt_[:, tt * CH:(tt + 1) * CH],
                              u_bf[:, tt * CH:(tt + 1) * CH], pbank(BP[tt]))
                    mkb[n] = bd.sems["ev"].n
                    for tt in range(2):
                        bd.op("vector", "tensor_tensor_scan",
                              Ht[:, tt * CH:(tt + 1) * CH],
                              at2[n % 2][:, tt * CH:(tt + 1) * CH],
                              bt_[:, tt * CH:(tt + 1) * CH],
                              0.0, ALU.mult, ALU.add)
                        bd.op("vector", "tensor_copy",
                              fstage_b[:, 256 * tt + 16 * fbq + n:
                                       256 * tt + 16 * fbq + n + 1],
                              Ht[:, (tt + 1) * CH - 1:(tt + 1) * CH])
                    mkscan[n] = bd.sems["ev"].n
                    bd.wait("vector", ep, mkC[n])
                    for tt in range(2):
                        bd.op("vector", "tensor_mul", Ht[:, tt * CH:(tt + 1) * CH],
                              Ht[:, tt * CH:(tt + 1) * CH],
                              pbank(5 if tt == 0 else 0))
                    if n == 0:
                        for tt in range(2):
                            bd.op("vector", "tensor_copy", pbank(6 + tt),
                                  Ht[:, tt * CH:(tt + 1) * CH], sem=ev)
                    else:
                        for tt in range(2):
                            bd.op("vector", "tensor_add", pbank(6 + tt),
                                  pbank(6 + tt), Ht[:, tt * CH:(tt + 1) * CH],
                                  sem=ev)
                    mkym[n] = bd.sems["ev"].n
                    if n + 1 < NST:
                        prep_nC(n + 1)
                zc_mms(fb, z_bf, 0, True)
                vwait(ea)
                for tt in range(2):
                    sl = slice(tt * CH, (tt + 1) * CH)
                    bd.op("vector", "scalar_tensor_tensor", Ht[:, sl],
                          xact_bf[:, sl], b_Dp_s[:, fb:fb + 1], pbank(6 + tt),
                          ALU.mult, ALU.add)
                    bd.op("vector", "tensor_mul", ysl_ap(fbq, tt),
                          Ht[:, sl], z_bf[:, sl], sem=ev)
                nfb3 += 1
            awaitx(ev)
            for c in range(2):
                qa.dma(b_pub[half].ap()[c, 0:2048]
                       .rearrange("(p k) -> p k", p=128),
                       dstage_b[:, 16 * c:16 * (c + 1)])
                qa.dma(b_pub[half].ap()[c, 2048:]
                       .rearrange("(p k) -> p k", p=128),
                       fstage_b[:, 256 * c:256 * (c + 1)])
            qa.flip()
            gwait(qa)
            ccg = bd.collective(f"cc_b_g{half}_", "AllGather", ALU.bypass, G8,
                                [b_pub[half].ap().opt()],
                                [b_gathb[half].ap().opt()])
            gwait(ccg)

            def s16_of(b2, c):
                r, s = _rank_of(b2, c)
                return 2 * r + s

            for b2 in range(2):
                for c in range(NCHK):
                    cseg = ccur if c % 2 else cprev
                    psg = cprev if c % 2 else ccur
                    vwait(qa)
                    if c == 0:
                        bd.op("vector", "memset", cseg, 0.0, sem=ev)
                    else:
                        ps16 = s16_of(b2, c - 1)
                        gwait(ev)
                        qg.dma(dts_roll[c % 2],
                               b_gathb[half].ap()[ps16, 0:2048]
                               .rearrange("(p k) -> p k", p=128))
                        qg.dma(F_roll[c % 2],
                               b_gathb[half].ap()[ps16, 2048:]
                               .rearrange("(p k) -> p k", p=128))
                        qg.flip()
                        vwait(qg)
                        awaitx(qg, ev)
                        for fbq in range(16):
                            fb = 16 * half + fbq
                            awaitx(ev)
                            bd.op("scalar", "activation", Ptmp,
                                  b_alog_s[:, NST * fb:NST * (fb + 1)], AF.Exp,
                                  scale=dts_roll[c % 2][:, fbq:fbq + 1], sem=ea)
                            vwait(ea)
                            bd.op("vector", "tensor_mul",
                                  cseg[:, 16 * fbq:16 * (fbq + 1)],
                                  psg[:, 16 * fbq:16 * (fbq + 1)], Ptmp)
                            bd.op("vector", "tensor_add",
                                  cseg[:, 16 * fbq:16 * (fbq + 1)],
                                  cseg[:, 16 * fbq:16 * (fbq + 1)],
                                  F_roll[c % 2][:, 16 * fbq:16 * (fbq + 1)],
                                  sem=ev)
                    awaitx(ev)
                    if c <= 3:
                        j, ci = 4 * b2 + c, 0
                    else:
                        j, ci = 4 * b2 + 7 - c, 1
                    qa.dma(b_sel_send[half].ap()[j, ci]
                           .rearrange("(p k) -> p k", p=128), cseg)
                    qa.flip()
            gwait(qa)
            ccs = bd.collective(f"cc_b_s{half}_", "AllToAll", ALU.bypass, G8,
                                [b_sel_send[half].ap().opt()],
                                [b_sel_recv[half].ap().opt()])
            gwait(ccs)
            gwait(ev)
            for ci in range(2):
                qg.dma(myc[ci],
                       b_sel_recv[half].ap()[0, ci]
                       .rearrange("(p k) -> p k", p=128))
            qg.flip()
            vwait(qg)
            awaitx(qg)
            DTb = dt_fb
            for fbq in range(16):
                fb = 16 * half + fbq
                dt_softplus(fb, False)
                for tt in range(2):
                    vwait(ev)
                    bd.op("vector", "tensor_tensor_scan",
                          Ht[:, tt * CH:(tt + 1) * CH], ones_f,
                          dt_fb[:, tt * CH:(tt + 1) * CH], 0.0,
                          ALU.mult, ALU.add, sem=ev)
                zc_mms(fb, C_sb, 128 * 48, False)
                mka, mkC = {}, {}
                _pv = bd.sems["ev"].n
                _pg = bd.sems["eg"].n
                mkstt = {-2: _pv, -1: _pv}
                mkg3f = {-2: _pg, -1: _pg}
                pwait(ea)

                def prep_fx(n, fb=fb):
                    CP = (1, 2) if n % 2 == 0 else (3, 4)
                    bd.wait("scalar", ev, mkstt[n - 2])
                    bd.op("scalar", "activation", at2[n % 2], Ht, AF.Exp,
                          scale=b_alog_s[:, NST * fb + n: NST * fb + n + 1],
                          sem=ea)
                    mka[n] = ea.n
                    bd.wait("tensor", ev, mkstt[n - 2])
                    for tt in range(2):
                        bd.op("tensor", "matmul", pbank(CP[tt]),
                              selc_s[n], C_sb[:, tt * CH:(tt + 1) * CH],
                              start=True, stop=True, sem=ep)
                    mkC[n] = ep.n

                prep_fx(0)
                for n in range(NST):
                    if n + 1 < NST:
                        prep_fx(n + 1)
                    CP = (1, 2) if n % 2 == 0 else (3, 4)
                    bd.wait("vector", ea, mka[n])
                    bd.wait("vector", ep, mkC[n])
                    bd.wait("vector", eg, mkg3f[n - 1])
                    for tt in range(2):
                        bd.op("vector", "scalar_tensor_tensor",
                              bt_[:, tt * CH:(tt + 1) * CH],
                              at2[n % 2][:, tt * CH:(tt + 1) * CH],
                              myc[tt][:, 16 * fbq + n: 16 * fbq + n + 1],
                              pbank(CP[tt]), ALU.mult, ALU.mult, sem=ev)
                    mkstt[n] = bd.sems["ev"].n
                    bd.wait("gpsimd", ev, mkstt[n])
                    if n == 0:
                        bd.op("gpsimd", "tensor_copy", fixsum, bt_)
                    else:
                        bd.op("gpsimd", "tensor_add", fixsum, fixsum, bt_)
                    mkg3f[n] = bd.sems["eg"].n
                zc_mms(fb, z_bf, 0, True)
                vwait(ea)
                vwait(eg)
                for tt in range(2):
                    sl = slice(tt * CH, (tt + 1) * CH)
                    ysl = ysl_ap(fbq, tt)
                    bd.op("vector", "tensor_mul", bt_[:, sl], fixsum[:, sl],
                          z_bf[:, sl])
                    bd.op("vector", "tensor_add", ysl, ysl, bt_[:, sl], sem=ev)
            for m in range(NF):
                wb = wbuf[m % 2]
                gwait(ep)
                qg.dma(wb.rearrange("p (k mm) -> p k mm", k=NF),
                       b_outprojT[2048 * half:2048 * (half + 1),
                                  128 * m:128 * (m + 1)]
                       .rearrange("(k p) mm -> p k mm", p=128))
                qg.flip()
                pwait(qg)
                pwait(ev)
                for tt in range(2):
                    dbank = 4 + tt
                    sl = slice(tt * CH, (tt + 1) * CH)
                    pwait(ev)
                    for k in range(16):
                        bd.op("tensor", "matmul", pbank(dbank),
                              wb[:, 128 * k:128 * (k + 1)], ysl_ap(k, tt),
                              start=(k == 0), stop=(k == 15),
                              sem=(ep if k == 15 else None))
                    vwait(ep)
                    bd.op("vector", "tensor_add", xt(m)[:, sl], xt(m)[:, sl],
                          pbank(dbank), sem=ev)
        scr.pop()
        barrier()

    def phase_out():
        swait(ev, ea, ep, eg)
        for f in range(NF):
            qs.dma(y0[128 * f:128 * (f + 1), :], xt(f))
        qs.wait("sync")

    phases = [phase0, phase1, lambda: phase_ffn('f1'), phase_b,
              lambda: phase_ffn('f2')]
    for i, ph in enumerate(phases):
        if i < n_phases:
            ph()
    phase_out()

    block = es.enter_context(nc.Block())

    @block.sync
    def _(e):
        for fn in bd.ops["sync"]:
            fn(e)

    @block.tensor
    def _(e):
        for fn in bd.ops["tensor"]:
            fn(e)

    @block.vector
    def _(e):
        for fn in bd.ops["vector"]:
            fn(e)

    @block.scalar
    def _(e):
        for fn in bd.ops["scalar"]:
            fn(e)

    @block.gpsimd
    def _(e):
        for fn in bd.ops["gpsimd"]:
            fn(e)

    es.close()
    return nc



def _host_constants():
    selx = np.zeros((32, 128), np.float32)
    for ch in range(128):
        selx[(ch // 64) * 16 + (ch % 16), ch] = 1.0
    selb = np.zeros((NST, 32, 128), np.float32)
    for n in range(NST):
        for ch in range(128):
            selb[n, (ch // 64) * 16 + n, ch] = 1.0
    selc = np.zeros((NST, 128, 128), np.float32)
    for n in range(NST):
        for ch in range(128):
            selc[n, (ch // 16) * 16 + n, ch] = 1.0
    selone = np.zeros((16, 16, 128), np.float32)
    for n in range(16):
        selone[n, n, :] = 1.0
    trimask = np.zeros((4, 128, CH), np.float32)
    for s in range(4):
        for k in range(128):
            kg = 128 * s + k
            trimask[s, k, :kg] = BIG_NEG
    return selx, selb, selc, selone, trimask


def prepare_inputs(full):
    f32 = np.float32
    cT = np.ascontiguousarray
    x = np.asarray(full["L0_attn_a_x"], f32)
    selx, selb, selc, selone, trimask = _host_constants()
    shared = {
        "a_xprojT": cT(np.asarray(full["L0_attn_a_W_xproj"], f32).T),
        "a_dtT": cT(np.asarray(full["L0_attn_a_W_dt"], f32).T),
        "a_bdt": cT(np.asarray(full["L0_attn_a_b_dt"], f32).reshape(NF, 128, 1)),
        "a_alog": cT(np.asarray(full["L0_attn_a_A_log"], f32).reshape(NF, 128, NSA)),
        "a_Dp": cT(np.asarray(full["L0_attn_a_D_param"], f32).reshape(NF, 128, 1)),
        "b_inprojT": cT(np.asarray(full["L0_attn_b_in_proj_w"], f32).T),
        "b_dtinT": cT(np.asarray(full["L0_attn_b_dt_in_proj_w"], f32).T),
        "b_dtprojT": cT(np.asarray(full["L0_attn_b_dt_proj_w"], f32).T),
        "b_dtb": cT(np.asarray(full["L0_attn_b_dt_proj_b"], f32).reshape(NFB, 128, 1)),
        "b_convw": cT(np.asarray(full["L0_attn_b_conv_w"], f32).reshape(NFB, 128, 4)),
        "b_convb": cT(np.asarray(full["L0_attn_b_conv_b"], f32).reshape(NFB, 128, 1)),
        "b_alog": cT(np.asarray(full["L0_attn_b_A_log"], f32).reshape(NFB, 128, NST)),
        "b_Dp": cT(np.asarray(full["L0_attn_b_D"], f32).reshape(NFB, 128, 1)),
        "b_outprojT": cT(np.asarray(full["L0_attn_b_out_proj_w"], f32).T),
        "f_WqT": cT(np.asarray(full["L0_ffn_Wq"], f32).T),
        "f_WkT": cT(np.asarray(full["L0_ffn_Wk"], f32).T),
        "f_WvT": cT(np.asarray(full["L0_ffn_Wv"], f32).T),
        "f_Wo": cT(np.asarray(full["L0_ffn_Wo"], f32)),
        "f_A": cT(np.asarray(full["L0_ffn_A"], f32)),
        "f_B": cT(np.asarray(full["L0_ffn_B"], f32)),
        "f_ln1": cT(np.asarray(full["L0_ffn_ln1_w"], f32).reshape(NF, 128, 1)),
        "f_ln2": cT(np.asarray(full["L0_ffn_ln2_w"], f32).reshape(NF, 128, 1)),
        "f_WinT": cT(np.asarray(full["L0_ffn_W_in_base"], f32).T),
        "f_WupT": cT(np.asarray(full["L0_ffn_W_up_base"], f32).T),
        "f_WdownT": cT(np.asarray(full["L0_ffn_W_down"], f32).T),
        "selx": selx, "selb": selb, "selc": selc, "selone": selone,
        "trimask": trimask,
    }
    in_maps = []
    for c in range(8):
        b = c // 4
        jA, jB = c % 4, 7 - c % 4
        xc = np.concatenate([x[b, CH * jA:CH * (jA + 1)],
                             x[b, CH * jB:CH * (jB + 1)]], axis=0)
        m = {"x0": np.ascontiguousarray(xc.T)}
        m.update(shared)
        in_maps.append(m)
    return in_maps


def gather_output(results):
    y = np.empty((NB, 4096, D), np.float32)
    for c in range(8):
        b = c // 4
        jA, jB = c % 4, 7 - c % 4
        out = results[c]["y0"]
        y[b, CH * jA:CH * (jA + 1)] = out[:, :CH].T
        y[b, CH * jB:CH * (jB + 1)] = out[:, CH:].T
    return y


_NC_CACHE = {}


def kernel(**inputs):
    key = N_PHASES
    if key not in _NC_CACHE:
        _NC_CACHE[key] = build_program(N_PHASES)
    nc = _NC_CACHE[key]
    in_maps = prepare_inputs(inputs)
    res = run_bass_kernel_spmd(nc, in_maps, core_ids=list(range(8)))
    return gather_output(res.results)


# version 06
# speedup vs baseline: 1.0339x; 1.0339x over previous
import contextlib
import numpy as np
import concourse.bass as bass
import concourse.mybir as mybir
from concourse.bass_utils import run_bass_kernel_spmd
from concourse.aot_env import allow_ici_all_to_all

F32 = mybir.dt.float32
BF16 = mybir.dt.bfloat16
AF = mybir.ActivationFunctionType
ALU = mybir.AluOpType
AX = mybir.AxisListType

D = 2048
NF = D // 128
T = 1024
CH = 512
NB = 2
NCHK = 8
NH = 16
DI = 4096
NFB = DI // 128
NST = 16
NSA = 8
FF = 8192
NFF = FF // 128
EPS_LN = 1e-5
EPS_RMS = 1e-6
BIG_NEG = -1.0e30
BIAS_OFF = 60.0
QSCALE = float(1.0 / np.sqrt(128.0))

N_PHASES = 99


def _chunk_owner(c):
    return (c, 0) if c <= 3 else (7 - c, 1)


def _rank_of(b, c):
    r, s = _chunk_owner(c)
    return 4 * b + r, s


class Sem:
    def __init__(self, nc, es, name):
        self.h = es.enter_context(nc.semaphore(name))
        self.n = 0


class Queue:

    def __init__(self, bd, eng):
        self.bd = bd
        self.eng = eng
        self.sems = [bd.sem(f"q_{eng}_a"), bd.sem(f"q_{eng}_b")]
        self.epoch = 0
        self.guarded = False
        self.totals = []

    def cur(self):
        return self.sems[self.epoch % 2]

    def dma(self, out, in_, transpose=False):
        self.pending = getattr(self, "pending", 0) + 1
        if not self.guarded:
            if self.epoch >= 2:
                si, tot = self.totals[self.epoch - 2]
                self.bd.wait(self.eng, self.sems[si], tot)
            self.guarded = True
        s = self.cur()
        s.n += 16

        def fn(e, out=out, in_=in_, h=s.h, tr=transpose):
            kw = {"transpose": True} if tr else {}
            e.dma_start(out=out, in_=in_, **kw).then_inc(h, 16)

        self.bd.emit(self.eng, fn)

    def flip(self):
        if getattr(self, "pending", 0) == 0:
            return self.epoch - 1
        s = self.cur()
        self.totals.append((self.epoch % 2, s.n))
        tok = self.epoch
        self.epoch += 1
        self.guarded = False
        self.pending = 0
        return tok

    def wait(self, eng, tok=None):
        if tok is None:
            if getattr(self, "pending", 0):
                self.flip()
            last = [0, 0]
            for si, tot in self.totals:
                last[si] = max(last[si], tot)
            for si in (0, 1):
                if last[si] > 0:
                    self.bd.wait(eng, self.sems[si], last[si])
            return
        if tok < 0 or tok >= len(self.totals):
            return
        si, tot = self.totals[tok]
        self.bd.wait(eng, self.sems[si], tot)


class Builder:
    ENGS = ("sync", "tensor", "vector", "scalar", "gpsimd")

    def __init__(self, nc, es):
        self.nc = nc
        self.es = es
        self.ops = {e: [] for e in self.ENGS}
        self.sems = {}

    def sem(self, name):
        if name not in self.sems:
            self.sems[name] = Sem(self.nc, self.es, name)
        return self.sems[name]

    def emit(self, eng, fn):
        self.ops[eng].append(fn)

    def wait(self, eng, sem, val=None):
        v = sem.n if val is None else val
        if v <= 0:
            return
        self.emit(eng, lambda e, h=sem.h, v=v: e.wait_ge(h, v))

    CHAIN = {"vector": "ev", "scalar": "ea", "gpsimd": "eg"}

    def op(self, eng, meth, *args, sem=None, seq=True, **kwargs):
        if eng in self.CHAIN:
            own = self.sem(self.CHAIN[eng])
            assert sem is None or sem is own, "use the engine chain sem"
            if seq and own.n > 0:
                self.wait(eng, own, own.n)
            own.n += 1

            def fn(e, a=args, k=kwargs, h=own.h):
                getattr(e, meth)(*a, **k).then_inc(h, 1)
        elif sem is not None:
            sem.n += 1

            def fn(e, a=args, k=kwargs, h=sem.h):
                getattr(e, meth)(*a, **k).then_inc(h, 1)
        else:
            def fn(e, a=args, k=kwargs):
                getattr(e, meth)(*a, **k)

        self.emit(eng, fn)

    def collective(self, name, kind, op_, groups, ins, outs):
        s = self.sem(name)
        s.n += 1

        def fn(e, kind=kind, op_=op_, groups=groups, ins=ins, outs=outs, h=s.h):
            with allow_ici_all_to_all():
                e.collective_compute(kind, op_, replica_groups=groups,
                                     ins=ins, outs=outs).then_inc(h)

        self.emit("gpsimd", fn)
        return s


class Carver:
    def __init__(self, tensor, nbytes):
        self.t = tensor
        self.nbytes = nbytes
        self.off = 0
        self.marks = []
        self.peak = 0

    def take(self, rows, cols, dtype, align=64):
        isz = 4 if dtype == F32 else 2
        self.off = (self.off + align - 1) // align * align
        start = self.off
        self.off += cols * isz
        self.peak = max(self.peak, self.off)
        assert self.off <= self.nbytes, f"scratch overflow: {self.off} > {self.nbytes}"
        base = self.t.ap().bitcast(dtype)
        return base[0:rows, start // isz: start // isz + cols]

    def push(self):
        self.marks.append(self.off)

    def pop(self):
        self.off = self.marks.pop()


def build_program(n_phases=99):
    nc = bass.Bass("TRN2", target_bir_lowering=False)

    def inp(name, shape):
        return nc.dram_tensor(name, list(shape), F32, kind="ExternalInput").ap()

    x0 = inp("x0", (D, T))
    a_xprojT = inp("a_xprojT", (D, 272))
    a_dtT = inp("a_dtT", (256, D))
    a_bdt = inp("a_bdt", (NF, 128, 1))
    a_alog = inp("a_alog", (NF, 128, NSA))
    a_Dp = inp("a_Dp", (NF, 128, 1))
    b_inprojT = inp("b_inprojT", (D, 10240))
    b_dtinT = inp("b_dtinT", (D, 128))
    b_dtprojT = inp("b_dtprojT", (128, DI))
    b_dtb = inp("b_dtb", (NFB, 128, 1))
    b_convw = inp("b_convw", (NFB, 128, 4))
    b_convb = inp("b_convb", (NFB, 128, 1))
    b_alog = inp("b_alog", (NFB, 128, NST))
    b_Dp = inp("b_Dp", (NFB, 128, 1))
    b_outprojT = inp("b_outprojT", (DI, D))
    f_WqT = inp("f_WqT", (D, D))
    f_WkT = inp("f_WkT", (D, D))
    f_WvT = inp("f_WvT", (D, D))
    f_Wo = inp("f_Wo", (D, D))
    f_A = inp("f_A", (D, NST))
    f_B = inp("f_B", (NST, FF))
    f_ln1 = inp("f_ln1", (NF, 128, 1))
    f_ln2 = inp("f_ln2", (NF, 128, 1))
    f_WinT = inp("f_WinT", (D, FF))
    f_WupT = inp("f_WupT", (D, FF))
    f_WdownT = inp("f_WdownT", (FF, D))
    selx_d = inp("selx", (32, 128))
    selb_d = inp("selb", (NST, 32, 128))
    selc_d = inp("selc", (NST, 128, 128))
    selone_d = inp("selone", (16, 16, 128))
    trimask_d = inp("trimask", (4, 128, CH))

    y0 = nc.dram_tensor("y0", [D, T], F32, kind="ExternalOutput").ap()

    kvq_send = nc.dram_tensor("kvq_send", [8, 6, 128, T], BF16)
    kvq_recv = nc.dram_tensor("kvq_recv", [8, 6, 128, T], BF16)
    o_send = nc.dram_tensor("o_send", [8, 2, 128, T], BF16)
    o_recv = nc.dram_tensor("o_recv", [8, 2, 128, T], BF16)
    a_pub = nc.dram_tensor("a_pub", [2, D + D * NSA], F32)
    a_gath = nc.dram_tensor("a_gath", [16, D + D * NSA], F32, addr_space="Shared")
    a_sel_send = nc.dram_tensor("a_sel_send", [8, 2, D * NSA], F32)
    a_sel_recv = nc.dram_tensor("a_sel_recv", [8, 2, D * NSA], F32)
    bnd_pub = nc.dram_tensor("bnd_pub", [2, 1024, 4], F32)
    bnd_gath = nc.dram_tensor("bnd_gath", [16, 1024, 4], F32, addr_space="Shared")
    bnd_send = nc.dram_tensor("bnd_send", [8, 2, 1024, 4], F32)
    bnd_recv = nc.dram_tensor("bnd_recv", [8, 2, 1024, 4], F32)
    b_pub = [nc.dram_tensor(f"b_pub{h}", [2, 2048 + 2048 * NST], F32) for h in range(2)]
    b_gathb = [nc.dram_tensor(f"b_gath{h}", [16, 2048 + 2048 * NST], F32,
                              addr_space="Shared") for h in range(2)]
    b_sel_send = [nc.dram_tensor(f"b_sel_send{h}", [8, 2, 2048 * NST], F32)
                  for h in range(2)]
    b_sel_recv = [nc.dram_tensor(f"b_sel_recv{h}", [8, 2, 2048 * NST], F32)
                  for h in range(2)]
    xrb_d = nc.dram_tensor("xrb_d", [2048, T], BF16)
    C_d = nc.dram_tensor("C_d", [NFB, 128, T], BF16)
    sz_d = nc.dram_tensor("sz_d", [NFB, 128, T], BF16)
    b01_d = nc.dram_tensor("b01_d", [NST, FF], BF16)

    es = contextlib.ExitStack()
    bd = Builder(nc, es)

    x_sb = nc.alloc_sbuf_tensor("x_sb", [128, NF * T], F32)
    big1 = nc.alloc_sbuf_tensor("big1", [128, NF * T], BF16)
    big2 = nc.alloc_sbuf_tensor("big2", [128, NF * T], BF16)
    SCR_BYTES = 81600
    scr_t = nc.alloc_sbuf_tensor("scr", [128, SCR_BYTES], mybir.dt.uint8)
    scr = Carver(scr_t, SCR_BYTES)
    bigc = Carver(big2, NF * T * 2)

    def xt(f):
        return x_sb.ap()[:, T * f: T * (f + 1)]

    def b1(f):
        return big1.ap()[:, T * f: T * (f + 1)]

    def b2(f):
        return big2.ap()[:, T * f: T * (f + 1)]

    selx_s = scr.take(32, 128, BF16)
    selb_s = [scr.take(32, 128, BF16) for _ in range(NST)]
    selc_s = [scr.take(128, 128, BF16) for _ in range(NST)]
    ones_bf = scr.take(128, 128, BF16)
    ones_f1 = scr.take(1, 128, F32)
    ones_f = scr.take(128, CH, F32)
    CONSTVALS = [0.0, 1.0, -1.0, -0.5, EPS_LN, EPS_RMS, BIAS_OFF, -BIAS_OFF, 0.5]
    const_tile = scr.take(128, len(CONSTVALS), F32)
    for ci_, cv_ in enumerate(CONSTVALS):
        nc.const_aps.register(F32, cv_, const_tile[:, ci_:ci_ + 1])
    wc_s = [scr.take(128, NST, BF16) for _ in range(NH)]
    a_bdt_s = scr.take(128, NF, F32)
    a_alog_s = scr.take(128, NF * NSA, F32)
    a_Dp_s = scr.take(128, NF, F32)
    b_dtb_s = scr.take(128, NFB, F32)
    b_convw_s = scr.take(128, NFB * 4, F32)
    b_convb_s = scr.take(128, NFB, F32)
    b_alog_s = scr.take(128, NFB * NST, F32)
    b_Dp_s = scr.take(128, NFB, F32)
    f_ln1_s = scr.take(128, NF, F32)
    f_ln2_s = scr.take(128, NF, F32)
    f_A_s = [scr.take(128, NST, BF16) for _ in range(NF)]
    rows = scr.take(1, 2 * T, F32)

    def row(i, tt=None):
        if tt is None:
            return rows[:, i * T:(i + 1) * T]
        return rows[:, i * T + tt * CH: i * T + (tt + 1) * CH]

    pb = [nc.alloc_psum_tensor(f"pb{i}", [128, CH], F32) for i in range(8)]

    def pbank(i, rows_=128, cols=CH):
        return pb[i].ap()[0:rows_, 0:cols]

    def pbank_bf(i, sl=None):
        v = pb[i].ap().bitcast(BF16)
        if sl is None:
            return v[0:128, 0:2 * CH]
        return v[0:128, sl * CH:(sl + 1) * CH]

    qs = Queue(bd, "sync")
    qg = Queue(bd, "gpsimd")
    qa = Queue(bd, "scalar")
    qv = Queue(bd, "vector")

    G8 = [list(range(8))]

    ev = bd.sem("ev")
    ea = bd.sem("ea")
    ep = bd.sem("ep")
    eg = bd.sem("eg")
    esy = bd.sem("esy")

    def barrier():
        qg.wait("gpsimd")
        qa.wait("scalar")
        qs.wait("sync")
        bd.wait("tensor", ep, ep.n)
        bd.wait("sync", esy, esy.n)
        bd.op("gpsimd", "nop", sem=eg)
        bd.op("scalar", "nop", sem=ea)
        bd.op("sync", "nop", sem=esy)
        bd.op("vector", "nop", sem=ev)
        bd.op("tensor", "nop", sem=ep)
        marks = [(ev, "vector"), (ea, "scalar"), (ep, "tensor"),
                 (eg, "gpsimd"), (esy, "sync")]
        for s, _ in marks:
            for _, eng in marks:
                bd.wait(eng, s, s.n)

    def _w(eng, sems):
        for s in sems:
            if isinstance(s, Queue):
                s.wait(eng)
            else:
                bd.wait(eng, s)

    def vwait(*sems):
        _w("vector", sems)

    def awaitx(*sems):
        _w("scalar", sems)

    def pwait(*sems):
        _w("tensor", sems)

    def gwait(*sems):
        _w("gpsimd", sems)

    def swait(*sems):
        _w("sync", sems)

    def phase0():
        qg.dma(selx_s, selx_d)
        for n in range(NST):
            qg.dma(selb_s[n], selb_d[n])
            qg.dma(selc_s[n], selc_d[n])
        for f in range(NF):
            qs.dma(a_bdt_s[:, f:f + 1], a_bdt[f])
            qs.dma(a_alog_s[:, NSA * f:NSA * (f + 1)], a_alog[f])
            qs.dma(a_Dp_s[:, f:f + 1], a_Dp[f])
            qs.dma(f_ln1_s[:, f:f + 1], f_ln1[f])
            qs.dma(f_ln2_s[:, f:f + 1], f_ln2[f])
            qg.dma(f_A_s[f], f_A[128 * f:128 * (f + 1), :])
        for f in range(NFB):
            qs.dma(b_dtb_s[:, f:f + 1], b_dtb[f])
            qs.dma(b_convw_s[:, 4 * f:4 * (f + 1)], b_convw[f])
            qs.dma(b_convb_s[:, f:f + 1], b_convb[f])
            qs.dma(b_alog_s[:, NST * f:NST * (f + 1)], b_alog[f])
            qs.dma(b_Dp_s[:, f:f + 1], b_Dp[f])
        for f in range(NF):
            qs.dma(xt(f), x0[128 * f:128 * (f + 1), :])
        scr.push()
        FQ = FF // 4
        tmpB = scr.take(NST, FQ, F32)
        tmpB2 = scr.take(NST, FQ, BF16)
        vwait(qs, qg)
        bd.op("vector", "memset", ones_bf, 1.0)
        bd.op("vector", "memset", ones_f1, 1.0)
        bd.op("vector", "memset", ones_f, 1.0)
        for ci_, cv_ in enumerate(CONSTVALS):
            bd.op("vector", "memset", const_tile[:, ci_:ci_ + 1], cv_)
        bd.op("vector", "engine_nop", sem=ev)
        awaitx(ev)
        for q4 in range(4):
            swait(ev)
            qs.dma(tmpB, f_B[:, FQ * q4:FQ * (q4 + 1)])
            vwait(qs)
            bd.op("vector", "tensor_scalar_mul", tmpB2, tmpB, 0.1, sem=ev)
            swait(ev)
            qs.dma(b01_d.ap()[:, FQ * q4:FQ * (q4 + 1)], tmpB2)
        scr.push()
        atmp = scr.take(128, NFB * NST, F32)
        awaitx(qs)
        bd.op("scalar", "activation", atmp[:, 0:NF * NSA], a_alog_s, AF.Exp, sem=ea)
        vwait(ea)
        bd.op("vector", "tensor_scalar_mul", a_alog_s, atmp[:, 0:NF * NSA], -1.0, sem=ev)
        awaitx(ev)
        bd.op("scalar", "activation", atmp, b_alog_s, AF.Exp, sem=ea)
        vwait(ea)
        bd.op("vector", "tensor_scalar_mul", b_alog_s, atmp, -1.0, sem=ev)
        scr.pop()
        barrier()
        woT = [scr.take(128, 128, BF16) for _ in range(2)]
        nmm = 0
        for h in range(NH):
            for k in range(NF):
                buf = woT[(h * NF + k) % 2]
                if h * NF + k >= 2:
                    gwait(ep)
                qg.dma(buf, f_Wo[128 * k:128 * (k + 1),
                                           128 * h:128 * (h + 1)])
                pwait(qg)
                bd.op("tensor", "matmul", pbank(0, 128, NST), buf, f_A_s[k],
                      start=(k == 0), stop=(k == NF - 1), sem=ep)
                nmm += 1
            awaitx(ep)
            bd.op("scalar", "activation", wc_s[h], pbank(0, 128, NST), AF.Copy,
                  sem=ea)
            pwait(ea)
        scr.pop()
        barrier()

    def ln_stats(tag, with_rms, s1seg=None):
        scr.push()
        xb = [scr.take(128, T, BF16) for _ in range(2)]
        xb2 = [scr.take(128, T, BF16) for _ in range(2)]
        for f in range(NF):
            b = f % 2
            if f >= 2:
                vwait(ep)
            bd.op("vector", "tensor_copy", xb[b], xt(f), sem=ev)
            awaitx(ev)
            bd.op("scalar", "activation", xb2[b], xb[b], AF.Square, sem=ea)
            pwait(ev, ea)
            for tt in range(2):
                bd.op("tensor", "matmul", pbank(0 if tt == 0 else 6, 1, CH),
                      ones_bf[:, 0:1], xb[b][:, tt * CH:(tt + 1) * CH],
                      start=(f == 0), stop=(f == NF - 1), sem=None)
                bd.op("tensor", "matmul", pbank(1 if tt == 0 else 7, 1, CH),
                      ones_bf[:, 0:1], xb2[b][:, tt * CH:(tt + 1) * CH],
                      start=(f == 0), stop=(f == NF - 1), sem=ep)
        scr.pop()
        scr.push()
        tmp = scr.take(1, T, F32)
        t2 = scr.take(1, T, F32)
        vwait(ep)
        for tt in range(2):
            sb, qb = (0, 1) if tt == 0 else (6, 7)
            bd.op("vector", "tensor_scalar_mul", row(0, tt), pbank(sb, 1, CH), 1.0 / D)
            bd.op("vector", "tensor_scalar_mul", row(1, tt), pbank(qb, 1, CH), 1.0 / D)
        bd.op("vector", "tensor_mul", tmp, row(0), row(0))
        bd.op("vector", "tensor_sub", row(1), row(1), tmp, sem=ev)
        awaitx(ev)
        bd.op("scalar", "activation", tmp, row(1), AF.Ln, bias=EPS_LN, sem=ea)
        if with_rms:
            vwait(ea)
            bd.op("vector", "tensor_scalar_add", t2, row(1), EPS_LN)
            bd.op("vector", "reciprocal", t2, t2)
            bd.op("vector", "tensor_mul", t2, row(1), t2, sem=ev)
            awaitx(ev)
        bd.op("scalar", "activation", row(1), tmp, AF.Exp, scale=-0.5, sem=ea)
        if with_rms:
            awaitx(ev)
            bd.op("scalar", "activation", t2, t2, AF.Ln, bias=EPS_RMS)
            bd.op("scalar", "activation", t2, t2, AF.Exp, scale=-0.5, sem=ea)
            vwait(ea)
            bd.op("vector", "tensor_mul", s1seg, row(1), t2, sem=ev)
        scr.pop()

    def bcast_row(bank, rseg):
        bd.op("tensor", "matmul", pbank(bank), ones_f1[:, :], rseg,
              start=True, stop=True, sem=ep)

    def phase1():
        scr.push()
        bigc.push()
        selone_s = [scr.take(16, 128, BF16) for _ in range(16)]
        aw_xproj = [bigc.take(128, 272, BF16) for _ in range(NF)]
        aw_dt = [[bigc.take(128, 128, BF16) for _ in range(NF)] for _ in range(2)]
        for n in range(16):
            qg.dma(selone_s[n], selone_d[n])
        for f in range(NF):
            qg.dma(aw_xproj[f], a_xprojT[128 * f:128 * (f + 1), :])
            qg.dma(aw_dt[0][f], a_dtT[0:128, 128 * f:128 * (f + 1)])
            qg.dma(aw_dt[1][f], a_dtT[128:256, 128 * f:128 * (f + 1)])
        pwait(qg)
        vwait(qg)
        ln_stats("p1", with_rms=False)
        pwait(ev, ea)
        bcast_row(2, row(0, 0))
        bcast_row(3, row(1, 0))
        bcast_row(4, row(0, 1))
        bcast_row(5, row(1, 1))
        MU, RS = [2, 4], [3, 5]
        xc = scr.take(128, CH, F32)
        xa32 = scr.take(128, CH, F32)
        vwait(ep)
        for f in range(NF):
            for tt in range(2):
                sl = slice(tt * CH, (tt + 1) * CH)
                bd.op("vector", "tensor_sub", xc, xt(f)[:, sl], pbank(MU[tt]))
                bd.op("vector", "tensor_mul", xa32, xc, pbank(RS[tt]))
                bd.op("vector", "tensor_copy", b1(f)[:, sl], xa32)
                bd.op("vector", "scalar_tensor_tensor", xt(f)[:, sl], xa32,
                      a_Dp_s[:, f:f + 1], xt(f)[:, sl], ALU.mult, ALU.add, sem=ev)
        dtr = [bigc.take(128, T, BF16) for _ in range(2)]
        bca = scr.take(16, T, BF16)
        for tt in range(2):
            for mb in range(3):
                mrows = 128 if mb < 2 else 16
                pwait(ev, ea)
                for k in range(NF):
                    bd.op("tensor", "matmul", pbank(6, mrows, CH),
                          aw_xproj[k][:, 128 * mb:128 * mb + mrows],
                          b1(k)[:, tt * CH:(tt + 1) * CH],
                          start=(k == 0), stop=(k == NF - 1),
                          sem=(ep if k == NF - 1 else None))
                awaitx(ep)
                dst = (dtr[mb][:, tt * CH:(tt + 1) * CH] if mb < 2
                      else bca[:, tt * CH:(tt + 1) * CH])
                bd.op("scalar", "activation", dst, pbank(6, mrows, CH), AF.Copy,
                      sem=ea)
        delta = bigc.take(128, T, F32)
        sp1 = scr.take(128, CH, F32)
        sp2 = scr.take(128, CH, F32)
        fstage = scr.take(128, 2 * NF * NSA, F32)
        dstage = scr.take(128, 2 * NF, F32)
        bigc.push()
        scr.push()
        ub = bigc.take(128, T, BF16)
        at_ = scr.take(128, T, F32)
        bt_ = scr.take(128, T, F32)
        Ht = bigc.take(128, T, F32)
        at2 = [at_, scr.t.ap().bitcast(F32)[0:128, sp1.offset:sp1.offset + T]]
        for f in range(NF):
            for tt in range(2):
                pwait(ev, ea)
                for k in range(2):
                    bd.op("tensor", "matmul", pbank(0),
                          aw_dt[k][f], dtr[k][:, tt * CH:(tt + 1) * CH],
                          start=(k == 0), stop=(k == 1),
                          sem=(ep if k == 1 else None))
                awaitx(ep, ev)
                bd.op("scalar", "activation", sp1, pbank(0), AF.Abs,
                      bias=a_bdt_s[:, f:f + 1])
                bd.op("scalar", "activation", sp1, sp1, AF.Exp, scale=-1.0)
                bd.op("scalar", "activation", sp1, sp1, AF.Ln, bias=1.0)
                bd.op("scalar", "activation", sp2, pbank(0), AF.Relu,
                      bias=a_bdt_s[:, f:f + 1], sem=ea)
                vwait(ea)
                bd.op("vector", "scalar_tensor_tensor",
                      delta[:, tt * CH:(tt + 1) * CH], sp1, 1.0, sp2,
                      ALU.mult, ALU.add,
                      accum_out=dstage[:, NF * tt + f: NF * tt + f + 1], sem=ev)
                bd.op("vector", "tensor_mul", ub[:, tt * CH:(tt + 1) * CH],
                      delta[:, tt * CH:(tt + 1) * CH],
                      b1(f)[:, tt * CH:(tt + 1) * CH], sem=ev)
            mka, mkB, mkC = {}, {}, {}
            _pv = bd.sems["ev"].n
            _pg = bd.sems["eg"].n
            mkscan = {-2: _pv, -1: _pv}
            mkb = {-2: _pv, -1: _pv}
            mkym = {-2: _pv, -1: _pv}
            mkg = {-2: _pg, -1: _pg}
            pwait(ea)

            def prep1B(n, f=f):
                BP = (1, 2) if n % 2 == 0 else (3, 4)
                bd.wait("scalar", ev, mkscan[n - 2])
                bd.op("scalar", "activation", at2[n % 2], delta, AF.Exp,
                      scale=a_alog_s[:, NSA * f + n: NSA * f + n + 1], sem=ea)
                mka[n] = ea.n
                bd.wait("tensor", ev, mkb[n - 2])
                for tt in range(2):
                    bd.op("tensor", "matmul", pbank(BP[tt]),
                          selone_s[n][0:16, :], bca[0:16, tt * CH:(tt + 1) * CH],
                          start=True, stop=True, sem=ep)
                mkB[n] = ep.n

            def prep1C(n, f=f):
                bd.wait("tensor", ev, mkym[n - 1])
                for tt in range(2):
                    bd.op("tensor", "matmul", pbank(5 + tt),
                          selone_s[8 + n][0:16, :], bca[0:16, tt * CH:(tt + 1) * CH],
                          start=True, stop=True, sem=ep)
                mkC[n] = ep.n

            prep1B(0)
            prep1C(0)
            for n in range(NSA):
                if n + 1 < NSA:
                    prep1B(n + 1)
                BP = (1, 2) if n % 2 == 0 else (3, 4)
                bd.wait("vector", ea, mka[n])
                bd.wait("vector", ep, mkB[n])
                for tt in range(2):
                    bd.op("vector", "tensor_mul", bt_[:, tt * CH:(tt + 1) * CH],
                          ub[:, tt * CH:(tt + 1) * CH], pbank(BP[tt]))
                mkb[n] = bd.sems["ev"].n
                bd.wait("vector", eg, mkg[n - 1])
                for tt in range(2):
                    bd.op("vector", "tensor_tensor_scan",
                          Ht[:, tt * CH:(tt + 1) * CH],
                          at2[n % 2][:, tt * CH:(tt + 1) * CH],
                          bt_[:, tt * CH:(tt + 1) * CH],
                          0.0, ALU.mult, ALU.add)
                    bd.op("vector", "tensor_copy",
                          fstage[:, 128 * tt + NSA * f + n: 128 * tt + NSA * f + n + 1],
                          Ht[:, (tt + 1) * CH - 1:(tt + 1) * CH])
                mkscan[n] = bd.sems["ev"].n
                bd.wait("vector", ep, mkC[n])
                bd.wait("vector", eg, mkg[n - 1])
                for tt in range(2):
                    bd.op("vector", "tensor_mul", Ht[:, tt * CH:(tt + 1) * CH],
                          Ht[:, tt * CH:(tt + 1) * CH], pbank(5 + tt))
                mkym[n] = bd.sems["ev"].n
                bd.wait("gpsimd", ev, mkym[n])
                bd.op("gpsimd", "tensor_add", xt(f), xt(f), Ht)
                mkg[n] = bd.sems["eg"].n
                if n + 1 < NSA:
                    prep1C(n + 1)
        bigc.pop()
        scr.pop()
        barrier()
        awaitx(ev)
        for c in range(2):
            qa.dma(
                   a_pub[c, 0:D].rearrange("(p f) -> p f", p=128),
                   dstage[:, NF * c:NF * (c + 1)])
            qa.dma(
                   a_pub[c, D:].rearrange("(p k) -> p k", p=128),
                   fstage[:, 128 * c:128 * (c + 1)])
        gwait(qa)
        cc1 = bd.collective("cc_a_g", "AllGather", ALU.bypass, G8,
                            [a_pub.ap().opt()], [a_gath.ap().opt()])
        gwait(cc1)
        bigc.push()
        dts_roll = [bigc.take(128, NF, F32) for _ in range(2)]
        F_roll = [bigc.take(128, NF * NSA, F32) for _ in range(2)]
        cprev = bigc.take(128, NF * NSA, F32)
        ccur = bigc.take(128, NF * NSA, F32)
        Ptmp = bigc.take(128, NSA, F32)

        def s16_of(b2, c):
            r, s = _rank_of(b2, c)
            return 2 * r + s

        for b2 in range(2):
            for c in range(NCHK):
                cseg = ccur if c % 2 else cprev
                psg = cprev if c % 2 else ccur
                vwait(qa)
                if c == 0:
                    bd.op("vector", "memset", cseg, 0.0, sem=ev)
                else:
                    ps16 = s16_of(b2, c - 1)
                    gwait(ev)
                    qg.dma(dts_roll[c % 2],
                           a_gath[ps16, 0:D].rearrange("(p f) -> p f", p=128),
                           )
                    qg.dma(F_roll[c % 2],
                           a_gath[ps16, D:].rearrange("(p k) -> p k", p=128),
                           )
                    vwait(qg)
                    awaitx(qg, ev)
                    for f in range(NF):
                        awaitx(ev)
                        bd.op("scalar", "activation", Ptmp,
                              a_alog_s[:, NSA * f:NSA * (f + 1)], AF.Exp,
                              scale=dts_roll[c % 2][:, f:f + 1], sem=ea)
                        vwait(ea)
                        bd.op("vector", "tensor_mul",
                              cseg[:, NSA * f:NSA * (f + 1)],
                              psg[:, NSA * f:NSA * (f + 1)], Ptmp)
                        bd.op("vector", "tensor_add",
                              cseg[:, NSA * f:NSA * (f + 1)],
                              cseg[:, NSA * f:NSA * (f + 1)],
                              F_roll[c % 2][:, NSA * f:NSA * (f + 1)], sem=ev)
                if c <= 3:
                    j, ci = 4 * b2 + c, 0
                else:
                    j, ci = 4 * b2 + 7 - c, 1
                awaitx(ev)
                qa.dma(
                       a_sel_send[j, ci, :].rearrange("(p k) -> p k", p=128),
                       cseg)
                gwait(qa)
        gwait(qa)
        cc2 = bd.collective("cc_a_s", "AllToAll", ALU.bypass, G8,
                            [a_sel_send.ap().opt()], [a_sel_recv.ap().opt()])
        gwait(cc2)
        myc = [bigc.take(128, NF * NSA, F32) for _ in range(2)]
        for ci in range(2):
            qg.dma(myc[ci],
                   a_sel_recv[0, ci, :].rearrange("(p k) -> p k", p=128))
        DT = scr.take(128, T, F32)
        cpA = scr.take(128, T, F32)
        fxt = scr.take(128, T, F32)
        at2 = [cpA, scr.t.ap().bitcast(F32)[0:128, sp1.offset:sp1.offset + T]]
        vwait(qg)
        awaitx(qg)
        for f in range(NF):
            for tt in range(2):
                pwait(ev, ea)
                for k in range(2):
                    bd.op("tensor", "matmul", pbank(0),
                          aw_dt[k][f], dtr[k][:, tt * CH:(tt + 1) * CH],
                          start=(k == 0), stop=(k == 1),
                          sem=(ep if k == 1 else None))
                awaitx(ep, ev)
                bd.op("scalar", "activation", sp1, pbank(0), AF.Abs,
                      bias=a_bdt_s[:, f:f + 1])
                bd.op("scalar", "activation", sp1, sp1, AF.Exp, scale=-1.0)
                bd.op("scalar", "activation", sp1, sp1, AF.Ln, bias=1.0)
                bd.op("scalar", "activation", sp2, pbank(0), AF.Relu,
                      bias=a_bdt_s[:, f:f + 1], sem=ea)
                vwait(ea)
                bd.op("vector", "scalar_tensor_tensor",
                      delta[:, tt * CH:(tt + 1) * CH], sp1, 1.0, sp2,
                      ALU.mult, ALU.add, sem=ev)
                bd.op("vector", "tensor_tensor_scan",
                      DT[:, tt * CH:(tt + 1) * CH], ones_f,
                      delta[:, tt * CH:(tt + 1) * CH], 0.0,
                      ALU.mult, ALU.add, sem=ev)
            mka, mkC = {}, {}
            _pv = bd.sems["ev"].n
            _pg = bd.sems["eg"].n
            mkstt = {-2: _pv, -1: _pv}
            mkg1f = {-2: _pg, -1: _pg}
            pwait(ea)

            def prep1f(n, f=f):
                CP = (1, 2) if n % 2 == 0 else (3, 4)
                bd.wait("scalar", ev, mkstt[n - 2])
                bd.op("scalar", "activation", at2[n % 2], DT, AF.Exp,
                      scale=a_alog_s[:, NSA * f + n: NSA * f + n + 1], sem=ea)
                mka[n] = ea.n
                bd.wait("tensor", ev, mkstt[n - 2])
                for tt in range(2):
                    bd.op("tensor", "matmul", pbank(CP[tt]),
                          selone_s[8 + n][0:16, :], bca[0:16, tt * CH:(tt + 1) * CH],
                          start=True, stop=True, sem=ep)
                mkC[n] = ep.n

            prep1f(0)
            for n in range(NSA):
                if n + 1 < NSA:
                    prep1f(n + 1)
                CP = (1, 2) if n % 2 == 0 else (3, 4)
                bd.wait("vector", ea, mka[n])
                bd.wait("vector", ep, mkC[n])
                bd.wait("vector", eg, mkg1f[n - 1])
                for tt in range(2):
                    bd.op("vector", "scalar_tensor_tensor",
                          fxt[:, tt * CH:(tt + 1) * CH],
                          at2[n % 2][:, tt * CH:(tt + 1) * CH],
                          myc[tt][:, NSA * f + n: NSA * f + n + 1],
                          pbank(CP[tt]), ALU.mult, ALU.mult, sem=ev)
                mkstt[n] = bd.sems["ev"].n
                bd.wait("gpsimd", ev, mkstt[n])
                bd.op("gpsimd", "tensor_add", xt(f), xt(f), fxt)
                mkg1f[n] = bd.sems["eg"].n
        bigc.pop()
        bigc.pop()
        scr.pop()
        barrier()


    def phase_ffn(tag):
        scr.push()
        trimask_s = [scr.take(128, CH, BF16) for _ in range(4)]
        for s in range(4):
            qg.dma(trimask_s[s], trimask_d[s])
        qg.flip()
        vwait(qg)
        s1row = scr.take(1, T, F32)
        brow = scr.take(1, CH, F32)
        lrow = scr.take(1, CH, F32)
        ln_stats(tag, with_rms=True, s1seg=s1row)
        pwait(ev, ea)
        bcast_row(2, row(0, 0))
        bcast_row(3, s1row[:, 0:CH])
        bcast_row(4, row(0, 1))
        bcast_row(5, s1row[:, CH:2 * CH])
        MU, S1 = [2, 4], [3, 5]
        xc = scr.take(128, CH, F32)
        vwait(ep)
        for f in range(NF):
            for tt in range(2):
                sl = slice(tt * CH, (tt + 1) * CH)
                bd.op("vector", "tensor_sub", xc, xt(f)[:, sl], pbank(MU[tt]))
                bd.op("vector", "scalar_tensor_tensor", b1(f)[:, sl], xc,
                      f_ln1_s[:, f:f + 1], pbank(S1[tt]), ALU.mult, ALU.mult,
                      sem=ev)
        wbuf = [scr.take(128, NF * 128, BF16) for _ in range(2)]
        wbuf2 = [scr.take(128, NF * 128, BF16) for _ in range(2)]
        qkvt = [scr.take(128, CH, BF16) for _ in range(4)]
        WT = [f_WqT, f_WkT, f_WvT]
        gwait(ep, ev, ea)
        nblk = 0
        for qkv in range(3):
            for mb in range(NH):
                wb = wbuf[nblk % 2]
                if nblk >= 2:
                    gwait(ep)
                qg.dma(wb.rearrange("p (k m) -> p k m", k=NF),
                       WT[qkv][:, 128 * mb:128 * (mb + 1)]
                       .rearrange("(k p) m -> p k m", p=128))
                qg.flip()
                pwait(qg)
                if nblk == 0:
                    pwait(ev)
                for tt in range(2):
                    bank = tt
                    pwait(ea)
                    for k in range(NF):
                        bd.op("tensor", "matmul", pbank(bank),
                              wb[:, 128 * k:128 * (k + 1)],
                              b1(k)[:, tt * CH:(tt + 1) * CH],
                              start=(k == 0), stop=(k == NF - 1),
                              sem=(ep if k == NF - 1 else None))
                    awaitx(ep)
                    if nblk >= 2:
                        qa.wait("scalar", qa.epoch - 2)
                    qt = qkvt[2 * (nblk % 2) + tt]
                    sc = QSCALE if qkv == 0 else 1.0
                    bd.op("scalar", "activation", qt, pbank(bank), AF.Copy,
                          scale=sc, sem=ea)
                    awaitx(ea)
                    s6 = 2 * qkv + (mb % 2)
                    qa.dma(kvq_send.ap()[mb // 2, s6, :, tt * CH:(tt + 1) * CH],
                           qt)
                qa.flip()
                nblk += 1
        gwait(qa)
        gwait(ev, ea)
        cc_kvq = bd.collective("cc_kvq_" + tag, "AllToAll", ALU.bypass, G8,
                               [kvq_send.ap().opt()], [kvq_recv.ap().opt()])
        gwait(cc_kvq)
        bd.wait("sync", cc_kvq, 1)
        bigc.push()
        qtile = [bigc.take(128, CH, BF16) for _ in range(2)]
        ktile = [bigc.take(128, CH, BF16) for _ in range(3)]
        vtile = [bigc.take(128, CH, BF16) for _ in range(3)]
        ptile = [bigc.take(128, CH, BF16) for _ in range(2)]
        tmpbf = bigc.take(128, CH, BF16)
        linv_sb = bigc.take(128, CH, F32)
        o_sb = [bigc.take(128, CH, BF16) for _ in range(2)]
        SB = [0, 1]
        nq = 0

        def load_kv(buf_i, b2, kc, hh):
            rk2, sl2 = _rank_of(b2, kc)
            qs.dma(ktile[buf_i],
                   kvq_recv.ap()[rk2, 2 + hh, :, sl2 * CH:(sl2 + 1) * CH])
            for s4 in range(4):
                qs.dma(vtile[buf_i][:, 128 * s4:128 * (s4 + 1)],
                       kvq_recv.ap()[rk2, 4 + hh, :,
                                     sl2 * CH + 128 * s4: sl2 * CH + 128 * (s4 + 1)],
                       transpose=True)
            return qs.flip()

        for b2 in range(2):
            for hh in range(2):
                for qrow in range(NCHK):
                    rk, sl_ = _rank_of(b2, qrow)
                    qb = qtile[nq % 2]
                    bd.wait("sync", ev, ev.n)
                    bd.wait("sync", ep, ep.n)
                    qs.dma(qb, kvq_recv.ap()[rk, hh, :, sl_ * CH:(sl_ + 1) * CH])
                    tok = load_kv(qrow % 3, b2, qrow, hh)
                    vwait(qs)
                    bd.op("vector", "tensor_mul", tmpbf, qb,
                          ktile[qrow % 3], sem=ev)
                    pwait(ev, ea)
                    bd.op("tensor", "matmul", pbank(4, 1, CH), ones_bf[:, 0:1],
                          tmpbf, start=True, stop=True, sem=ep)
                    vwait(ep)
                    bd.op("vector", "tensor_scalar", brow, pbank(4, 1, CH),
                          -1.0, -BIAS_OFF, ALU.mult, ALU.add, sem=ev)
                    nsubs = 4 * (qrow + 1)
                    act_mark = {}
                    pv_mark = {-2: 0, -1: 0}
                    toks = None
                    subq = []
                    for kc_ in range(qrow, -1, -1):
                        for s4_ in range(4):
                            subq.append((kc_, s4_))

                    def emit_qk(i):
                        kc, s4 = subq[i]
                        sbank = SB[i % 2]
                        if i >= 2:
                            bd.wait("tensor", ea, act_mark[i - 2])
                        bd.op("tensor", "matmul", pbank(sbank),
                              ones_f1[:, :], brow,
                              start=True, stop=False)
                        bd.op("tensor", "matmul", pbank(sbank),
                              ktile[kc % 3][:, 128 * s4:128 * (s4 + 1)], qb,
                              start=False, stop=True, sem=ep)
                        return ep.n

                    def emit_exp(i, qkm):
                        kc, s4 = subq[i]
                        sbank = SB[i % 2]
                        if kc == qrow:
                            bd.wait("vector", ep, qkm)
                            bd.op("vector", "tensor_add", pbank(sbank),
                                  pbank(sbank), trimask_s[s4], sem=ev)
                            awaitx(ev)
                        else:
                            bd.wait("scalar", ep, qkm)
                        bd.wait("scalar", ep, pv_mark[i - 2])
                        bd.op("scalar", "activation", ptile[i % 2], pbank(sbank),
                              AF.Exp, sem=ea)
                        act_mark[i] = ea.n

                    def emit_lpv(i):
                        kc, s4 = subq[i]
                        first = (i == 0)
                        last = (i == nsubs - 1)
                        bd.wait("tensor", ea, act_mark[i])
                        bd.op("tensor", "matmul", pbank(3, 1, CH),
                              ones_bf[:, 0:1], ptile[i % 2],
                              start=first, stop=last, sem=None)
                        bd.op("tensor", "matmul", pbank(2),
                              vtile[kc % 3][:, 128 * s4:128 * (s4 + 1)],
                              ptile[i % 2],
                              start=first, stop=last, sem=ep)
                        pv_mark[i] = ep.n

                    pwait(ev)
                    prev_kc = None
                    for i in range(nsubs):
                        kc, s4 = subq[i]
                        if i >= 1:
                            emit_lpv(i - 1)
                        if kc != prev_kc:
                            if kc != qrow:
                                qs.wait("tensor", toks)
                                qs.wait("scalar", toks)
                            if kc > 0:
                                bd.wait("sync", ep, ep.n)
                                toks = load_kv((kc - 1) % 3, b2, kc - 1, hh)
                            prev_kc = kc
                        qkm = emit_qk(i)
                        emit_exp(i, qkm)
                    emit_lpv(nsubs - 1)
                    vwait(ep)
                    bd.op("vector", "reciprocal", lrow, pbank(3, 1, CH),
                          sem=ev)
                    pwait(ev)
                    bd.op("tensor", "matmul", pbank(4), ones_f1[:, :],
                          lrow, start=True, stop=True, sem=ep)
                    awaitx(ep)
                    bd.op("scalar", "activation", linv_sb, pbank(4), AF.Copy,
                          sem=ea)
                    vwait(ea)
                    if nq >= 2:
                        qa.wait("vector", qa.epoch - 2)
                    osb = o_sb[nq % 2]
                    bd.op("vector", "tensor_mul", osb, pbank(2), linv_sb,
                          sem=ev)
                    awaitx(ev)
                    qa.dma(o_send.ap()[rk, hh, :, sl_ * CH:(sl_ + 1) * CH], osb)
                    qa.flip()
                    nq += 1
        gwait(qa)
        gwait(ev, ea)
        cc_o = bd.collective("cc_o_" + tag, "AllToAll", ALU.bypass, G8,
                             [o_send.ap().opt()], [o_recv.ap().opt()])
        gwait(cc_o)
        bd.wait("sync", cc_o, 1)
        o2 = [bigc.take(128, CH, BF16) for _ in range(2)]
        tanh_sb = bigc.take(16, T, F32)
        modh_bf = scr.take(16, T, BF16)
        aA_mark = {}
        nh2 = 0
        for tt in range(2):
            for h in range(NH):
                ob = o2[nh2 % 2]
                if nh2 >= 2:
                    bd.wait("sync", ep, aA_mark[nh2 - 2])
                qs.dma(ob, o_recv.ap()[h // 2, h % 2, :, tt * CH:(tt + 1) * CH])
                qs.flip()
                pwait(qs)
                if nh2 == 0:
                    pwait(ea, ev)
                bd.op("tensor", "matmul", pbank(5 + tt, 16, CH), wc_s[h], ob,
                      start=(h == 0), stop=(h == NH - 1), sem=ep)
                aA_mark[nh2] = ep.n
                nh2 += 1
        awaitx(ep)
        for tt in range(2):
            bd.op("scalar", "activation", tanh_sb[:, tt * CH:(tt + 1) * CH],
                  pbank(5 + tt, 16, CH), AF.Tanh, sem=ea)
        pwait(ev, ea)
        bcast_row(2, row(0, 0))
        bcast_row(3, s1row[:, 0:CH])
        bcast_row(4, row(0, 1))
        bcast_row(5, s1row[:, CH:2 * CH])
        vwait(ep)
        for f in range(NF):
            for tt in range(2):
                sl = slice(tt * CH, (tt + 1) * CH)
                bd.op("vector", "tensor_sub", xc, xt(f)[:, sl], pbank(MU[tt]))
                bd.op("vector", "scalar_tensor_tensor", b1(f)[:, sl], xc,
                      f_ln2_s[:, f:f + 1], pbank(S1[tt]), ALU.mult, ALU.mult,
                      sem=ev)
        for tt in range(2):
            pwait(ev)
            pwait(ea)
            for k in range(NF):
                bd.op("tensor", "matmul", pbank(5 + tt, 16, CH), f_A_s[k],
                      b1(k)[:, tt * CH:(tt + 1) * CH],
                      start=(k == 0), stop=(k == NF - 1),
                      sem=(ep if k == NF - 1 else None))
            vwait(ep)
            bd.op("vector", "tensor_mul", modh_bf[:, tt * CH:(tt + 1) * CH],
                  tanh_sb[:, tt * CH:(tt + 1) * CH], pbank(5 + tt, 16, CH),
                  sem=ev)
        barrier()
        bigc.pop()
        bigc.push()
        gu = bigc.take(128, 16 * 2 * CH, BF16)

        def gusl(fb, tt):
            o = (fb * 2 + tt) * CH
            return gu[:, o:o + CH]

        wbufd = scr.take(128, 16 * 128, BF16)
        b01t = [scr.take(16, 128, BF16) for _ in range(2)]
        gate_bf = scr.take(128, CH, BF16)
        nfbq = 0

        def mlp_load(i):
            gwait(ep)
            qg.dma(wbuf[i % 2].rearrange("p (k m) -> p k m", k=NF),
                   f_WinT[:, 128 * i:128 * (i + 1)]
                   .rearrange("(k p) m -> p k m", p=128))
            qg.dma(b01t[i % 2], b01_d.ap()[:, 128 * i:128 * (i + 1)])
            qg.dma(wbuf2[i % 2].rearrange("p (k m) -> p k m", k=NF),
                   f_WupT[:, 128 * i:128 * (i + 1)]
                   .rearrange("(k p) m -> p k m", p=128))
            return qg.flip()

        pend2 = mlp_load(0)
        for quarter in range(4):
            for fbq in range(16):
                fb = 16 * quarter + fbq
                wbi = wbuf[nfbq % 2]
                wbu2 = wbuf2[nfbq % 2]
                tok = pend2
                if fb + 1 < NFF:
                    pend2 = mlp_load(fb + 1)
                qg.wait("tensor", tok)
                for tt in range(2):
                    gb = (2 * nfbq + tt) % 2
                    ub = 2 + (2 * nfbq + tt) % 2
                    pwait(ea, ev)
                    for k in range(NF):
                        bd.op("tensor", "matmul", pbank(gb),
                              wbi[:, 128 * k:128 * (k + 1)],
                              b1(k)[:, tt * CH:(tt + 1) * CH],
                              start=(k == 0), stop=False)
                    bd.op("tensor", "matmul", pbank(gb),
                          b01t[nfbq % 2][0:16, :],
                          modh_bf[0:16, tt * CH:(tt + 1) * CH],
                          start=False, stop=True, sem=ep)
                    for k in range(NF):
                        bd.op("tensor", "matmul", pbank(ub),
                              wbu2[:, 128 * k:128 * (k + 1)],
                              b1(k)[:, tt * CH:(tt + 1) * CH],
                              start=(k == 0), stop=(k == NF - 1),
                              sem=(ep if k == NF - 1 else None))
                    awaitx(ep)
                    bd.op("scalar", "activation", gate_bf, pbank(gb), AF.Silu,
                          sem=ea)
                    vwait(ea)
                    bd.op("vector", "tensor_mul", gusl(fbq, tt), gate_bf,
                          pbank(ub), sem=ev)
                nfbq += 1
            for m in range(NF):
                gwait(ep)
                qg.dma(wbufd.rearrange("p (k mm) -> p k mm", k=16),
                       f_WdownT[2048 * quarter:2048 * (quarter + 1),
                                128 * m:128 * (m + 1)]
                       .rearrange("(k p) mm -> p k mm", p=128))
                qg.flip()
                pwait(qg)
                for tt in range(2):
                    dbank = 4 + tt
                    sl = slice(tt * CH, (tt + 1) * CH)
                    pwait(ev)
                    for k in range(16):
                        bd.op("tensor", "matmul", pbank(dbank),
                              wbufd[:, 128 * k:128 * (k + 1)], gusl(k, tt),
                              start=(k == 0), stop=(k == 15),
                              sem=(ep if k == 15 else None))
                    if quarter == 0:
                        bcast_row(6, row(0, tt))
                        bcast_row(7, row(1, tt))
                    vwait(ep)
                    if quarter == 0:
                        bd.op("vector", "tensor_sub", xc, xt(m)[:, sl],
                              pbank(6))
                        bd.op("vector", "tensor_mul", xc, xc, pbank(7))
                        bd.op("vector", "tensor_add", xc, xc, pbank(dbank))
                        bd.op("vector", "tensor_add", xt(m)[:, sl],
                              xt(m)[:, sl], xc, sem=ev)
                    else:
                        bd.op("vector", "tensor_add", xt(m)[:, sl],
                              xt(m)[:, sl], pbank(dbank), sem=ev)
        bigc.pop()
        scr.pop()
        barrier()


    def phase_b():
        scr.push()
        ln_stats("pb", with_rms=False)
        pwait(ev, ea)
        bcast_row(2, row(0, 0))
        bcast_row(3, row(1, 0))
        bcast_row(4, row(0, 1))
        bcast_row(5, row(1, 1))
        MU, RS = [2, 4], [3, 5]
        xc = scr.take(128, CH, F32)
        vwait(ep)
        for f in range(NF):
            for tt in range(2):
                sl = slice(tt * CH, (tt + 1) * CH)
                bd.op("vector", "tensor_sub", xc, xt(f)[:, sl], pbank(MU[tt]))
                bd.op("vector", "tensor_mul", b1(f)[:, sl], xc, pbank(RS[tt]),
                      sem=ev)
        wbuf = [scr.take(128, NF * 128, BF16) for _ in range(2)]
        evt = [scr.take(128, CH, BF16) for _ in range(2)]
        dtin_sb = scr.take(128, T, BF16)
        gwait(ep, ev, ea)
        qg.dma(wbuf[0].rearrange("p (k m) -> p k m", k=NF),
               b_dtinT.rearrange("(k p) m -> p k m", p=128))
        qg.flip()
        pwait(qg)
        pwait(ev)
        for tt in range(2):
            pwait(ea)
            for k in range(NF):
                bd.op("tensor", "matmul", pbank(0),
                      wbuf[0][:, 128 * k:128 * (k + 1)],
                      b1(k)[:, tt * CH:(tt + 1) * CH],
                      start=(k == 0), stop=(k == NF - 1),
                      sem=(ep if k == NF - 1 else None))
            awaitx(ep)
            bd.op("scalar", "activation", dtin_sb[:, tt * CH:(tt + 1) * CH],
                  pbank(0), AF.Copy, sem=ea)
        nblk = 0
        for mb in range(16):
            wb = wbuf[nblk % 2]
            gwait(ep)
            qg.dma(wb.rearrange("p (k m) -> p k m", k=NF),
                   b_inprojT[:, 128 * (32 + mb):128 * (33 + mb)]
                   .rearrange("(k p) m -> p k m", p=128))
            qg.flip()
            pwait(qg)
            for tt in range(2):
                bank = tt
                pwait(ea)
                for k in range(NF):
                    bd.op("tensor", "matmul", pbank(bank),
                          wb[:, 128 * k:128 * (k + 1)],
                          b1(k)[:, tt * CH:(tt + 1) * CH],
                          start=(k == 0), stop=(k == NF - 1),
                          sem=(ep if k == NF - 1 else None))
                awaitx(ep)
                if nblk >= 1:
                    qa.wait("scalar", qa.epoch - 1)
                et = evt[tt]
                bd.op("scalar", "activation", et, pbank(bank), AF.Copy, sem=ea)
                awaitx(ea)
                qa.dma(xrb_d.ap()[128 * mb:128 * (mb + 1),
                                  tt * CH:(tt + 1) * CH], et)
            qa.flip()
            nblk += 1
        zz_sb = scr.take(128, 32, BF16)
        bd.op("vector", "memset", zz_sb, 0.0, sem=ev)
        zz_d = nc.dram_tensor("zz_" + "b", [1024, 4], BF16)
        vwait(ev)
        gwait(qa)
        gwait(ev)
        qg.dma(zz_d.ap().rearrange("(a p) c -> p a c", p=128), 
               zz_sb.rearrange("p (a c) -> p a c", c=4))
        for c in range(2):
            qg.dma(bnd_pub.ap()[c], xrb_d.ap()[0:1024, c * CH + 508:c * CH + 512])
        qg.flip()
        gwait(qg)
        ccb1 = bd.collective("cc_bnd_g", "AllGather", ALU.bypass, G8,
                             [bnd_pub.ap().opt()], [bnd_gath.ap().opt()])
        gwait(ccb1)
        for j in range(8):
            b2 = j // 4
            for ci, c in enumerate((j % 4, 7 - j % 4)):
                if c == 0:
                    qg.dma(bnd_send.ap()[j, ci], zz_d.ap())
                else:
                    rk, sl2 = _rank_of(b2, c - 1)
                    qg.dma(bnd_send.ap()[j, ci], bnd_gath.ap()[2 * rk + sl2])
        qg.flip()
        gwait(qg)
        ccb2 = bd.collective("cc_bnd_s", "AllToAll", ALU.bypass, G8,
                             [bnd_send.ap().opt()], [bnd_recv.ap().opt()])
        gwait(ccb2)
        bnd_sb = [scr.take(32, 32 * 4, BF16) for _ in range(2)]
        for ci in range(2):
            qg.dma(bnd_sb[ci].rearrange("p (w c) -> p w c", c=4),
                   bnd_recv.ap()[0, ci].rearrange("(w p) c -> p w c", p=32))
        qg.flip()
        bdtp_sb = scr.take(128, 16 * 128, BF16)
        dt_fb = scr.take(128, T, F32)
        xact_bf = scr.take(128, T, BF16)
        u_bf = scr.take(128, T, BF16)
        _xoff = xact_bf.offset
        fixsum = scr.t.ap().bitcast(F32)[0:128, _xoff // 2:_xoff // 2 + T]
        at_ = scr.take(128, T, F32)
        bt_ = scr.take(128, T, F32)
        Ht = scr.take(128, T, F32)
        C_sb = scr.take(128, T, BF16)
        z_bf = scr.take(128, T, BF16)
        xrw = [scr.take(32, T, BF16)]
        Bw = [scr.take(32, T, BF16)]
        dstage_b = scr.take(128, 2 * 16, F32)
        fstage_b = scr.take(128, 2 * 256, F32)
        F_roll = [scr.take(128, 256, F32) for _ in range(2)]
        myc = F_roll
        cprev = bt_[:, 0:256]
        ccur = bt_[:, 256:512]
        Ptmp = bt_[:, 512:528]
        dts_roll = [bt_[:, 528 + 16 * i:544 + 16 * i] for i in range(2)]

        def zc_mms(fb, dst_bf, rowbase, silu):
            nonlocal_wb = wbuf[fb % 2]
            gwait(ep)
            qg.dma(nonlocal_wb.rearrange("p (k m) -> p k m", k=NF),
                   b_inprojT[:, rowbase + 128 * fb: rowbase + 128 * (fb + 1)]
                   .rearrange("(k p) m -> p k m", p=128))
            qg.flip()
            pwait(qg)
            pwait(ev)
            for tt in range(2):
                pwait(ea)
                for k in range(NF):
                    bd.op("tensor", "matmul", pbank(3),
                          nonlocal_wb[:, 128 * k:128 * (k + 1)],
                          b1(k)[:, tt * CH:(tt + 1) * CH],
                          start=(k == 0), stop=(k == NF - 1),
                          sem=(ep if k == NF - 1 else None))
                awaitx(ep)
                bd.op("scalar", "activation", dst_bf[:, tt * CH:(tt + 1) * CH],
                      pbank(3), AF.Silu if silu else AF.Copy, sem=ea)

        bdtp_tok = [None]

        def dt_softplus(fb, want_dtsum):
            fbq = fb % 16
            qs_ = qg.wait("tensor", bdtp_tok[0])
            for tt in range(2):
                pwait(ea, ev)
                bd.op("tensor", "matmul", pbank(0),
                      bdtp_sb[:, 128 * (fb % 16):128 * (fb % 16 + 1)],
                      dtin_sb[:, tt * CH:(tt + 1) * CH],
                      start=True, stop=True, sem=ep)
                awaitx(ep, ev)
                bd.op("scalar", "activation", sp1b, pbank(0), AF.Abs,
                      bias=b_dtb_s[:, fb:fb + 1])
                bd.op("scalar", "activation", sp1b, sp1b, AF.Exp, scale=-1.0)
                bd.op("scalar", "activation", sp1b, sp1b, AF.Ln, bias=1.0)
                bd.op("scalar", "activation", sp2b, pbank(0), AF.Relu,
                      bias=b_dtb_s[:, fb:fb + 1], sem=ea)
                vwait(ea)
                kw = {}
                if want_dtsum:
                    kw["accum_out"] = dstage_b[:, 16 * tt + fbq: 16 * tt + fbq + 1]
                bd.op("vector", "scalar_tensor_tensor",
                      dt_fb[:, tt * CH:(tt + 1) * CH], sp1b, 1.0, sp2b,
                      ALU.mult, ALU.add, sem=ev, **kw)

        _reg = scr.take(128, 1040, F32)
        sp1b = _reg[:, 0:512]
        sp2b = _reg[:, 512:1024]
        xpad = [_reg[:, 0:520]]
        cacc = _reg[:, 520:1032]
        at2 = [at_, _reg[:, 0:T]]
        def ysl_ap(k, tt):
            o = (2 * k + tt) * CH
            return big2.ap()[:, o:o + CH]

        nfb3 = 0
        qa.wait("sync")
        for half in range(2):
            gwait(ep)
            qg.dma(bdtp_sb, b_dtprojT[:, 2048 * half:2048 * (half + 1)])
            bdtp_tok[0] = qg.flip()
            vwait(qa)
            for fbq in range(16):
                fb = 16 * half + fbq
                xw = xrw[0]
                bw = Bw[0]
                bd.wait("sync", ep, ep.n)
                bd.wait("sync", ev, ev.n)
                qs.dma(xw, xrb_d.ap()[32 * fb:32 * (fb + 1), :])
                qs.dma(bw, xrb_d.ap()[1024 + 32 * fb:1024 + 32 * (fb + 1), :])
                qs.flip()
                dt_softplus(fb, True)
                pwait(qs, qg)
                for c in range(2):
                    pwait(ea)
                    bd.op("tensor", "matmul", pbank(1),
                          selx_s[0:32, :], xw[0:32, c * CH:(c + 1) * CH],
                          start=True, stop=True)
                    bd.op("tensor", "matmul", pbank(2, 128, 4),
                          selx_s[0:32, :], bnd_sb[c][0:32, 4 * fb:4 * (fb + 1)],
                          start=True, stop=True, sem=ep)
                    awaitx(ep, ev)
                    awaitx(ev)
                    bd.op("scalar", "activation", xpad[0][:, 4:516], pbank(1),
                          AF.Copy)
                    bd.op("scalar", "activation", xpad[0][:, 0:4],
                          pbank(2, 128, 4), AF.Copy, sem=ea)
                    vwait(ea)
                    bd.op("vector", "tensor_scalar",
                          cacc, xpad[0][:, 1:513],
                          b_convw_s[:, 4 * fb:4 * fb + 1], None, ALU.mult)
                    for kk in range(1, 4):
                        bd.op("vector", "scalar_tensor_tensor", cacc,
                              xpad[0][:, 1 + kk:513 + kk],
                              b_convw_s[:, 4 * fb + kk:4 * fb + kk + 1],
                              cacc, ALU.mult, ALU.add)
                    bd.op("vector", "engine_nop", sem=ev)
                    awaitx(ev)
                    bd.op("scalar", "activation",
                          xact_bf[:, c * CH:(c + 1) * CH], cacc, AF.Silu,
                          bias=b_convb_s[:, fb:fb + 1], sem=ea)
                vwait(ea, ev)
                bd.op("vector", "tensor_mul", u_bf, dt_fb, xact_bf, sem=ev)
                awaitx(qa)
                zc_mms(fb, C_sb, 128 * 48, False)
                awaitx(ea)
                qa.dma(C_d.ap()[fb], C_sb)
                qa.flip()
                mka, mkB, mkC = {}, {}, {}
                _pv = bd.sems["ev"].n
                mkscan = {-2: _pv, -1: _pv}
                mkb = {-2: _pv, -1: _pv}
                mkym = {-2: _pv, -1: _pv}
                pwait(ea)

                def prep_nB(n, fb=fb):
                    BP = (1, 2) if n % 2 == 0 else (3, 4)
                    bd.wait("scalar", ev, mkscan[n - 2])
                    bd.op("scalar", "activation", at2[n % 2], dt_fb, AF.Exp,
                          scale=b_alog_s[:, NST * fb + n: NST * fb + n + 1],
                          sem=ea)
                    mka[n] = ea.n
                    bd.wait("tensor", ev, mkb[n - 2])
                    for tt in range(2):
                        bd.op("tensor", "matmul", pbank(BP[tt]),
                              selb_s[n][0:32, :],
                              bw[0:32, tt * CH:(tt + 1) * CH],
                              start=True, stop=True, sem=ep)
                    mkB[n] = ep.n

                def prep_nC(n, fb=fb):
                    bd.wait("tensor", ev, mkym[n - 1])
                    for tt in range(2):
                        bd.op("tensor", "matmul", pbank(5 if tt == 0 else 0),
                              selc_s[n], C_sb[:, tt * CH:(tt + 1) * CH],
                              start=True, stop=True, sem=ep)
                    mkC[n] = ep.n

                prep_nB(0)
                prep_nC(0)
                for n in range(NST):
                    if n + 1 < NST:
                        prep_nB(n + 1)
                    BP = (1, 2) if n % 2 == 0 else (3, 4)
                    bd.wait("vector", ea, mka[n])
                    bd.wait("vector", ep, mkB[n])
                    for tt in range(2):
                        bd.op("vector", "tensor_mul", bt_[:, tt * CH:(tt + 1) * CH],
                              u_bf[:, tt * CH:(tt + 1) * CH], pbank(BP[tt]))
                    mkb[n] = bd.sems["ev"].n
                    for tt in range(2):
                        bd.op("vector", "tensor_tensor_scan",
                              Ht[:, tt * CH:(tt + 1) * CH],
                              at2[n % 2][:, tt * CH:(tt + 1) * CH],
                              bt_[:, tt * CH:(tt + 1) * CH],
                              0.0, ALU.mult, ALU.add)
                        bd.op("vector", "tensor_copy",
                              fstage_b[:, 256 * tt + 16 * fbq + n:
                                       256 * tt + 16 * fbq + n + 1],
                              Ht[:, (tt + 1) * CH - 1:(tt + 1) * CH])
                    mkscan[n] = bd.sems["ev"].n
                    bd.wait("vector", ep, mkC[n])
                    for tt in range(2):
                        bd.op("vector", "tensor_mul", Ht[:, tt * CH:(tt + 1) * CH],
                              Ht[:, tt * CH:(tt + 1) * CH],
                              pbank(5 if tt == 0 else 0))
                    if n == 0:
                        for tt in range(2):
                            bd.op("vector", "tensor_copy", pbank(6 + tt),
                                  Ht[:, tt * CH:(tt + 1) * CH], sem=ev)
                    else:
                        for tt in range(2):
                            bd.op("vector", "tensor_add", pbank(6 + tt),
                                  pbank(6 + tt), Ht[:, tt * CH:(tt + 1) * CH],
                                  sem=ev)
                    mkym[n] = bd.sems["ev"].n
                    if n + 1 < NST:
                        prep_nC(n + 1)
                awaitx(qa)
                zc_mms(fb, z_bf, 0, True)
                awaitx(ea)
                qa.dma(sz_d.ap()[fb], z_bf)
                qa.flip()
                vwait(ea)
                for tt in range(2):
                    sl = slice(tt * CH, (tt + 1) * CH)
                    bd.op("vector", "scalar_tensor_tensor", Ht[:, sl],
                          xact_bf[:, sl], b_Dp_s[:, fb:fb + 1], pbank(6 + tt),
                          ALU.mult, ALU.add)
                    bd.op("vector", "tensor_mul", ysl_ap(fbq, tt),
                          Ht[:, sl], z_bf[:, sl], sem=ev)
                nfb3 += 1
            awaitx(ev)
            for c in range(2):
                qa.dma(b_pub[half].ap()[c, 0:2048]
                       .rearrange("(p k) -> p k", p=128),
                       dstage_b[:, 16 * c:16 * (c + 1)])
                qa.dma(b_pub[half].ap()[c, 2048:]
                       .rearrange("(p k) -> p k", p=128),
                       fstage_b[:, 256 * c:256 * (c + 1)])
            qa.flip()
            gwait(qa)
            ccg = bd.collective(f"cc_b_g{half}_", "AllGather", ALU.bypass, G8,
                                [b_pub[half].ap().opt()],
                                [b_gathb[half].ap().opt()])
            gwait(ccg)

            def s16_of(b2, c):
                r, s = _rank_of(b2, c)
                return 2 * r + s

            for b2 in range(2):
                for c in range(NCHK):
                    cseg = ccur if c % 2 else cprev
                    psg = cprev if c % 2 else ccur
                    vwait(qa)
                    if c == 0:
                        bd.op("vector", "memset", cseg, 0.0, sem=ev)
                    else:
                        ps16 = s16_of(b2, c - 1)
                        gwait(ev)
                        qg.dma(dts_roll[c % 2],
                               b_gathb[half].ap()[ps16, 0:2048]
                               .rearrange("(p k) -> p k", p=128))
                        qg.dma(F_roll[c % 2],
                               b_gathb[half].ap()[ps16, 2048:]
                               .rearrange("(p k) -> p k", p=128))
                        qg.flip()
                        vwait(qg)
                        awaitx(qg, ev)
                        for fbq in range(16):
                            fb = 16 * half + fbq
                            awaitx(ev)
                            bd.op("scalar", "activation", Ptmp,
                                  b_alog_s[:, NST * fb:NST * (fb + 1)], AF.Exp,
                                  scale=dts_roll[c % 2][:, fbq:fbq + 1], sem=ea)
                            vwait(ea)
                            bd.op("vector", "tensor_mul",
                                  cseg[:, 16 * fbq:16 * (fbq + 1)],
                                  psg[:, 16 * fbq:16 * (fbq + 1)], Ptmp)
                            bd.op("vector", "tensor_add",
                                  cseg[:, 16 * fbq:16 * (fbq + 1)],
                                  cseg[:, 16 * fbq:16 * (fbq + 1)],
                                  F_roll[c % 2][:, 16 * fbq:16 * (fbq + 1)],
                                  sem=ev)
                    awaitx(ev)
                    if c <= 3:
                        j, ci = 4 * b2 + c, 0
                    else:
                        j, ci = 4 * b2 + 7 - c, 1
                    qa.dma(b_sel_send[half].ap()[j, ci]
                           .rearrange("(p k) -> p k", p=128), cseg)
                    qa.flip()
            gwait(qa)
            ccs = bd.collective(f"cc_b_s{half}_", "AllToAll", ALU.bypass, G8,
                                [b_sel_send[half].ap().opt()],
                                [b_sel_recv[half].ap().opt()])
            gwait(ccs)
            gwait(ev)
            for ci in range(2):
                qg.dma(myc[ci],
                       b_sel_recv[half].ap()[0, ci]
                       .rearrange("(p k) -> p k", p=128))
            qg.flip()
            vwait(qg)
            awaitx(qg)
            swait(qa)
            DTb = dt_fb
            for fbq in range(16):
                fb = 16 * half + fbq
                dt_softplus(fb, False)
                for tt in range(2):
                    vwait(ev)
                    bd.op("vector", "tensor_tensor_scan",
                          Ht[:, tt * CH:(tt + 1) * CH], ones_f,
                          dt_fb[:, tt * CH:(tt + 1) * CH], 0.0,
                          ALU.mult, ALU.add, sem=ev)
                bd.wait("sync", ep, ep.n)
                bd.wait("sync", ev, ev.n)
                qs.dma(C_sb, C_d.ap()[fb])
                qs.dma(z_bf, sz_d.ap()[fb])
                qs.flip()
                pwait(qs)
                vwait(qs)
                mka, mkC = {}, {}
                _pv = bd.sems["ev"].n
                _pg = bd.sems["eg"].n
                mkstt = {-2: _pv, -1: _pv}
                mkg3f = {-2: _pg, -1: _pg}
                pwait(ea)

                def prep_fx(n, fb=fb):
                    CP = (1, 2) if n % 2 == 0 else (3, 4)
                    bd.wait("scalar", ev, mkstt[n - 2])
                    bd.op("scalar", "activation", at2[n % 2], Ht, AF.Exp,
                          scale=b_alog_s[:, NST * fb + n: NST * fb + n + 1],
                          sem=ea)
                    mka[n] = ea.n
                    bd.wait("tensor", ev, mkstt[n - 2])
                    for tt in range(2):
                        bd.op("tensor", "matmul", pbank(CP[tt]),
                              selc_s[n], C_sb[:, tt * CH:(tt + 1) * CH],
                              start=True, stop=True, sem=ep)
                    mkC[n] = ep.n

                prep_fx(0)
                for n in range(NST):
                    if n + 1 < NST:
                        prep_fx(n + 1)
                    CP = (1, 2) if n % 2 == 0 else (3, 4)
                    bd.wait("vector", ea, mka[n])
                    bd.wait("vector", ep, mkC[n])
                    bd.wait("vector", eg, mkg3f[n - 1])
                    for tt in range(2):
                        bd.op("vector", "scalar_tensor_tensor",
                              bt_[:, tt * CH:(tt + 1) * CH],
                              at2[n % 2][:, tt * CH:(tt + 1) * CH],
                              myc[tt][:, 16 * fbq + n: 16 * fbq + n + 1],
                              pbank(CP[tt]), ALU.mult, ALU.mult, sem=ev)
                    mkstt[n] = bd.sems["ev"].n
                    bd.wait("gpsimd", ev, mkstt[n])
                    if n == 0:
                        bd.op("gpsimd", "tensor_copy", fixsum, bt_)
                    else:
                        bd.op("gpsimd", "tensor_add", fixsum, fixsum, bt_)
                    mkg3f[n] = bd.sems["eg"].n
                vwait(eg)
                for tt in range(2):
                    sl = slice(tt * CH, (tt + 1) * CH)
                    ysl = ysl_ap(fbq, tt)
                    bd.op("vector", "tensor_mul", bt_[:, sl], fixsum[:, sl],
                          z_bf[:, sl])
                    bd.op("vector", "tensor_add", ysl, ysl, bt_[:, sl], sem=ev)
            for m in range(NF):
                wb = wbuf[m % 2]
                gwait(ep)
                qg.dma(wb.rearrange("p (k mm) -> p k mm", k=NF),
                       b_outprojT[2048 * half:2048 * (half + 1),
                                  128 * m:128 * (m + 1)]
                       .rearrange("(k p) mm -> p k mm", p=128))
                qg.flip()
                pwait(qg)
                pwait(ev)
                for tt in range(2):
                    dbank = 4 + tt
                    sl = slice(tt * CH, (tt + 1) * CH)
                    pwait(ev)
                    for k in range(16):
                        bd.op("tensor", "matmul", pbank(dbank),
                              wb[:, 128 * k:128 * (k + 1)], ysl_ap(k, tt),
                              start=(k == 0), stop=(k == 15),
                              sem=(ep if k == 15 else None))
                    vwait(ep)
                    bd.op("vector", "tensor_add", xt(m)[:, sl], xt(m)[:, sl],
                          pbank(dbank), sem=ev)
        scr.pop()
        barrier()

    def phase_out():
        swait(ev, ea, ep, eg)
        for f in range(NF):
            qs.dma(y0[128 * f:128 * (f + 1), :], xt(f))
        qs.wait("sync")

    phases = [phase0, phase1, lambda: phase_ffn('f1'), phase_b,
              lambda: phase_ffn('f2')]
    for i, ph in enumerate(phases):
        if i < n_phases:
            ph()
    phase_out()

    block = es.enter_context(nc.Block())

    @block.sync
    def _(e):
        for fn in bd.ops["sync"]:
            fn(e)

    @block.tensor
    def _(e):
        for fn in bd.ops["tensor"]:
            fn(e)

    @block.vector
    def _(e):
        for fn in bd.ops["vector"]:
            fn(e)

    @block.scalar
    def _(e):
        for fn in bd.ops["scalar"]:
            fn(e)

    @block.gpsimd
    def _(e):
        for fn in bd.ops["gpsimd"]:
            fn(e)

    es.close()
    return nc



def _host_constants():
    selx = np.zeros((32, 128), np.float32)
    for ch in range(128):
        selx[(ch // 64) * 16 + (ch % 16), ch] = 1.0
    selb = np.zeros((NST, 32, 128), np.float32)
    for n in range(NST):
        for ch in range(128):
            selb[n, (ch // 64) * 16 + n, ch] = 1.0
    selc = np.zeros((NST, 128, 128), np.float32)
    for n in range(NST):
        for ch in range(128):
            selc[n, (ch // 16) * 16 + n, ch] = 1.0
    selone = np.zeros((16, 16, 128), np.float32)
    for n in range(16):
        selone[n, n, :] = 1.0
    trimask = np.zeros((4, 128, CH), np.float32)
    for s in range(4):
        for k in range(128):
            kg = 128 * s + k
            trimask[s, k, :kg] = BIG_NEG
    return selx, selb, selc, selone, trimask


def prepare_inputs(full):
    f32 = np.float32
    cT = np.ascontiguousarray
    x = np.asarray(full["L0_attn_a_x"], f32)
    selx, selb, selc, selone, trimask = _host_constants()
    shared = {
        "a_xprojT": cT(np.asarray(full["L0_attn_a_W_xproj"], f32).T),
        "a_dtT": cT(np.asarray(full["L0_attn_a_W_dt"], f32).T),
        "a_bdt": cT(np.asarray(full["L0_attn_a_b_dt"], f32).reshape(NF, 128, 1)),
        "a_alog": cT(np.asarray(full["L0_attn_a_A_log"], f32).reshape(NF, 128, NSA)),
        "a_Dp": cT(np.asarray(full["L0_attn_a_D_param"], f32).reshape(NF, 128, 1)),
        "b_inprojT": cT(np.asarray(full["L0_attn_b_in_proj_w"], f32).T),
        "b_dtinT": cT(np.asarray(full["L0_attn_b_dt_in_proj_w"], f32).T),
        "b_dtprojT": cT(np.asarray(full["L0_attn_b_dt_proj_w"], f32).T),
        "b_dtb": cT(np.asarray(full["L0_attn_b_dt_proj_b"], f32).reshape(NFB, 128, 1)),
        "b_convw": cT(np.asarray(full["L0_attn_b_conv_w"], f32).reshape(NFB, 128, 4)),
        "b_convb": cT(np.asarray(full["L0_attn_b_conv_b"], f32).reshape(NFB, 128, 1)),
        "b_alog": cT(np.asarray(full["L0_attn_b_A_log"], f32).reshape(NFB, 128, NST)),
        "b_Dp": cT(np.asarray(full["L0_attn_b_D"], f32).reshape(NFB, 128, 1)),
        "b_outprojT": cT(np.asarray(full["L0_attn_b_out_proj_w"], f32).T),
        "f_WqT": cT(np.asarray(full["L0_ffn_Wq"], f32).T),
        "f_WkT": cT(np.asarray(full["L0_ffn_Wk"], f32).T),
        "f_WvT": cT(np.asarray(full["L0_ffn_Wv"], f32).T),
        "f_Wo": cT(np.asarray(full["L0_ffn_Wo"], f32)),
        "f_A": cT(np.asarray(full["L0_ffn_A"], f32)),
        "f_B": cT(np.asarray(full["L0_ffn_B"], f32)),
        "f_ln1": cT(np.asarray(full["L0_ffn_ln1_w"], f32).reshape(NF, 128, 1)),
        "f_ln2": cT(np.asarray(full["L0_ffn_ln2_w"], f32).reshape(NF, 128, 1)),
        "f_WinT": cT(np.asarray(full["L0_ffn_W_in_base"], f32).T),
        "f_WupT": cT(np.asarray(full["L0_ffn_W_up_base"], f32).T),
        "f_WdownT": cT(np.asarray(full["L0_ffn_W_down"], f32).T),
        "selx": selx, "selb": selb, "selc": selc, "selone": selone,
        "trimask": trimask,
    }
    in_maps = []
    for c in range(8):
        b = c // 4
        jA, jB = c % 4, 7 - c % 4
        xc = np.concatenate([x[b, CH * jA:CH * (jA + 1)],
                             x[b, CH * jB:CH * (jB + 1)]], axis=0)
        m = {"x0": np.ascontiguousarray(xc.T)}
        m.update(shared)
        in_maps.append(m)
    return in_maps


def gather_output(results):
    y = np.empty((NB, 4096, D), np.float32)
    for c in range(8):
        b = c // 4
        jA, jB = c % 4, 7 - c % 4
        out = results[c]["y0"]
        y[b, CH * jA:CH * (jA + 1)] = out[:, :CH].T
        y[b, CH * jB:CH * (jB + 1)] = out[:, CH:].T
    return y


_NC_CACHE = {}


def kernel(**inputs):
    key = N_PHASES
    if key not in _NC_CACHE:
        _NC_CACHE[key] = build_program(N_PHASES)
    nc = _NC_CACHE[key]
    in_maps = prepare_inputs(inputs)
    res = run_bass_kernel_spmd(nc, in_maps, core_ids=list(range(8)))
    return gather_output(res.results)


# version 07
# speedup vs baseline: 1.2222x; 1.1821x over previous
import contextlib
import numpy as np
import concourse.bass as bass
import concourse.mybir as mybir
from concourse.bass_utils import run_bass_kernel_spmd
from concourse.aot_env import allow_ici_all_to_all

F32 = mybir.dt.float32
BF16 = mybir.dt.bfloat16
AF = mybir.ActivationFunctionType
ALU = mybir.AluOpType
AX = mybir.AxisListType

D = 2048
NF = D // 128
T = 1024
CH = 512
NB = 2
NCHK = 8
NH = 16
DI = 4096
NFB = DI // 128
NST = 16
NSA = 8
FF = 8192
NFF = FF // 128
EPS_LN = 1e-5
EPS_RMS = 1e-6
BIG_NEG = -1.0e30
BIAS_OFF = 60.0
QSCALE = float(1.0 / np.sqrt(128.0))

N_PHASES = 99


def _chunk_owner(c):
    return (c, 0) if c <= 3 else (7 - c, 1)


def _rank_of(b, c):
    r, s = _chunk_owner(c)
    return 4 * b + r, s


class Sem:
    def __init__(self, nc, es, name):
        self.h = es.enter_context(nc.semaphore(name))
        self.n = 0


class Queue:

    def __init__(self, bd, eng):
        self.bd = bd
        self.eng = eng
        self.sems = [bd.sem(f"q_{eng}_a"), bd.sem(f"q_{eng}_b")]
        self.epoch = 0
        self.guarded = False
        self.totals = []

    def cur(self):
        return self.sems[self.epoch % 2]

    def dma(self, out, in_, transpose=False):
        self.pending = getattr(self, "pending", 0) + 1
        if not self.guarded:
            if self.epoch >= 2:
                si, tot = self.totals[self.epoch - 2]
                self.bd.wait(self.eng, self.sems[si], tot)
            self.guarded = True
        s = self.cur()
        s.n += 16

        def fn(e, out=out, in_=in_, h=s.h, tr=transpose):
            kw = {"transpose": True} if tr else {}
            e.dma_start(out=out, in_=in_, **kw).then_inc(h, 16)

        self.bd.emit(self.eng, fn)

    def flip(self):
        if getattr(self, "pending", 0) == 0:
            return self.epoch - 1
        s = self.cur()
        self.totals.append((self.epoch % 2, s.n))
        tok = self.epoch
        self.epoch += 1
        self.guarded = False
        self.pending = 0
        return tok

    def wait(self, eng, tok=None):
        if tok is None:
            if getattr(self, "pending", 0):
                self.flip()
            last = [0, 0]
            for si, tot in self.totals:
                last[si] = max(last[si], tot)
            for si in (0, 1):
                if last[si] > 0:
                    self.bd.wait(eng, self.sems[si], last[si])
            return
        if tok < 0 or tok >= len(self.totals):
            return
        si, tot = self.totals[tok]
        self.bd.wait(eng, self.sems[si], tot)


class Builder:
    ENGS = ("sync", "tensor", "vector", "scalar", "gpsimd")

    def __init__(self, nc, es):
        self.nc = nc
        self.es = es
        self.ops = {e: [] for e in self.ENGS}
        self.sems = {}

    def sem(self, name):
        if name not in self.sems:
            self.sems[name] = Sem(self.nc, self.es, name)
        return self.sems[name]

    def emit(self, eng, fn):
        self.ops[eng].append(fn)

    def wait(self, eng, sem, val=None):
        v = sem.n if val is None else val
        if v <= 0:
            return
        self.emit(eng, lambda e, h=sem.h, v=v: e.wait_ge(h, v))

    CHAIN = {"vector": "ev", "scalar": "ea", "gpsimd": "eg"}

    def op(self, eng, meth, *args, sem=None, seq=True, **kwargs):
        if eng in self.CHAIN:
            own = self.sem(self.CHAIN[eng])
            assert sem is None or sem is own, "use the engine chain sem"
            if seq and own.n > 0:
                self.wait(eng, own, own.n)
            own.n += 1

            def fn(e, a=args, k=kwargs, h=own.h):
                getattr(e, meth)(*a, **k).then_inc(h, 1)
        elif sem is not None:
            sem.n += 1

            def fn(e, a=args, k=kwargs, h=sem.h):
                getattr(e, meth)(*a, **k).then_inc(h, 1)
        else:
            def fn(e, a=args, k=kwargs):
                getattr(e, meth)(*a, **k)

        self.emit(eng, fn)

    def collective(self, name, kind, op_, groups, ins, outs):
        s = self.sem(name)
        s.n += 1

        def fn(e, kind=kind, op_=op_, groups=groups, ins=ins, outs=outs, h=s.h):
            with allow_ici_all_to_all():
                e.collective_compute(kind, op_, replica_groups=groups,
                                     ins=ins, outs=outs).then_inc(h)

        self.emit("gpsimd", fn)
        return s


class Carver:
    def __init__(self, tensor, nbytes):
        self.t = tensor
        self.nbytes = nbytes
        self.off = 0
        self.marks = []
        self.peak = 0

    def take(self, rows, cols, dtype, align=64):
        isz = 4 if dtype == F32 else 2
        self.off = (self.off + align - 1) // align * align
        start = self.off
        self.off += cols * isz
        self.peak = max(self.peak, self.off)
        assert self.off <= self.nbytes, f"scratch overflow: {self.off} > {self.nbytes}"
        base = self.t.ap().bitcast(dtype)
        return base[0:rows, start // isz: start // isz + cols]

    def push(self):
        self.marks.append(self.off)

    def pop(self):
        self.off = self.marks.pop()


def build_program(n_phases=99):
    nc = bass.Bass("TRN2", target_bir_lowering=False)

    def inp(name, shape):
        return nc.dram_tensor(name, list(shape), F32, kind="ExternalInput").ap()

    x0 = inp("x0", (D, T))
    a_xprojT = inp("a_xprojT", (D, 272))
    a_dtT = inp("a_dtT", (256, D))
    a_bdt = inp("a_bdt", (NF, 128, 1))
    a_alog = inp("a_alog", (NF, 128, NSA))
    a_Dp = inp("a_Dp", (NF, 128, 1))
    b_inprojT = inp("b_inprojT", (D, 10240))
    b_dtinT = inp("b_dtinT", (D, 128))
    b_dtprojT = inp("b_dtprojT", (128, DI))
    b_dtb = inp("b_dtb", (NFB, 128, 1))
    b_convw = inp("b_convw", (NFB, 128, 4))
    b_convb = inp("b_convb", (NFB, 128, 1))
    b_alog = inp("b_alog", (NFB, 128, NST))
    b_Dp = inp("b_Dp", (NFB, 128, 1))
    b_outprojT = inp("b_outprojT", (DI, D))
    f_WqT = inp("f_WqT", (D, D))
    f_WkT = inp("f_WkT", (D, D))
    f_WvT = inp("f_WvT", (D, D))
    f_Wo = inp("f_Wo", (D, D))
    f_A = inp("f_A", (D, NST))
    f_B = inp("f_B", (NST, FF))
    f_ln1 = inp("f_ln1", (NF, 128, 1))
    f_ln2 = inp("f_ln2", (NF, 128, 1))
    f_WinT = inp("f_WinT", (D, FF))
    f_WupT = inp("f_WupT", (D, FF))
    f_WdownT = inp("f_WdownT", (FF, D))
    selx_d = inp("selx", (32, 128))
    selb_d = inp("selb", (NST, 32, 128))
    selc_d = inp("selc", (NST, 128, 128))
    selone_d = inp("selone", (16, 16, 128))
    trimask_d = inp("trimask", (4, 128, CH))

    y0 = nc.dram_tensor("y0", [D, T], F32, kind="ExternalOutput").ap()

    kvq_send = nc.dram_tensor("kvq_send", [8, 6, 128, T], BF16)
    kvq_recv = nc.dram_tensor("kvq_recv", [8, 6, 128, T], BF16)
    o_send = nc.dram_tensor("o_send", [8, 2, 128, T], BF16)
    o_recv = nc.dram_tensor("o_recv", [8, 2, 128, T], BF16)
    a_pub = nc.dram_tensor("a_pub", [2, D + D * NSA], F32)
    a_gath = nc.dram_tensor("a_gath", [16, D + D * NSA], F32, addr_space="Shared")
    a_sel_send = nc.dram_tensor("a_sel_send", [8, 2, D * NSA], F32)
    a_sel_recv = nc.dram_tensor("a_sel_recv", [8, 2, D * NSA], F32)
    bnd_pub = nc.dram_tensor("bnd_pub", [2, 1024, 4], F32)
    bnd_gath = nc.dram_tensor("bnd_gath", [16, 1024, 4], F32, addr_space="Shared")
    bnd_send = nc.dram_tensor("bnd_send", [8, 2, 1024, 4], F32)
    bnd_recv = nc.dram_tensor("bnd_recv", [8, 2, 1024, 4], F32)
    b_pub = [nc.dram_tensor(f"b_pub{h}", [2, 2048 + 2048 * NST], F32) for h in range(2)]
    b_gathb = [nc.dram_tensor(f"b_gath{h}", [16, 2048 + 2048 * NST], F32,
                              addr_space="Shared") for h in range(2)]
    b_sel_send = [nc.dram_tensor(f"b_sel_send{h}", [8, 2, 2048 * NST], F32)
                  for h in range(2)]
    b_sel_recv = [nc.dram_tensor(f"b_sel_recv{h}", [8, 2, 2048 * NST], F32)
                  for h in range(2)]
    xrb_d = nc.dram_tensor("xrb_d", [2048, T], BF16)
    C_d = nc.dram_tensor("C_d", [NFB, 128, T], BF16)
    sz_d = nc.dram_tensor("sz_d", [NFB, 128, T], BF16)
    b01_d = nc.dram_tensor("b01_d", [NST, FF], BF16)

    es = contextlib.ExitStack()
    bd = Builder(nc, es)

    x_sb = nc.alloc_sbuf_tensor("x_sb", [128, NF * T], F32)
    big1 = nc.alloc_sbuf_tensor("big1", [128, NF * T], BF16)
    big2 = nc.alloc_sbuf_tensor("big2", [128, NF * T], BF16)
    SCR_BYTES = 81600
    scr_t = nc.alloc_sbuf_tensor("scr", [128, SCR_BYTES], mybir.dt.uint8)
    scr = Carver(scr_t, SCR_BYTES)
    bigc = Carver(big2, NF * T * 2)

    def xt(f):
        return x_sb.ap()[:, T * f: T * (f + 1)]

    def b1(f):
        return big1.ap()[:, T * f: T * (f + 1)]

    def b2(f):
        return big2.ap()[:, T * f: T * (f + 1)]

    selx_s = scr.take(32, 128, BF16)
    selb_s = [scr.take(32, 128, BF16) for _ in range(NST)]
    selc_s = [scr.take(128, 128, BF16) for _ in range(NST)]
    ones_bf = scr.take(128, 128, BF16)
    ones_f1 = scr.take(1, 128, F32)
    ones_f = scr.take(128, CH, F32)
    CONSTVALS = [0.0, 1.0, -1.0, -0.5, EPS_LN, EPS_RMS, BIAS_OFF, -BIAS_OFF, 0.5]
    const_tile = scr.take(128, len(CONSTVALS), F32)
    for ci_, cv_ in enumerate(CONSTVALS):
        nc.const_aps.register(F32, cv_, const_tile[:, ci_:ci_ + 1])
    wc_s = [scr.take(128, NST, BF16) for _ in range(NH)]
    a_bdt_s = scr.take(128, NF, F32)
    a_alog_s = scr.take(128, NF * NSA, F32)
    a_Dp_s = scr.take(128, NF, F32)
    b_dtb_s = scr.take(128, NFB, F32)
    b_convw_s = scr.take(128, NFB * 4, F32)
    b_convb_s = scr.take(128, NFB, F32)
    b_alog_s = scr.take(128, NFB * NST, F32)
    b_Dp_s = scr.take(128, NFB, F32)
    f_ln1_s = scr.take(128, NF, F32)
    f_ln2_s = scr.take(128, NF, F32)
    f_A_s = [scr.take(128, NST, BF16) for _ in range(NF)]
    rows = scr.take(1, 2 * T, F32)

    def row(i, tt=None):
        if tt is None:
            return rows[:, i * T:(i + 1) * T]
        return rows[:, i * T + tt * CH: i * T + (tt + 1) * CH]

    pb = [nc.alloc_psum_tensor(f"pb{i}", [128, CH], F32) for i in range(8)]

    def pbank(i, rows_=128, cols=CH):
        return pb[i].ap()[0:rows_, 0:cols]

    def pbank_bf(i, sl=None):
        v = pb[i].ap().bitcast(BF16)
        if sl is None:
            return v[0:128, 0:2 * CH]
        return v[0:128, sl * CH:(sl + 1) * CH]

    qs = Queue(bd, "sync")
    qg = Queue(bd, "gpsimd")
    qa = Queue(bd, "scalar")
    qv = Queue(bd, "vector")

    G8 = [list(range(8))]

    ev = bd.sem("ev")
    ea = bd.sem("ea")
    ep = bd.sem("ep")
    eg = bd.sem("eg")
    esy = bd.sem("esy")

    def barrier():
        qg.wait("gpsimd")
        qa.wait("scalar")
        qs.wait("sync")
        bd.wait("tensor", ep, ep.n)
        bd.wait("sync", esy, esy.n)
        bd.op("gpsimd", "nop", sem=eg)
        bd.op("scalar", "nop", sem=ea)
        bd.op("sync", "nop", sem=esy)
        bd.op("vector", "nop", sem=ev)
        bd.op("tensor", "nop", sem=ep)
        marks = [(ev, "vector"), (ea, "scalar"), (ep, "tensor"),
                 (eg, "gpsimd"), (esy, "sync")]
        for s, _ in marks:
            for _, eng in marks:
                bd.wait(eng, s, s.n)

    def _w(eng, sems):
        for s in sems:
            if isinstance(s, Queue):
                s.wait(eng)
            else:
                bd.wait(eng, s)

    def vwait(*sems):
        _w("vector", sems)

    def awaitx(*sems):
        _w("scalar", sems)

    def pwait(*sems):
        _w("tensor", sems)

    def gwait(*sems):
        _w("gpsimd", sems)

    def swait(*sems):
        _w("sync", sems)

    def phase0():
        qg.dma(selx_s, selx_d)
        for n in range(NST):
            qg.dma(selb_s[n], selb_d[n])
            qg.dma(selc_s[n], selc_d[n])
        for f in range(NF):
            qs.dma(a_bdt_s[:, f:f + 1], a_bdt[f])
            qs.dma(a_alog_s[:, NSA * f:NSA * (f + 1)], a_alog[f])
            qs.dma(a_Dp_s[:, f:f + 1], a_Dp[f])
            qs.dma(f_ln1_s[:, f:f + 1], f_ln1[f])
            qs.dma(f_ln2_s[:, f:f + 1], f_ln2[f])
            qg.dma(f_A_s[f], f_A[128 * f:128 * (f + 1), :])
        for f in range(NFB):
            qs.dma(b_dtb_s[:, f:f + 1], b_dtb[f])
            qs.dma(b_convw_s[:, 4 * f:4 * (f + 1)], b_convw[f])
            qs.dma(b_convb_s[:, f:f + 1], b_convb[f])
            qs.dma(b_alog_s[:, NST * f:NST * (f + 1)], b_alog[f])
            qs.dma(b_Dp_s[:, f:f + 1], b_Dp[f])
        for f in range(NF):
            qs.dma(xt(f), x0[128 * f:128 * (f + 1), :])
        scr.push()
        FQ = FF // 4
        tmpB = scr.take(NST, FQ, F32)
        tmpB2 = scr.take(NST, FQ, BF16)
        vwait(qs, qg)
        bd.op("vector", "memset", ones_bf, 1.0)
        bd.op("vector", "memset", ones_f1, 1.0)
        bd.op("vector", "memset", ones_f, 1.0)
        for ci_, cv_ in enumerate(CONSTVALS):
            bd.op("vector", "memset", const_tile[:, ci_:ci_ + 1], cv_)
        bd.op("vector", "engine_nop", sem=ev)
        awaitx(ev)
        for q4 in range(4):
            swait(ev)
            qs.dma(tmpB, f_B[:, FQ * q4:FQ * (q4 + 1)])
            vwait(qs)
            bd.op("vector", "tensor_scalar_mul", tmpB2, tmpB, 0.1, sem=ev)
            swait(ev)
            qs.dma(b01_d.ap()[:, FQ * q4:FQ * (q4 + 1)], tmpB2)
        scr.push()
        atmp = scr.take(128, NFB * NST, F32)
        awaitx(qs)
        bd.op("scalar", "activation", atmp[:, 0:NF * NSA], a_alog_s, AF.Exp, sem=ea)
        vwait(ea)
        bd.op("vector", "tensor_scalar_mul", a_alog_s, atmp[:, 0:NF * NSA], -1.0, sem=ev)
        awaitx(ev)
        bd.op("scalar", "activation", atmp, b_alog_s, AF.Exp, sem=ea)
        vwait(ea)
        bd.op("vector", "tensor_scalar_mul", b_alog_s, atmp, -1.0, sem=ev)
        scr.pop()
        barrier()
        woT = [scr.take(128, 128, BF16) for _ in range(2)]
        nmm = 0
        for h in range(NH):
            for k in range(NF):
                buf = woT[(h * NF + k) % 2]
                if h * NF + k >= 2:
                    gwait(ep)
                qg.dma(buf, f_Wo[128 * k:128 * (k + 1),
                                           128 * h:128 * (h + 1)])
                pwait(qg)
                bd.op("tensor", "matmul", pbank(0, 128, NST), buf, f_A_s[k],
                      start=(k == 0), stop=(k == NF - 1), sem=ep)
                nmm += 1
            awaitx(ep)
            bd.op("scalar", "activation", wc_s[h], pbank(0, 128, NST), AF.Copy,
                  sem=ea)
            pwait(ea)
        scr.pop()
        barrier()

    def ln_stats(tag, with_rms, s1seg=None):
        scr.push()
        xb = [scr.take(128, T, BF16) for _ in range(2)]
        xb2 = [scr.take(128, T, BF16) for _ in range(2)]
        for f in range(NF):
            b = f % 2
            if f >= 2:
                vwait(ep)
            bd.op("vector", "tensor_copy", xb[b], xt(f), sem=ev)
            awaitx(ev)
            bd.op("scalar", "activation", xb2[b], xb[b], AF.Square, sem=ea)
            pwait(ev, ea)
            for tt in range(2):
                bd.op("tensor", "matmul", pbank(0 if tt == 0 else 6, 1, CH),
                      ones_bf[:, 0:1], xb[b][:, tt * CH:(tt + 1) * CH],
                      start=(f == 0), stop=(f == NF - 1), sem=None)
                bd.op("tensor", "matmul", pbank(1 if tt == 0 else 7, 1, CH),
                      ones_bf[:, 0:1], xb2[b][:, tt * CH:(tt + 1) * CH],
                      start=(f == 0), stop=(f == NF - 1), sem=ep)
        scr.pop()
        scr.push()
        tmp = scr.take(1, T, F32)
        t2 = scr.take(1, T, F32)
        vwait(ep)
        for tt in range(2):
            sb, qb = (0, 1) if tt == 0 else (6, 7)
            bd.op("vector", "tensor_scalar_mul", row(0, tt), pbank(sb, 1, CH), 1.0 / D)
            bd.op("vector", "tensor_scalar_mul", row(1, tt), pbank(qb, 1, CH), 1.0 / D)
        bd.op("vector", "tensor_mul", tmp, row(0), row(0))
        bd.op("vector", "tensor_sub", row(1), row(1), tmp, sem=ev)
        awaitx(ev)
        bd.op("scalar", "activation", tmp, row(1), AF.Ln, bias=EPS_LN, sem=ea)
        if with_rms:
            vwait(ea)
            bd.op("vector", "tensor_scalar_add", t2, row(1), EPS_LN)
            bd.op("vector", "reciprocal", t2, t2)
            bd.op("vector", "tensor_mul", t2, row(1), t2, sem=ev)
            awaitx(ev)
        bd.op("scalar", "activation", row(1), tmp, AF.Exp, scale=-0.5, sem=ea)
        if with_rms:
            awaitx(ev)
            bd.op("scalar", "activation", t2, t2, AF.Ln, bias=EPS_RMS)
            bd.op("scalar", "activation", t2, t2, AF.Exp, scale=-0.5, sem=ea)
            vwait(ea)
            bd.op("vector", "tensor_mul", s1seg, row(1), t2, sem=ev)
        scr.pop()

    def bcast_row(bank, rseg):
        bd.op("tensor", "matmul", pbank(bank), ones_f1[:, :], rseg,
              start=True, stop=True, sem=ep)

    def phase1():
        scr.push()
        bigc.push()
        selone_s = [scr.take(16, 128, BF16) for _ in range(16)]
        aw_xproj = [bigc.take(128, 272, BF16) for _ in range(NF)]
        aw_dt = [[bigc.take(128, 128, BF16) for _ in range(NF)] for _ in range(2)]
        for n in range(16):
            qg.dma(selone_s[n], selone_d[n])
        for f in range(NF):
            qg.dma(aw_xproj[f], a_xprojT[128 * f:128 * (f + 1), :])
            qg.dma(aw_dt[0][f], a_dtT[0:128, 128 * f:128 * (f + 1)])
            qg.dma(aw_dt[1][f], a_dtT[128:256, 128 * f:128 * (f + 1)])
        pwait(qg)
        vwait(qg)
        ln_stats("p1", with_rms=False)
        pwait(ev, ea)
        bcast_row(2, row(0, 0))
        bcast_row(3, row(1, 0))
        bcast_row(4, row(0, 1))
        bcast_row(5, row(1, 1))
        MU, RS = [2, 4], [3, 5]
        xc = scr.take(128, CH, F32)
        xa32 = scr.take(128, CH, F32)
        vwait(ep)
        for f in range(NF):
            for tt in range(2):
                sl = slice(tt * CH, (tt + 1) * CH)
                bd.op("vector", "tensor_sub", xc, xt(f)[:, sl], pbank(MU[tt]))
                bd.op("vector", "tensor_mul", xa32, xc, pbank(RS[tt]))
                bd.op("vector", "tensor_copy", b1(f)[:, sl], xa32)
                bd.op("vector", "scalar_tensor_tensor", xt(f)[:, sl], xa32,
                      a_Dp_s[:, f:f + 1], xt(f)[:, sl], ALU.mult, ALU.add, sem=ev)
        dtr = [bigc.take(128, T, BF16) for _ in range(2)]
        bca = scr.take(16, T, BF16)
        for tt in range(2):
            for mb in range(3):
                mrows = 128 if mb < 2 else 16
                pwait(ev, ea)
                for k in range(NF):
                    bd.op("tensor", "matmul", pbank(6, mrows, CH),
                          aw_xproj[k][:, 128 * mb:128 * mb + mrows],
                          b1(k)[:, tt * CH:(tt + 1) * CH],
                          start=(k == 0), stop=(k == NF - 1),
                          sem=(ep if k == NF - 1 else None))
                awaitx(ep)
                dst = (dtr[mb][:, tt * CH:(tt + 1) * CH] if mb < 2
                      else bca[:, tt * CH:(tt + 1) * CH])
                bd.op("scalar", "activation", dst, pbank(6, mrows, CH), AF.Copy,
                      sem=ea)
        delta = bigc.take(128, T, F32)
        sp1 = scr.take(128, CH, F32)
        sp2 = scr.take(128, CH, F32)
        fstage = scr.take(128, 2 * NF * NSA, F32)
        dstage = scr.take(128, 2 * NF, F32)
        bigc.push()
        scr.push()
        ub = bigc.take(128, T, BF16)
        at_ = scr.take(128, T, F32)
        bt_ = scr.take(128, T, F32)
        Ht = bigc.take(128, T, F32)
        at2 = [at_, scr.t.ap().bitcast(F32)[0:128, sp1.offset:sp1.offset + T]]
        for f in range(NF):
            for tt in range(2):
                pwait(ev, ea)
                for k in range(2):
                    bd.op("tensor", "matmul", pbank(0),
                          aw_dt[k][f], dtr[k][:, tt * CH:(tt + 1) * CH],
                          start=(k == 0), stop=(k == 1),
                          sem=(ep if k == 1 else None))
                awaitx(ep, ev)
                bd.op("scalar", "activation", sp1, pbank(0), AF.Abs,
                      bias=a_bdt_s[:, f:f + 1])
                bd.op("scalar", "activation", sp1, sp1, AF.Exp, scale=-1.0)
                bd.op("scalar", "activation", sp1, sp1, AF.Ln, bias=1.0)
                bd.op("scalar", "activation", sp2, pbank(0), AF.Relu,
                      bias=a_bdt_s[:, f:f + 1], sem=ea)
                vwait(ea)
                bd.op("vector", "scalar_tensor_tensor",
                      delta[:, tt * CH:(tt + 1) * CH], sp1, 1.0, sp2,
                      ALU.mult, ALU.add,
                      accum_out=dstage[:, NF * tt + f: NF * tt + f + 1], sem=ev)
                bd.op("vector", "tensor_mul", ub[:, tt * CH:(tt + 1) * CH],
                      delta[:, tt * CH:(tt + 1) * CH],
                      b1(f)[:, tt * CH:(tt + 1) * CH], sem=ev)
            mka, mkB, mkC = {}, {}, {}
            _pv = bd.sems["ev"].n
            _pg = bd.sems["eg"].n
            mkscan = {-2: _pv, -1: _pv}
            mkb = {-2: _pv, -1: _pv}
            mkym = {-2: _pv, -1: _pv}
            mkg = {-2: _pg, -1: _pg}
            pwait(ea)

            def prep1B(n, f=f):
                BP = (1, 2) if n % 2 == 0 else (3, 4)
                bd.wait("scalar", ev, mkscan[n - 2])
                bd.op("scalar", "activation", at2[n % 2], delta, AF.Exp,
                      scale=a_alog_s[:, NSA * f + n: NSA * f + n + 1], sem=ea)
                mka[n] = ea.n
                bd.wait("tensor", ev, mkb[n - 2])
                for tt in range(2):
                    bd.op("tensor", "matmul", pbank(BP[tt]),
                          selone_s[n][0:16, :], bca[0:16, tt * CH:(tt + 1) * CH],
                          start=True, stop=True, sem=ep)
                mkB[n] = ep.n

            def prep1C(n, f=f):
                bd.wait("tensor", ev, mkym[n - 1])
                for tt in range(2):
                    bd.op("tensor", "matmul", pbank(5 + tt),
                          selone_s[8 + n][0:16, :], bca[0:16, tt * CH:(tt + 1) * CH],
                          start=True, stop=True, sem=ep)
                mkC[n] = ep.n

            prep1B(0)
            prep1C(0)
            for n in range(NSA):
                if n + 1 < NSA:
                    prep1B(n + 1)
                BP = (1, 2) if n % 2 == 0 else (3, 4)
                bd.wait("vector", ea, mka[n])
                bd.wait("vector", ep, mkB[n])
                for tt in range(2):
                    bd.op("vector", "tensor_mul", bt_[:, tt * CH:(tt + 1) * CH],
                          ub[:, tt * CH:(tt + 1) * CH], pbank(BP[tt]))
                mkb[n] = bd.sems["ev"].n
                bd.wait("vector", eg, mkg[n - 1])
                for tt in range(2):
                    bd.op("vector", "tensor_tensor_scan",
                          Ht[:, tt * CH:(tt + 1) * CH],
                          at2[n % 2][:, tt * CH:(tt + 1) * CH],
                          bt_[:, tt * CH:(tt + 1) * CH],
                          0.0, ALU.mult, ALU.add)
                    bd.op("vector", "tensor_copy",
                          fstage[:, 128 * tt + NSA * f + n: 128 * tt + NSA * f + n + 1],
                          Ht[:, (tt + 1) * CH - 1:(tt + 1) * CH])
                mkscan[n] = bd.sems["ev"].n
                bd.wait("vector", ep, mkC[n])
                bd.wait("vector", eg, mkg[n - 1])
                for tt in range(2):
                    bd.op("vector", "tensor_mul", Ht[:, tt * CH:(tt + 1) * CH],
                          Ht[:, tt * CH:(tt + 1) * CH], pbank(5 + tt))
                mkym[n] = bd.sems["ev"].n
                bd.wait("gpsimd", ev, mkym[n])
                bd.op("gpsimd", "tensor_add", xt(f), xt(f), Ht)
                mkg[n] = bd.sems["eg"].n
                if n + 1 < NSA:
                    prep1C(n + 1)
        bigc.pop()
        scr.pop()
        barrier()
        awaitx(ev)
        for c in range(2):
            qa.dma(
                   a_pub[c, 0:D].rearrange("(p f) -> p f", p=128),
                   dstage[:, NF * c:NF * (c + 1)])
            qa.dma(
                   a_pub[c, D:].rearrange("(p k) -> p k", p=128),
                   fstage[:, 128 * c:128 * (c + 1)])
        gwait(qa)
        cc1 = bd.collective("cc_a_g", "AllGather", ALU.bypass, G8,
                            [a_pub.ap().opt()], [a_gath.ap().opt()])
        gwait(cc1)
        bigc.push()
        dts_roll = [bigc.take(128, NF, F32) for _ in range(2)]
        F_roll = [bigc.take(128, NF * NSA, F32) for _ in range(2)]
        cprev = bigc.take(128, NF * NSA, F32)
        ccur = bigc.take(128, NF * NSA, F32)
        Ptmp = bigc.take(128, NSA, F32)

        def s16_of(b2, c):
            r, s = _rank_of(b2, c)
            return 2 * r + s

        for b2 in range(2):
            for c in range(NCHK):
                cseg = ccur if c % 2 else cprev
                psg = cprev if c % 2 else ccur
                vwait(qa)
                if c == 0:
                    bd.op("vector", "memset", cseg, 0.0, sem=ev)
                else:
                    ps16 = s16_of(b2, c - 1)
                    gwait(ev)
                    qg.dma(dts_roll[c % 2],
                           a_gath[ps16, 0:D].rearrange("(p f) -> p f", p=128),
                           )
                    qg.dma(F_roll[c % 2],
                           a_gath[ps16, D:].rearrange("(p k) -> p k", p=128),
                           )
                    vwait(qg)
                    awaitx(qg, ev)
                    for f in range(NF):
                        awaitx(ev)
                        bd.op("scalar", "activation", Ptmp,
                              a_alog_s[:, NSA * f:NSA * (f + 1)], AF.Exp,
                              scale=dts_roll[c % 2][:, f:f + 1], sem=ea)
                        vwait(ea)
                        bd.op("vector", "tensor_mul",
                              cseg[:, NSA * f:NSA * (f + 1)],
                              psg[:, NSA * f:NSA * (f + 1)], Ptmp)
                        bd.op("vector", "tensor_add",
                              cseg[:, NSA * f:NSA * (f + 1)],
                              cseg[:, NSA * f:NSA * (f + 1)],
                              F_roll[c % 2][:, NSA * f:NSA * (f + 1)], sem=ev)
                if c <= 3:
                    j, ci = 4 * b2 + c, 0
                else:
                    j, ci = 4 * b2 + 7 - c, 1
                awaitx(ev)
                qa.dma(
                       a_sel_send[j, ci, :].rearrange("(p k) -> p k", p=128),
                       cseg)
                gwait(qa)
        gwait(qa)
        cc2 = bd.collective("cc_a_s", "AllToAll", ALU.bypass, G8,
                            [a_sel_send.ap().opt()], [a_sel_recv.ap().opt()])
        gwait(cc2)
        myc = [bigc.take(128, NF * NSA, F32) for _ in range(2)]
        for ci in range(2):
            qg.dma(myc[ci],
                   a_sel_recv[0, ci, :].rearrange("(p k) -> p k", p=128))
        DT = scr.take(128, T, F32)
        cpA = scr.take(128, T, F32)
        fxt = scr.take(128, T, F32)
        at2 = [cpA, scr.t.ap().bitcast(F32)[0:128, sp1.offset:sp1.offset + T]]
        vwait(qg)
        awaitx(qg)
        for f in range(NF):
            for tt in range(2):
                pwait(ev, ea)
                for k in range(2):
                    bd.op("tensor", "matmul", pbank(0),
                          aw_dt[k][f], dtr[k][:, tt * CH:(tt + 1) * CH],
                          start=(k == 0), stop=(k == 1),
                          sem=(ep if k == 1 else None))
                awaitx(ep, ev)
                bd.op("scalar", "activation", sp1, pbank(0), AF.Abs,
                      bias=a_bdt_s[:, f:f + 1])
                bd.op("scalar", "activation", sp1, sp1, AF.Exp, scale=-1.0)
                bd.op("scalar", "activation", sp1, sp1, AF.Ln, bias=1.0)
                bd.op("scalar", "activation", sp2, pbank(0), AF.Relu,
                      bias=a_bdt_s[:, f:f + 1], sem=ea)
                vwait(ea)
                bd.op("vector", "scalar_tensor_tensor",
                      delta[:, tt * CH:(tt + 1) * CH], sp1, 1.0, sp2,
                      ALU.mult, ALU.add, sem=ev)
                bd.op("vector", "tensor_tensor_scan",
                      DT[:, tt * CH:(tt + 1) * CH], ones_f,
                      delta[:, tt * CH:(tt + 1) * CH], 0.0,
                      ALU.mult, ALU.add, sem=ev)
            mka, mkC = {}, {}
            _pv = bd.sems["ev"].n
            _pg = bd.sems["eg"].n
            mkstt = {-2: _pv, -1: _pv}
            mkg1f = {-2: _pg, -1: _pg}
            pwait(ea)

            def prep1f(n, f=f):
                CP = (1, 2) if n % 2 == 0 else (3, 4)
                bd.wait("scalar", ev, mkstt[n - 2])
                bd.op("scalar", "activation", at2[n % 2], DT, AF.Exp,
                      scale=a_alog_s[:, NSA * f + n: NSA * f + n + 1], sem=ea)
                mka[n] = ea.n
                bd.wait("tensor", ev, mkstt[n - 2])
                for tt in range(2):
                    bd.op("tensor", "matmul", pbank(CP[tt]),
                          selone_s[8 + n][0:16, :], bca[0:16, tt * CH:(tt + 1) * CH],
                          start=True, stop=True, sem=ep)
                mkC[n] = ep.n

            prep1f(0)
            for n in range(NSA):
                if n + 1 < NSA:
                    prep1f(n + 1)
                CP = (1, 2) if n % 2 == 0 else (3, 4)
                bd.wait("vector", ea, mka[n])
                bd.wait("vector", ep, mkC[n])
                bd.wait("vector", eg, mkg1f[n - 1])
                for tt in range(2):
                    bd.op("vector", "scalar_tensor_tensor",
                          fxt[:, tt * CH:(tt + 1) * CH],
                          at2[n % 2][:, tt * CH:(tt + 1) * CH],
                          myc[tt][:, NSA * f + n: NSA * f + n + 1],
                          pbank(CP[tt]), ALU.mult, ALU.mult, sem=ev)
                mkstt[n] = bd.sems["ev"].n
                bd.wait("gpsimd", ev, mkstt[n])
                bd.op("gpsimd", "tensor_add", xt(f), xt(f), fxt)
                mkg1f[n] = bd.sems["eg"].n
        bigc.pop()
        bigc.pop()
        scr.pop()
        barrier()


    def phase_ffn(tag):
        scr.push()
        trimask_s = [scr.take(128, CH, BF16) for _ in range(4)]
        for s in range(4):
            qg.dma(trimask_s[s], trimask_d[s])
        qg.flip()
        vwait(qg)
        s1row = scr.take(1, T, F32)
        brow = scr.take(1, CH, F32)
        lrow = scr.take(1, CH, F32)
        ln_stats(tag, with_rms=True, s1seg=s1row)
        pwait(ev, ea)
        bcast_row(2, row(0, 0))
        bcast_row(3, s1row[:, 0:CH])
        bcast_row(4, row(0, 1))
        bcast_row(5, s1row[:, CH:2 * CH])
        MU, S1 = [2, 4], [3, 5]
        xc = scr.take(128, CH, F32)
        vwait(ep)
        for f in range(NF):
            for tt in range(2):
                sl = slice(tt * CH, (tt + 1) * CH)
                bd.op("vector", "tensor_sub", xc, xt(f)[:, sl], pbank(MU[tt]))
                bd.op("vector", "scalar_tensor_tensor", b1(f)[:, sl], xc,
                      f_ln1_s[:, f:f + 1], pbank(S1[tt]), ALU.mult, ALU.mult,
                      sem=ev)
        wbuf = [scr.take(128, NF * 128, BF16) for _ in range(2)]
        wbuf2 = [scr.take(128, NF * 128, BF16) for _ in range(2)]
        qkvt = [scr.take(128, CH, BF16) for _ in range(4)]
        WT = [f_WqT, f_WkT, f_WvT]
        gwait(ep, ev, ea)
        nblk = 0
        for qkv in range(3):
            for mb in range(NH):
                wb = wbuf[nblk % 2]
                if nblk >= 2:
                    gwait(ep)
                qg.dma(wb.rearrange("p (k m) -> p k m", k=NF),
                       WT[qkv][:, 128 * mb:128 * (mb + 1)]
                       .rearrange("(k p) m -> p k m", p=128))
                qg.flip()
                pwait(qg)
                if nblk == 0:
                    pwait(ev)
                for tt in range(2):
                    bank = tt
                    pwait(ea)
                    for k in range(NF):
                        bd.op("tensor", "matmul", pbank(bank),
                              wb[:, 128 * k:128 * (k + 1)],
                              b1(k)[:, tt * CH:(tt + 1) * CH],
                              start=(k == 0), stop=(k == NF - 1),
                              sem=(ep if k == NF - 1 else None))
                    awaitx(ep)
                    if nblk >= 2:
                        qa.wait("scalar", qa.epoch - 2)
                    qt = qkvt[2 * (nblk % 2) + tt]
                    sc = QSCALE if qkv == 0 else 1.0
                    bd.op("scalar", "activation", qt, pbank(bank), AF.Copy,
                          scale=sc, sem=ea)
                    awaitx(ea)
                    s6 = 2 * qkv + (mb % 2)
                    qa.dma(kvq_send.ap()[mb // 2, s6, :, tt * CH:(tt + 1) * CH],
                           qt)
                qa.flip()
                nblk += 1
        gwait(qa)
        gwait(ev, ea)
        cc_kvq = bd.collective("cc_kvq_" + tag, "AllToAll", ALU.bypass, G8,
                               [kvq_send.ap().opt()], [kvq_recv.ap().opt()])
        gwait(cc_kvq)
        bd.wait("sync", cc_kvq, 1)
        bigc.push()
        qtile = [bigc.take(128, CH, BF16) for _ in range(2)]
        ktile = [bigc.take(128, CH, BF16) for _ in range(3)]
        vtile = [bigc.take(128, CH, BF16) for _ in range(3)]
        ptile = [bigc.take(128, CH, BF16) for _ in range(2)]
        tmpbf = bigc.take(128, CH, BF16)
        linv_sb = bigc.take(128, CH, F32)
        o_sb = [bigc.take(128, CH, BF16) for _ in range(2)]
        SB = [0, 1]
        nq = 0

        def load_kv(buf_i, b2, kc, hh):
            rk2, sl2 = _rank_of(b2, kc)
            qs.dma(ktile[buf_i],
                   kvq_recv.ap()[rk2, 2 + hh, :, sl2 * CH:(sl2 + 1) * CH])
            for s4 in range(4):
                qs.dma(vtile[buf_i][:, 128 * s4:128 * (s4 + 1)],
                       kvq_recv.ap()[rk2, 4 + hh, :,
                                     sl2 * CH + 128 * s4: sl2 * CH + 128 * (s4 + 1)],
                       transpose=True)
            return qs.flip()

        for b2 in range(2):
            for hh in range(2):
                for qrow in range(NCHK):
                    rk, sl_ = _rank_of(b2, qrow)
                    qb = qtile[nq % 2]
                    bd.wait("sync", ev, ev.n)
                    bd.wait("sync", ep, ep.n)
                    qs.dma(qb, kvq_recv.ap()[rk, hh, :, sl_ * CH:(sl_ + 1) * CH])
                    tok = load_kv(qrow % 3, b2, qrow, hh)
                    vwait(qs)
                    bd.op("vector", "tensor_mul", tmpbf, qb,
                          ktile[qrow % 3], sem=ev)
                    pwait(ev, ea)
                    bd.op("tensor", "matmul", pbank(4, 1, CH), ones_bf[:, 0:1],
                          tmpbf, start=True, stop=True, sem=ep)
                    vwait(ep)
                    bd.op("vector", "tensor_scalar", brow, pbank(4, 1, CH),
                          -1.0, -BIAS_OFF, ALU.mult, ALU.add, sem=ev)
                    nsubs = 4 * (qrow + 1)
                    act_mark = {}
                    pv_mark = {-2: 0, -1: 0}
                    toks = None
                    subq = []
                    for kc_ in range(qrow, -1, -1):
                        for s4_ in range(4):
                            subq.append((kc_, s4_))

                    def emit_qk(i):
                        kc, s4 = subq[i]
                        sbank = SB[i % 2]
                        if i >= 2:
                            bd.wait("tensor", ea, act_mark[i - 2])
                        bd.op("tensor", "matmul", pbank(sbank),
                              ones_f1[:, :], brow,
                              start=True, stop=False)
                        bd.op("tensor", "matmul", pbank(sbank),
                              ktile[kc % 3][:, 128 * s4:128 * (s4 + 1)], qb,
                              start=False, stop=True, sem=ep)
                        return ep.n

                    def emit_exp(i, qkm):
                        kc, s4 = subq[i]
                        sbank = SB[i % 2]
                        if kc == qrow:
                            bd.wait("vector", ep, qkm)
                            bd.op("vector", "tensor_add", pbank(sbank),
                                  pbank(sbank), trimask_s[s4], sem=ev)
                            awaitx(ev)
                        else:
                            bd.wait("scalar", ep, qkm)
                        bd.wait("scalar", ep, pv_mark[i - 2])
                        bd.op("scalar", "activation", ptile[i % 2], pbank(sbank),
                              AF.Exp, sem=ea)
                        act_mark[i] = ea.n

                    def emit_lpv(i):
                        kc, s4 = subq[i]
                        first = (i == 0)
                        last = (i == nsubs - 1)
                        bd.wait("tensor", ea, act_mark[i])
                        bd.op("tensor", "matmul", pbank(3, 1, CH),
                              ones_bf[:, 0:1], ptile[i % 2],
                              start=first, stop=last, sem=None)
                        bd.op("tensor", "matmul", pbank(2),
                              vtile[kc % 3][:, 128 * s4:128 * (s4 + 1)],
                              ptile[i % 2],
                              start=first, stop=last, sem=ep)
                        pv_mark[i] = ep.n

                    pwait(ev)
                    prev_kc = None
                    for i in range(nsubs):
                        kc, s4 = subq[i]
                        if i >= 1:
                            emit_lpv(i - 1)
                        if kc != prev_kc:
                            if kc != qrow:
                                qs.wait("tensor", toks)
                                qs.wait("scalar", toks)
                            if kc > 0:
                                bd.wait("sync", ep, ep.n)
                                toks = load_kv((kc - 1) % 3, b2, kc - 1, hh)
                            prev_kc = kc
                        qkm = emit_qk(i)
                        emit_exp(i, qkm)
                    emit_lpv(nsubs - 1)
                    vwait(ep)
                    bd.op("vector", "reciprocal", lrow, pbank(3, 1, CH),
                          sem=ev)
                    pwait(ev)
                    bd.op("tensor", "matmul", pbank(4), ones_f1[:, :],
                          lrow, start=True, stop=True, sem=ep)
                    awaitx(ep)
                    bd.op("scalar", "activation", linv_sb, pbank(4), AF.Copy,
                          sem=ea)
                    vwait(ea)
                    if nq >= 2:
                        qa.wait("vector", qa.epoch - 2)
                    osb = o_sb[nq % 2]
                    bd.op("vector", "tensor_mul", osb, pbank(2), linv_sb,
                          sem=ev)
                    awaitx(ev)
                    qa.dma(o_send.ap()[rk, hh, :, sl_ * CH:(sl_ + 1) * CH], osb)
                    qa.flip()
                    nq += 1
        gwait(qa)
        gwait(ev, ea)
        cc_o = bd.collective("cc_o_" + tag, "AllToAll", ALU.bypass, G8,
                             [o_send.ap().opt()], [o_recv.ap().opt()])
        gwait(cc_o)
        bd.wait("sync", cc_o, 1)
        o2 = [bigc.take(128, CH, BF16) for _ in range(2)]
        tanh_sb = bigc.take(16, T, F32)
        modh_bf = scr.take(16, T, BF16)
        aA_mark = {}
        nh2 = 0
        for tt in range(2):
            for h in range(NH):
                ob = o2[nh2 % 2]
                if nh2 >= 2:
                    bd.wait("sync", ep, aA_mark[nh2 - 2])
                qs.dma(ob, o_recv.ap()[h // 2, h % 2, :, tt * CH:(tt + 1) * CH])
                qs.flip()
                pwait(qs)
                if nh2 == 0:
                    pwait(ea, ev)
                bd.op("tensor", "matmul", pbank(5 + tt, 16, CH), wc_s[h], ob,
                      start=(h == 0), stop=(h == NH - 1), sem=ep)
                aA_mark[nh2] = ep.n
                nh2 += 1
        awaitx(ep)
        for tt in range(2):
            bd.op("scalar", "activation", tanh_sb[:, tt * CH:(tt + 1) * CH],
                  pbank(5 + tt, 16, CH), AF.Tanh, sem=ea)
        pwait(ev, ea)
        bcast_row(2, row(0, 0))
        bcast_row(3, s1row[:, 0:CH])
        bcast_row(4, row(0, 1))
        bcast_row(5, s1row[:, CH:2 * CH])
        vwait(ep)
        for f in range(NF):
            for tt in range(2):
                sl = slice(tt * CH, (tt + 1) * CH)
                bd.op("vector", "tensor_sub", xc, xt(f)[:, sl], pbank(MU[tt]))
                bd.op("vector", "scalar_tensor_tensor", b1(f)[:, sl], xc,
                      f_ln2_s[:, f:f + 1], pbank(S1[tt]), ALU.mult, ALU.mult,
                      sem=ev)
        for tt in range(2):
            pwait(ev)
            pwait(ea)
            for k in range(NF):
                bd.op("tensor", "matmul", pbank(5 + tt, 16, CH), f_A_s[k],
                      b1(k)[:, tt * CH:(tt + 1) * CH],
                      start=(k == 0), stop=(k == NF - 1),
                      sem=(ep if k == NF - 1 else None))
            vwait(ep)
            bd.op("vector", "tensor_mul", modh_bf[:, tt * CH:(tt + 1) * CH],
                  tanh_sb[:, tt * CH:(tt + 1) * CH], pbank(5 + tt, 16, CH),
                  sem=ev)
        barrier()
        bigc.pop()
        bigc.push()
        gu = bigc.take(128, 16 * 2 * CH, BF16)

        def gusl(fb, tt):
            o = (fb * 2 + tt) * CH
            return gu[:, o:o + CH]

        wbufd = scr.take(128, 16 * 128, BF16)
        b01t = [scr.take(16, 128, BF16) for _ in range(2)]
        gate_bf = scr.take(128, CH, BF16)
        nfbq = 0

        def mlp_load(i):
            gwait(ep)
            qg.dma(wbuf[i % 2].rearrange("p (k m) -> p k m", k=NF),
                   f_WinT[:, 128 * i:128 * (i + 1)]
                   .rearrange("(k p) m -> p k m", p=128))
            qg.dma(b01t[i % 2], b01_d.ap()[:, 128 * i:128 * (i + 1)])
            qg.dma(wbuf2[i % 2].rearrange("p (k m) -> p k m", k=NF),
                   f_WupT[:, 128 * i:128 * (i + 1)]
                   .rearrange("(k p) m -> p k m", p=128))
            return qg.flip()

        pend2 = mlp_load(0)
        for quarter in range(4):
            for fbq in range(16):
                fb = 16 * quarter + fbq
                wbi = wbuf[nfbq % 2]
                wbu2 = wbuf2[nfbq % 2]
                tok = pend2
                if fb + 1 < NFF:
                    pend2 = mlp_load(fb + 1)
                qg.wait("tensor", tok)
                for tt in range(2):
                    gb = (2 * nfbq + tt) % 2
                    ub = 2 + (2 * nfbq + tt) % 2
                    pwait(ea, ev)
                    for k in range(NF):
                        bd.op("tensor", "matmul", pbank(gb),
                              wbi[:, 128 * k:128 * (k + 1)],
                              b1(k)[:, tt * CH:(tt + 1) * CH],
                              start=(k == 0), stop=False)
                    bd.op("tensor", "matmul", pbank(gb),
                          b01t[nfbq % 2][0:16, :],
                          modh_bf[0:16, tt * CH:(tt + 1) * CH],
                          start=False, stop=True, sem=ep)
                    for k in range(NF):
                        bd.op("tensor", "matmul", pbank(ub),
                              wbu2[:, 128 * k:128 * (k + 1)],
                              b1(k)[:, tt * CH:(tt + 1) * CH],
                              start=(k == 0), stop=(k == NF - 1),
                              sem=(ep if k == NF - 1 else None))
                    awaitx(ep)
                    bd.op("scalar", "activation", gate_bf, pbank(gb), AF.Silu,
                          sem=ea)
                    vwait(ea)
                    bd.op("vector", "tensor_mul", gusl(fbq, tt), gate_bf,
                          pbank(ub), sem=ev)
                nfbq += 1
            for m in range(NF):
                gwait(ep)
                qg.dma(wbufd.rearrange("p (k mm) -> p k mm", k=16),
                       f_WdownT[2048 * quarter:2048 * (quarter + 1),
                                128 * m:128 * (m + 1)]
                       .rearrange("(k p) mm -> p k mm", p=128))
                qg.flip()
                pwait(qg)
                for tt in range(2):
                    dbank = 4 + tt
                    sl = slice(tt * CH, (tt + 1) * CH)
                    pwait(ev)
                    for k in range(16):
                        bd.op("tensor", "matmul", pbank(dbank),
                              wbufd[:, 128 * k:128 * (k + 1)], gusl(k, tt),
                              start=(k == 0), stop=(k == 15),
                              sem=(ep if k == 15 else None))
                    if quarter == 0:
                        bcast_row(6, row(0, tt))
                        bcast_row(7, row(1, tt))
                    vwait(ep)
                    if quarter == 0:
                        bd.op("vector", "tensor_sub", xc, xt(m)[:, sl],
                              pbank(6))
                        bd.op("vector", "tensor_mul", xc, xc, pbank(7))
                        bd.op("vector", "tensor_add", xc, xc, pbank(dbank))
                        bd.op("vector", "tensor_add", xt(m)[:, sl],
                              xt(m)[:, sl], xc, sem=ev)
                    else:
                        bd.op("vector", "tensor_add", xt(m)[:, sl],
                              xt(m)[:, sl], pbank(dbank), sem=ev)
        bigc.pop()
        scr.pop()
        barrier()


    def phase_b():
        scr.push()
        ln_stats("pb", with_rms=False)
        pwait(ev, ea)
        bcast_row(2, row(0, 0))
        bcast_row(3, row(1, 0))
        bcast_row(4, row(0, 1))
        bcast_row(5, row(1, 1))
        MU, RS = [2, 4], [3, 5]
        xc = scr.take(128, CH, F32)
        vwait(ep)
        for f in range(NF):
            for tt in range(2):
                sl = slice(tt * CH, (tt + 1) * CH)
                bd.op("vector", "tensor_sub", xc, xt(f)[:, sl], pbank(MU[tt]))
                bd.op("vector", "tensor_mul", b1(f)[:, sl], xc, pbank(RS[tt]),
                      sem=ev)
        wbuf = [scr.take(128, NF * 128, BF16) for _ in range(2)]
        evt = [scr.take(128, CH, BF16) for _ in range(2)]
        dtin_sb = scr.take(128, T, BF16)
        gwait(ep, ev, ea)
        qg.dma(wbuf[0].rearrange("p (k m) -> p k m", k=NF),
               b_dtinT.rearrange("(k p) m -> p k m", p=128))
        qg.flip()
        pwait(qg)
        pwait(ev)
        for tt in range(2):
            pwait(ea)
            for k in range(NF):
                bd.op("tensor", "matmul", pbank(0),
                      wbuf[0][:, 128 * k:128 * (k + 1)],
                      b1(k)[:, tt * CH:(tt + 1) * CH],
                      start=(k == 0), stop=(k == NF - 1),
                      sem=(ep if k == NF - 1 else None))
            awaitx(ep)
            bd.op("scalar", "activation", dtin_sb[:, tt * CH:(tt + 1) * CH],
                  pbank(0), AF.Copy, sem=ea)
        nblk = 0
        for mb in range(16):
            wb = wbuf[nblk % 2]
            gwait(ep)
            qg.dma(wb.rearrange("p (k m) -> p k m", k=NF),
                   b_inprojT[:, 128 * (32 + mb):128 * (33 + mb)]
                   .rearrange("(k p) m -> p k m", p=128))
            qg.flip()
            pwait(qg)
            for tt in range(2):
                bank = tt
                pwait(ea)
                for k in range(NF):
                    bd.op("tensor", "matmul", pbank(bank),
                          wb[:, 128 * k:128 * (k + 1)],
                          b1(k)[:, tt * CH:(tt + 1) * CH],
                          start=(k == 0), stop=(k == NF - 1),
                          sem=(ep if k == NF - 1 else None))
                awaitx(ep)
                if nblk >= 1:
                    qa.wait("scalar", qa.epoch - 1)
                et = evt[tt]
                bd.op("scalar", "activation", et, pbank(bank), AF.Copy, sem=ea)
                awaitx(ea)
                qa.dma(xrb_d.ap()[128 * mb:128 * (mb + 1),
                                  tt * CH:(tt + 1) * CH], et)
            qa.flip()
            nblk += 1
        zz_sb = scr.take(128, 32, BF16)
        bd.op("vector", "memset", zz_sb, 0.0, sem=ev)
        zz_d = nc.dram_tensor("zz_" + "b", [1024, 4], BF16)
        vwait(ev)
        gwait(qa)
        gwait(ev)
        qg.dma(zz_d.ap().rearrange("(a p) c -> p a c", p=128), 
               zz_sb.rearrange("p (a c) -> p a c", c=4))
        for c in range(2):
            qg.dma(bnd_pub.ap()[c], xrb_d.ap()[0:1024, c * CH + 508:c * CH + 512])
        qg.flip()
        gwait(qg)
        ccb1 = bd.collective("cc_bnd_g", "AllGather", ALU.bypass, G8,
                             [bnd_pub.ap().opt()], [bnd_gath.ap().opt()])
        gwait(ccb1)
        for j in range(8):
            b2 = j // 4
            for ci, c in enumerate((j % 4, 7 - j % 4)):
                if c == 0:
                    qg.dma(bnd_send.ap()[j, ci], zz_d.ap())
                else:
                    rk, sl2 = _rank_of(b2, c - 1)
                    qg.dma(bnd_send.ap()[j, ci], bnd_gath.ap()[2 * rk + sl2])
        qg.flip()
        gwait(qg)
        ccb2 = bd.collective("cc_bnd_s", "AllToAll", ALU.bypass, G8,
                             [bnd_send.ap().opt()], [bnd_recv.ap().opt()])
        gwait(ccb2)
        bnd_sb = [scr.take(32, 32 * 4, BF16) for _ in range(2)]
        for ci in range(2):
            qg.dma(bnd_sb[ci].rearrange("p (w c) -> p w c", c=4),
                   bnd_recv.ap()[0, ci].rearrange("(w p) c -> p w c", p=32))
        qg.flip()
        bdtp_sb = scr.take(128, 16 * 128, BF16)
        dt_fb = scr.take(128, T, F32)
        xact_bf = scr.take(128, T, BF16)
        u_bf = scr.take(128, T, BF16)
        _xoff = xact_bf.offset
        fixsum = scr.t.ap().bitcast(F32)[0:128, _xoff // 2:_xoff // 2 + T]
        at_ = scr.take(128, T, F32)
        bt_ = scr.take(128, T, F32)
        Ht = scr.take(128, T, F32)
        C_sb = scr.take(128, T, BF16)
        z_bf = scr.take(128, T, BF16)
        xrw = [scr.take(32, T, BF16)]
        Bw = [scr.take(32, T, BF16)]
        _wo = xrw[0].offset
        bt2_ = scr.t.ap().bitcast(F32)[0:128, _wo // 2:_wo // 2 + T]
        dstage_b = scr.take(128, 2 * 16, F32)
        fstage_b = scr.take(128, 2 * 256, F32)
        F_roll = [scr.take(128, 256, F32) for _ in range(2)]
        myc = F_roll
        cprev = bt_[:, 0:256]
        ccur = bt_[:, 256:512]
        Ptmp = bt_[:, 512:528]
        dts_roll = [bt_[:, 528 + 16 * i:544 + 16 * i] for i in range(2)]

        def zc_mms(fb, dst_bf, rowbase, silu):
            nonlocal_wb = wbuf[fb % 2]
            gwait(ep)
            qg.dma(nonlocal_wb.rearrange("p (k m) -> p k m", k=NF),
                   b_inprojT[:, rowbase + 128 * fb: rowbase + 128 * (fb + 1)]
                   .rearrange("(k p) m -> p k m", p=128))
            qg.flip()
            pwait(qg)
            pwait(ev)
            for tt in range(2):
                pwait(ea)
                for k in range(NF):
                    bd.op("tensor", "matmul", pbank(3),
                          nonlocal_wb[:, 128 * k:128 * (k + 1)],
                          b1(k)[:, tt * CH:(tt + 1) * CH],
                          start=(k == 0), stop=(k == NF - 1),
                          sem=(ep if k == NF - 1 else None))
                awaitx(ep)
                bd.op("scalar", "activation", dst_bf[:, tt * CH:(tt + 1) * CH],
                      pbank(3), AF.Silu if silu else AF.Copy, sem=ea)

        bdtp_tok = [None]

        def dt_softplus(fb, want_dtsum):
            fbq = fb % 16
            qs_ = qg.wait("tensor", bdtp_tok[0])
            for tt in range(2):
                pwait(ea, ev)
                bd.op("tensor", "matmul", pbank(0),
                      bdtp_sb[:, 128 * (fb % 16):128 * (fb % 16 + 1)],
                      dtin_sb[:, tt * CH:(tt + 1) * CH],
                      start=True, stop=True, sem=ep)
                awaitx(ep, ev)
                bd.op("scalar", "activation", sp1b, pbank(0), AF.Abs,
                      bias=b_dtb_s[:, fb:fb + 1])
                bd.op("scalar", "activation", sp1b, sp1b, AF.Exp, scale=-1.0)
                bd.op("scalar", "activation", sp1b, sp1b, AF.Ln, bias=1.0)
                bd.op("scalar", "activation", sp2b, pbank(0), AF.Relu,
                      bias=b_dtb_s[:, fb:fb + 1], sem=ea)
                vwait(ea)
                kw = {}
                if want_dtsum:
                    kw["accum_out"] = dstage_b[:, 16 * tt + fbq: 16 * tt + fbq + 1]
                bd.op("vector", "scalar_tensor_tensor",
                      dt_fb[:, tt * CH:(tt + 1) * CH], sp1b, 1.0, sp2b,
                      ALU.mult, ALU.add, sem=ev, **kw)

        _reg = scr.take(128, 1040, F32)
        sp1b = _reg[:, 0:512]
        sp2b = _reg[:, 512:1024]
        xpad = [_reg[:, 0:520]]
        cacc = _reg[:, 520:1032]
        at2 = [at_, _reg[:, 0:T]]
        def ysl_ap(k, tt):
            o = (2 * k + tt) * CH
            return big2.ap()[:, o:o + CH]

        nfb3 = 0
        qa.wait("sync")
        for half in range(2):
            gwait(ep)
            qg.dma(bdtp_sb, b_dtprojT[:, 2048 * half:2048 * (half + 1)])
            bdtp_tok[0] = qg.flip()
            vwait(qa)
            for fbq in range(16):
                fb = 16 * half + fbq
                xw = xrw[0]
                bw = Bw[0]
                bd.wait("sync", ep, ep.n)
                bd.wait("sync", ev, ev.n)
                qs.dma(xw, xrb_d.ap()[32 * fb:32 * (fb + 1), :])
                qs.dma(bw, xrb_d.ap()[1024 + 32 * fb:1024 + 32 * (fb + 1), :])
                qs.flip()
                dt_softplus(fb, True)
                pwait(qs, qg)
                for c in range(2):
                    pwait(ea)
                    bd.op("tensor", "matmul", pbank(1),
                          selx_s[0:32, :], xw[0:32, c * CH:(c + 1) * CH],
                          start=True, stop=True)
                    bd.op("tensor", "matmul", pbank(2, 128, 4),
                          selx_s[0:32, :], bnd_sb[c][0:32, 4 * fb:4 * (fb + 1)],
                          start=True, stop=True, sem=ep)
                    awaitx(ep, ev)
                    awaitx(ev)
                    bd.op("scalar", "activation", xpad[0][:, 4:516], pbank(1),
                          AF.Copy)
                    bd.op("scalar", "activation", xpad[0][:, 0:4],
                          pbank(2, 128, 4), AF.Copy, sem=ea)
                    vwait(ea)
                    bd.op("vector", "tensor_scalar",
                          cacc, xpad[0][:, 1:513],
                          b_convw_s[:, 4 * fb:4 * fb + 1], None, ALU.mult)
                    for kk in range(1, 4):
                        bd.op("vector", "scalar_tensor_tensor", cacc,
                              xpad[0][:, 1 + kk:513 + kk],
                              b_convw_s[:, 4 * fb + kk:4 * fb + kk + 1],
                              cacc, ALU.mult, ALU.add)
                    bd.op("vector", "engine_nop", sem=ev)
                    awaitx(ev)
                    bd.op("scalar", "activation",
                          xact_bf[:, c * CH:(c + 1) * CH], cacc, AF.Silu,
                          bias=b_convb_s[:, fb:fb + 1], sem=ea)
                vwait(ea, ev)
                bd.op("vector", "tensor_mul", u_bf, dt_fb, xact_bf, sem=ev)
                awaitx(qa)
                zc_mms(fb, C_sb, 128 * 48, False)
                awaitx(ea)
                qa.dma(C_d.ap()[fb], C_sb)
                qa.flip()
                mka, mkB, mkC = {}, {}, {}
                _pv = bd.sems["ev"].n
                mkscan = {-2: _pv, -1: _pv}
                mkb = {-2: _pv, -1: _pv}
                mkym = {-2: _pv, -1: _pv}
                pwait(ea)

                def prep_nB(n, fb=fb):
                    BP = (1, 2) if n % 2 == 0 else (3, 4)
                    bd.wait("scalar", ev, mkscan[n - 2])
                    bd.op("scalar", "activation", at2[n % 2], dt_fb, AF.Exp,
                          scale=b_alog_s[:, NST * fb + n: NST * fb + n + 1],
                          sem=ea)
                    mka[n] = ea.n
                    bd.wait("tensor", ev, mkb[n - 2])
                    for tt in range(2):
                        bd.op("tensor", "matmul", pbank(BP[tt]),
                              selb_s[n][0:32, :],
                              bw[0:32, tt * CH:(tt + 1) * CH],
                              start=True, stop=True, sem=ep)
                    mkB[n] = ep.n

                def prep_nC(n, fb=fb):
                    bd.wait("tensor", ev, mkym[n - 1])
                    for tt in range(2):
                        bd.op("tensor", "matmul", pbank(5 if tt == 0 else 0),
                              selc_s[n], C_sb[:, tt * CH:(tt + 1) * CH],
                              start=True, stop=True, sem=ep)
                    mkC[n] = ep.n

                prep_nB(0)
                prep_nC(0)
                for n in range(NST):
                    if n + 1 < NST:
                        prep_nB(n + 1)
                    BP = (1, 2) if n % 2 == 0 else (3, 4)
                    bd.wait("vector", ea, mka[n])
                    bd.wait("vector", ep, mkB[n])
                    for tt in range(2):
                        bd.op("vector", "tensor_mul", bt_[:, tt * CH:(tt + 1) * CH],
                              u_bf[:, tt * CH:(tt + 1) * CH], pbank(BP[tt]))
                    mkb[n] = bd.sems["ev"].n
                    for tt in range(2):
                        bd.op("vector", "tensor_tensor_scan",
                              Ht[:, tt * CH:(tt + 1) * CH],
                              at2[n % 2][:, tt * CH:(tt + 1) * CH],
                              bt_[:, tt * CH:(tt + 1) * CH],
                              0.0, ALU.mult, ALU.add)
                        bd.op("vector", "tensor_copy",
                              fstage_b[:, 256 * tt + 16 * fbq + n:
                                       256 * tt + 16 * fbq + n + 1],
                              Ht[:, (tt + 1) * CH - 1:(tt + 1) * CH])
                    mkscan[n] = bd.sems["ev"].n
                    bd.wait("vector", ep, mkC[n])
                    for tt in range(2):
                        bd.op("vector", "tensor_mul", Ht[:, tt * CH:(tt + 1) * CH],
                              Ht[:, tt * CH:(tt + 1) * CH],
                              pbank(5 if tt == 0 else 0))
                    if n == 0:
                        for tt in range(2):
                            bd.op("vector", "tensor_copy", pbank(6 + tt),
                                  Ht[:, tt * CH:(tt + 1) * CH], sem=ev)
                    else:
                        for tt in range(2):
                            bd.op("vector", "tensor_add", pbank(6 + tt),
                                  pbank(6 + tt), Ht[:, tt * CH:(tt + 1) * CH],
                                  sem=ev)
                    mkym[n] = bd.sems["ev"].n
                    if n + 1 < NST:
                        prep_nC(n + 1)
                awaitx(qa)
                zc_mms(fb, z_bf, 0, True)
                awaitx(ea)
                qa.dma(sz_d.ap()[fb], z_bf)
                qa.flip()
                vwait(ea)
                for tt in range(2):
                    sl = slice(tt * CH, (tt + 1) * CH)
                    bd.op("vector", "scalar_tensor_tensor", Ht[:, sl],
                          xact_bf[:, sl], b_Dp_s[:, fb:fb + 1], pbank(6 + tt),
                          ALU.mult, ALU.add)
                    bd.op("vector", "tensor_mul", ysl_ap(fbq, tt),
                          Ht[:, sl], z_bf[:, sl], sem=ev)
                nfb3 += 1
            awaitx(ev)
            for c in range(2):
                qa.dma(b_pub[half].ap()[c, 0:2048]
                       .rearrange("(p k) -> p k", p=128),
                       dstage_b[:, 16 * c:16 * (c + 1)])
                qa.dma(b_pub[half].ap()[c, 2048:]
                       .rearrange("(p k) -> p k", p=128),
                       fstage_b[:, 256 * c:256 * (c + 1)])
            qa.flip()
            gwait(qa)
            ccg = bd.collective(f"cc_b_g{half}_", "AllGather", ALU.bypass, G8,
                                [b_pub[half].ap().opt()],
                                [b_gathb[half].ap().opt()])
            gwait(ccg)

            def s16_of(b2, c):
                r, s = _rank_of(b2, c)
                return 2 * r + s

            for b2 in range(2):
                for c in range(NCHK):
                    cseg = ccur if c % 2 else cprev
                    psg = cprev if c % 2 else ccur
                    vwait(qa)
                    if c == 0:
                        bd.op("vector", "memset", cseg, 0.0, sem=ev)
                    else:
                        ps16 = s16_of(b2, c - 1)
                        gwait(ev)
                        qg.dma(dts_roll[c % 2],
                               b_gathb[half].ap()[ps16, 0:2048]
                               .rearrange("(p k) -> p k", p=128))
                        qg.dma(F_roll[c % 2],
                               b_gathb[half].ap()[ps16, 2048:]
                               .rearrange("(p k) -> p k", p=128))
                        qg.flip()
                        vwait(qg)
                        awaitx(qg, ev)
                        for fbq in range(16):
                            fb = 16 * half + fbq
                            awaitx(ev)
                            bd.op("scalar", "activation", Ptmp,
                                  b_alog_s[:, NST * fb:NST * (fb + 1)], AF.Exp,
                                  scale=dts_roll[c % 2][:, fbq:fbq + 1], sem=ea)
                            vwait(ea)
                            bd.op("vector", "tensor_mul",
                                  cseg[:, 16 * fbq:16 * (fbq + 1)],
                                  psg[:, 16 * fbq:16 * (fbq + 1)], Ptmp)
                            bd.op("vector", "tensor_add",
                                  cseg[:, 16 * fbq:16 * (fbq + 1)],
                                  cseg[:, 16 * fbq:16 * (fbq + 1)],
                                  F_roll[c % 2][:, 16 * fbq:16 * (fbq + 1)],
                                  sem=ev)
                    awaitx(ev)
                    if c <= 3:
                        j, ci = 4 * b2 + c, 0
                    else:
                        j, ci = 4 * b2 + 7 - c, 1
                    qa.dma(b_sel_send[half].ap()[j, ci]
                           .rearrange("(p k) -> p k", p=128), cseg)
                    qa.flip()
            gwait(qa)
            ccs = bd.collective(f"cc_b_s{half}_", "AllToAll", ALU.bypass, G8,
                                [b_sel_send[half].ap().opt()],
                                [b_sel_recv[half].ap().opt()])
            gwait(ccs)
            gwait(ev)
            for ci in range(2):
                qg.dma(myc[ci],
                       b_sel_recv[half].ap()[0, ci]
                       .rearrange("(p k) -> p k", p=128))
            qg.flip()
            vwait(qg)
            awaitx(qg)
            swait(qa)
            DTb = dt_fb
            for fbq in range(16):
                fb = 16 * half + fbq
                dt_softplus(fb, False)
                for tt in range(2):
                    vwait(ev)
                    bd.op("vector", "tensor_tensor_scan",
                          Ht[:, tt * CH:(tt + 1) * CH], ones_f,
                          dt_fb[:, tt * CH:(tt + 1) * CH], 0.0,
                          ALU.mult, ALU.add, sem=ev)
                bd.wait("sync", ep, ep.n)
                bd.wait("sync", ev, ev.n)
                qs.dma(C_sb, C_d.ap()[fb])
                qs.dma(z_bf, sz_d.ap()[fb])
                qs.flip()
                pwait(qs)
                btb = [bt_, bt2_]
                mka, mkC = {}, {}
                _pv = bd.sems["ev"].n
                _pg = bd.sems["eg"].n
                mkstt = {-2: _pv, -1: _pv}
                mkg3f = {-2: _pg, -1: _pg}
                pwait(ea)

                def prep_fx(n, fb=fb):
                    CP = (1, 2) if n % 2 == 0 else (3, 4)
                    bd.wait("scalar", ev, mkstt[n - 2])
                    bd.op("scalar", "activation", at2[n % 2], Ht, AF.Exp,
                          scale=b_alog_s[:, NST * fb + n: NST * fb + n + 1],
                          sem=ea)
                    mka[n] = ea.n
                    bd.wait("tensor", ev, mkstt[n - 2])
                    for tt in range(2):
                        bd.op("tensor", "matmul", pbank(CP[tt]),
                              selc_s[n], C_sb[:, tt * CH:(tt + 1) * CH],
                              start=True, stop=True, sem=ep)
                    mkC[n] = ep.n

                prep_fx(0)
                for n in range(NST):
                    if n + 1 < NST:
                        prep_fx(n + 1)
                    CP = (1, 2) if n % 2 == 0 else (3, 4)
                    bd.wait("vector", ea, mka[n])
                    bd.wait("vector", ep, mkC[n])
                    bd.wait("vector", eg, mkg3f[n - 2])
                    for tt in range(2):
                        bd.op("vector", "scalar_tensor_tensor",
                              btb[n % 2][:, tt * CH:(tt + 1) * CH],
                              at2[n % 2][:, tt * CH:(tt + 1) * CH],
                              myc[tt][:, 16 * fbq + n: 16 * fbq + n + 1],
                              pbank(CP[tt]), ALU.mult, ALU.mult, sem=ev)
                    mkstt[n] = bd.sems["ev"].n
                    bd.wait("gpsimd", ev, mkstt[n])
                    if n == 0:
                        bd.op("gpsimd", "tensor_copy", fixsum, btb[0])
                    else:
                        bd.op("gpsimd", "tensor_add", fixsum, fixsum,
                              btb[n % 2])
                    mkg3f[n] = bd.sems["eg"].n
                vwait(qs)
                vwait(eg)
                for tt in range(2):
                    sl = slice(tt * CH, (tt + 1) * CH)
                    ysl = ysl_ap(fbq, tt)
                    bd.op("vector", "tensor_mul", bt_[:, sl], fixsum[:, sl],
                          z_bf[:, sl])
                    bd.op("vector", "tensor_add", ysl, ysl, bt_[:, sl], sem=ev)
            for m in range(NF):
                wb = wbuf[m % 2]
                gwait(ep)
                qg.dma(wb.rearrange("p (k mm) -> p k mm", k=NF),
                       b_outprojT[2048 * half:2048 * (half + 1),
                                  128 * m:128 * (m + 1)]
                       .rearrange("(k p) mm -> p k mm", p=128))
                qg.flip()
                pwait(qg)
                pwait(ev)
                for tt in range(2):
                    dbank = 4 + tt
                    sl = slice(tt * CH, (tt + 1) * CH)
                    pwait(ev)
                    for k in range(16):
                        bd.op("tensor", "matmul", pbank(dbank),
                              wb[:, 128 * k:128 * (k + 1)], ysl_ap(k, tt),
                              start=(k == 0), stop=(k == 15),
                              sem=(ep if k == 15 else None))
                    vwait(ep)
                    bd.op("vector", "tensor_add", xt(m)[:, sl], xt(m)[:, sl],
                          pbank(dbank), sem=ev)
        scr.pop()
        barrier()

    def phase_out():
        swait(ev, ea, ep, eg)
        for f in range(NF):
            qs.dma(y0[128 * f:128 * (f + 1), :], xt(f))
        qs.wait("sync")

    phases = [phase0, phase1, lambda: phase_ffn('f1'), phase_b,
              lambda: phase_ffn('f2')]
    for i, ph in enumerate(phases):
        if i < n_phases:
            ph()
    phase_out()

    block = es.enter_context(nc.Block())

    @block.sync
    def _(e):
        for fn in bd.ops["sync"]:
            fn(e)

    @block.tensor
    def _(e):
        for fn in bd.ops["tensor"]:
            fn(e)

    @block.vector
    def _(e):
        for fn in bd.ops["vector"]:
            fn(e)

    @block.scalar
    def _(e):
        for fn in bd.ops["scalar"]:
            fn(e)

    @block.gpsimd
    def _(e):
        for fn in bd.ops["gpsimd"]:
            fn(e)

    es.close()
    return nc



def _host_constants():
    selx = np.zeros((32, 128), np.float32)
    for ch in range(128):
        selx[(ch // 64) * 16 + (ch % 16), ch] = 1.0
    selb = np.zeros((NST, 32, 128), np.float32)
    for n in range(NST):
        for ch in range(128):
            selb[n, (ch // 64) * 16 + n, ch] = 1.0
    selc = np.zeros((NST, 128, 128), np.float32)
    for n in range(NST):
        for ch in range(128):
            selc[n, (ch // 16) * 16 + n, ch] = 1.0
    selone = np.zeros((16, 16, 128), np.float32)
    for n in range(16):
        selone[n, n, :] = 1.0
    trimask = np.zeros((4, 128, CH), np.float32)
    for s in range(4):
        for k in range(128):
            kg = 128 * s + k
            trimask[s, k, :kg] = BIG_NEG
    return selx, selb, selc, selone, trimask


def prepare_inputs(full):
    f32 = np.float32
    cT = np.ascontiguousarray
    x = np.asarray(full["L0_attn_a_x"], f32)
    selx, selb, selc, selone, trimask = _host_constants()
    shared = {
        "a_xprojT": cT(np.asarray(full["L0_attn_a_W_xproj"], f32).T),
        "a_dtT": cT(np.asarray(full["L0_attn_a_W_dt"], f32).T),
        "a_bdt": cT(np.asarray(full["L0_attn_a_b_dt"], f32).reshape(NF, 128, 1)),
        "a_alog": cT(np.asarray(full["L0_attn_a_A_log"], f32).reshape(NF, 128, NSA)),
        "a_Dp": cT(np.asarray(full["L0_attn_a_D_param"], f32).reshape(NF, 128, 1)),
        "b_inprojT": cT(np.asarray(full["L0_attn_b_in_proj_w"], f32).T),
        "b_dtinT": cT(np.asarray(full["L0_attn_b_dt_in_proj_w"], f32).T),
        "b_dtprojT": cT(np.asarray(full["L0_attn_b_dt_proj_w"], f32).T),
        "b_dtb": cT(np.asarray(full["L0_attn_b_dt_proj_b"], f32).reshape(NFB, 128, 1)),
        "b_convw": cT(np.asarray(full["L0_attn_b_conv_w"], f32).reshape(NFB, 128, 4)),
        "b_convb": cT(np.asarray(full["L0_attn_b_conv_b"], f32).reshape(NFB, 128, 1)),
        "b_alog": cT(np.asarray(full["L0_attn_b_A_log"], f32).reshape(NFB, 128, NST)),
        "b_Dp": cT(np.asarray(full["L0_attn_b_D"], f32).reshape(NFB, 128, 1)),
        "b_outprojT": cT(np.asarray(full["L0_attn_b_out_proj_w"], f32).T),
        "f_WqT": cT(np.asarray(full["L0_ffn_Wq"], f32).T),
        "f_WkT": cT(np.asarray(full["L0_ffn_Wk"], f32).T),
        "f_WvT": cT(np.asarray(full["L0_ffn_Wv"], f32).T),
        "f_Wo": cT(np.asarray(full["L0_ffn_Wo"], f32)),
        "f_A": cT(np.asarray(full["L0_ffn_A"], f32)),
        "f_B": cT(np.asarray(full["L0_ffn_B"], f32)),
        "f_ln1": cT(np.asarray(full["L0_ffn_ln1_w"], f32).reshape(NF, 128, 1)),
        "f_ln2": cT(np.asarray(full["L0_ffn_ln2_w"], f32).reshape(NF, 128, 1)),
        "f_WinT": cT(np.asarray(full["L0_ffn_W_in_base"], f32).T),
        "f_WupT": cT(np.asarray(full["L0_ffn_W_up_base"], f32).T),
        "f_WdownT": cT(np.asarray(full["L0_ffn_W_down"], f32).T),
        "selx": selx, "selb": selb, "selc": selc, "selone": selone,
        "trimask": trimask,
    }
    in_maps = []
    for c in range(8):
        b = c // 4
        jA, jB = c % 4, 7 - c % 4
        xc = np.concatenate([x[b, CH * jA:CH * (jA + 1)],
                             x[b, CH * jB:CH * (jB + 1)]], axis=0)
        m = {"x0": np.ascontiguousarray(xc.T)}
        m.update(shared)
        in_maps.append(m)
    return in_maps


def gather_output(results):
    y = np.empty((NB, 4096, D), np.float32)
    for c in range(8):
        b = c // 4
        jA, jB = c % 4, 7 - c % 4
        out = results[c]["y0"]
        y[b, CH * jA:CH * (jA + 1)] = out[:, :CH].T
        y[b, CH * jB:CH * (jB + 1)] = out[:, CH:].T
    return y


_NC_CACHE = {}


def kernel(**inputs):
    key = N_PHASES
    if key not in _NC_CACHE:
        _NC_CACHE[key] = build_program(N_PHASES)
    nc = _NC_CACHE[key]
    in_maps = prepare_inputs(inputs)
    res = run_bass_kernel_spmd(nc, in_maps, core_ids=list(range(8)))
    return gather_output(res.results)


# version 08 (final)
# speedup vs baseline: 1.2391x; 1.0139x over previous
import contextlib
import numpy as np
import concourse.bass as bass
import concourse.mybir as mybir
from concourse.bass_utils import run_bass_kernel_spmd
from concourse.aot_env import allow_ici_all_to_all

F32 = mybir.dt.float32
BF16 = mybir.dt.bfloat16
AF = mybir.ActivationFunctionType
ALU = mybir.AluOpType
AX = mybir.AxisListType

D = 2048
NF = D // 128
T = 1024
CH = 512
NB = 2
NCHK = 8
NH = 16
DI = 4096
NFB = DI // 128
NST = 16
NSA = 8
FF = 8192
NFF = FF // 128
EPS_LN = 1e-5
EPS_RMS = 1e-6
BIG_NEG = -1.0e30
BIAS_OFF = 60.0
QSCALE = float(1.0 / np.sqrt(128.0))

N_PHASES = 99


def _chunk_owner(c):
    return (c, 0) if c <= 3 else (7 - c, 1)


def _rank_of(b, c):
    r, s = _chunk_owner(c)
    return 4 * b + r, s


class Sem:
    def __init__(self, nc, es, name):
        self.h = es.enter_context(nc.semaphore(name))
        self.n = 0


class Queue:

    def __init__(self, bd, eng):
        self.bd = bd
        self.eng = eng
        self.sems = [bd.sem(f"q_{eng}_a"), bd.sem(f"q_{eng}_b")]
        self.epoch = 0
        self.guarded = False
        self.totals = []

    def cur(self):
        return self.sems[self.epoch % 2]

    def dma(self, out, in_, transpose=False):
        self.pending = getattr(self, "pending", 0) + 1
        if not self.guarded:
            if self.epoch >= 2:
                si, tot = self.totals[self.epoch - 2]
                self.bd.wait(self.eng, self.sems[si], tot)
            self.guarded = True
        s = self.cur()
        s.n += 16

        def fn(e, out=out, in_=in_, h=s.h, tr=transpose):
            kw = {"transpose": True} if tr else {}
            e.dma_start(out=out, in_=in_, **kw).then_inc(h, 16)

        self.bd.emit(self.eng, fn)

    def flip(self):
        if getattr(self, "pending", 0) == 0:
            return self.epoch - 1
        s = self.cur()
        self.totals.append((self.epoch % 2, s.n))
        tok = self.epoch
        self.epoch += 1
        self.guarded = False
        self.pending = 0
        return tok

    def wait(self, eng, tok=None):
        if tok is None:
            if getattr(self, "pending", 0):
                self.flip()
            last = [0, 0]
            for si, tot in self.totals:
                last[si] = max(last[si], tot)
            for si in (0, 1):
                if last[si] > 0:
                    self.bd.wait(eng, self.sems[si], last[si])
            return
        if tok < 0 or tok >= len(self.totals):
            return
        si, tot = self.totals[tok]
        self.bd.wait(eng, self.sems[si], tot)


class Builder:
    ENGS = ("sync", "tensor", "vector", "scalar", "gpsimd")

    def __init__(self, nc, es):
        self.nc = nc
        self.es = es
        self.ops = {e: [] for e in self.ENGS}
        self.sems = {}

    def sem(self, name):
        if name not in self.sems:
            self.sems[name] = Sem(self.nc, self.es, name)
        return self.sems[name]

    def emit(self, eng, fn):
        self.ops[eng].append(fn)

    def wait(self, eng, sem, val=None):
        v = sem.n if val is None else val
        if v <= 0:
            return
        self.emit(eng, lambda e, h=sem.h, v=v: e.wait_ge(h, v))

    CHAIN = {"vector": "ev", "scalar": "ea", "gpsimd": "eg"}

    def op(self, eng, meth, *args, sem=None, seq=True, **kwargs):
        if eng in self.CHAIN:
            own = self.sem(self.CHAIN[eng])
            assert sem is None or sem is own, "use the engine chain sem"
            if seq and own.n > 0:
                self.wait(eng, own, own.n)
            own.n += 1

            def fn(e, a=args, k=kwargs, h=own.h):
                getattr(e, meth)(*a, **k).then_inc(h, 1)
        elif sem is not None:
            sem.n += 1

            def fn(e, a=args, k=kwargs, h=sem.h):
                getattr(e, meth)(*a, **k).then_inc(h, 1)
        else:
            def fn(e, a=args, k=kwargs):
                getattr(e, meth)(*a, **k)

        self.emit(eng, fn)

    def collective(self, name, kind, op_, groups, ins, outs):
        s = self.sem(name)
        s.n += 1

        def fn(e, kind=kind, op_=op_, groups=groups, ins=ins, outs=outs, h=s.h):
            with allow_ici_all_to_all():
                e.collective_compute(kind, op_, replica_groups=groups,
                                     ins=ins, outs=outs).then_inc(h)

        self.emit("gpsimd", fn)
        return s


class Carver:
    def __init__(self, tensor, nbytes):
        self.t = tensor
        self.nbytes = nbytes
        self.off = 0
        self.marks = []
        self.peak = 0

    def take(self, rows, cols, dtype, align=64):
        isz = 4 if dtype == F32 else 2
        self.off = (self.off + align - 1) // align * align
        start = self.off
        self.off += cols * isz
        self.peak = max(self.peak, self.off)
        assert self.off <= self.nbytes, f"scratch overflow: {self.off} > {self.nbytes}"
        base = self.t.ap().bitcast(dtype)
        return base[0:rows, start // isz: start // isz + cols]

    def push(self):
        self.marks.append(self.off)

    def pop(self):
        self.off = self.marks.pop()


def build_program(n_phases=99):
    nc = bass.Bass("TRN2", target_bir_lowering=False)

    def inp(name, shape):
        return nc.dram_tensor(name, list(shape), F32, kind="ExternalInput").ap()

    x0 = inp("x0", (D, T))
    a_xprojT = inp("a_xprojT", (D, 272))
    a_dtT = inp("a_dtT", (256, D))
    a_bdt = inp("a_bdt", (NF, 128, 1))
    a_alog = inp("a_alog", (NF, 128, NSA))
    a_Dp = inp("a_Dp", (NF, 128, 1))
    b_inprojT = inp("b_inprojT", (D, 10240))
    b_dtinT = inp("b_dtinT", (D, 128))
    b_dtprojT = inp("b_dtprojT", (128, DI))
    b_dtb = inp("b_dtb", (NFB, 128, 1))
    b_convw = inp("b_convw", (NFB, 128, 4))
    b_convb = inp("b_convb", (NFB, 128, 1))
    b_alog = inp("b_alog", (NFB, 128, NST))
    b_Dp = inp("b_Dp", (NFB, 128, 1))
    b_outprojT = inp("b_outprojT", (DI, D))
    f_WqT = inp("f_WqT", (D, D))
    f_WkT = inp("f_WkT", (D, D))
    f_WvT = inp("f_WvT", (D, D))
    f_Wo = inp("f_Wo", (D, D))
    f_A = inp("f_A", (D, NST))
    f_B = inp("f_B", (NST, FF))
    f_ln1 = inp("f_ln1", (NF, 128, 1))
    f_ln2 = inp("f_ln2", (NF, 128, 1))
    f_WinT = inp("f_WinT", (D, FF))
    f_WupT = inp("f_WupT", (D, FF))
    f_WdownT = inp("f_WdownT", (FF, D))
    selx_d = inp("selx", (32, 128))
    selb_d = inp("selb", (NST, 32, 128))
    selc_d = inp("selc", (NST, 128, 128))
    selone_d = inp("selone", (16, 16, 128))
    trimask_d = inp("trimask", (4, 128, CH))

    y0 = nc.dram_tensor("y0", [D, T], F32, kind="ExternalOutput").ap()

    kvq_send = nc.dram_tensor("kvq_send", [8, 6, 128, T], BF16)
    kvq_recv = nc.dram_tensor("kvq_recv", [8, 6, 128, T], BF16)
    o_send = nc.dram_tensor("o_send", [8, 2, 128, T], BF16)
    o_recv = nc.dram_tensor("o_recv", [8, 2, 128, T], BF16)
    a_pub = nc.dram_tensor("a_pub", [2, D + D * NSA], F32)
    a_gath = nc.dram_tensor("a_gath", [16, D + D * NSA], F32, addr_space="Shared")
    a_sel_send = nc.dram_tensor("a_sel_send", [8, 2, D * NSA], F32)
    a_sel_recv = nc.dram_tensor("a_sel_recv", [8, 2, D * NSA], F32)
    bnd_pub = nc.dram_tensor("bnd_pub", [2, 1024, 4], F32)
    bnd_gath = nc.dram_tensor("bnd_gath", [16, 1024, 4], F32, addr_space="Shared")
    bnd_send = nc.dram_tensor("bnd_send", [8, 2, 1024, 4], F32)
    bnd_recv = nc.dram_tensor("bnd_recv", [8, 2, 1024, 4], F32)
    b_pub = [nc.dram_tensor(f"b_pub{h}", [2, 2048 + 2048 * NST], F32) for h in range(2)]
    b_gathb = [nc.dram_tensor(f"b_gath{h}", [16, 2048 + 2048 * NST], F32,
                              addr_space="Shared") for h in range(2)]
    b_sel_send = [nc.dram_tensor(f"b_sel_send{h}", [8, 2, 2048 * NST], F32)
                  for h in range(2)]
    b_sel_recv = [nc.dram_tensor(f"b_sel_recv{h}", [8, 2, 2048 * NST], F32)
                  for h in range(2)]
    xrb_d = nc.dram_tensor("xrb_d", [2048, T], BF16)
    C_d = nc.dram_tensor("C_d", [NFB, 128, T], BF16)
    sz_d = nc.dram_tensor("sz_d", [NFB, 128, T], BF16)
    b01_d = nc.dram_tensor("b01_d", [NST, FF], BF16)

    es = contextlib.ExitStack()
    bd = Builder(nc, es)

    x_sb = nc.alloc_sbuf_tensor("x_sb", [128, NF * T], F32)
    big1 = nc.alloc_sbuf_tensor("big1", [128, NF * T], BF16)
    big2 = nc.alloc_sbuf_tensor("big2", [128, NF * T], BF16)
    SCR_BYTES = 81600
    scr_t = nc.alloc_sbuf_tensor("scr", [128, SCR_BYTES], mybir.dt.uint8)
    scr = Carver(scr_t, SCR_BYTES)
    bigc = Carver(big2, NF * T * 2)

    def xt(f):
        return x_sb.ap()[:, T * f: T * (f + 1)]

    def b1(f):
        return big1.ap()[:, T * f: T * (f + 1)]

    def b2(f):
        return big2.ap()[:, T * f: T * (f + 1)]

    selx_s = scr.take(32, 128, BF16)
    selb_s = [scr.take(32, 128, BF16) for _ in range(NST)]
    selc_s = [scr.take(128, 128, BF16) for _ in range(NST)]
    ones_bf = scr.take(128, 128, BF16)
    ones_f1 = scr.take(1, 128, F32)
    ones_f = scr.take(128, CH, F32)
    CONSTVALS = [0.0, 1.0, -1.0, -0.5, EPS_LN, EPS_RMS, BIAS_OFF, -BIAS_OFF, 0.5]
    const_tile = scr.take(128, len(CONSTVALS), F32)
    for ci_, cv_ in enumerate(CONSTVALS):
        nc.const_aps.register(F32, cv_, const_tile[:, ci_:ci_ + 1])
    wc_s = [scr.take(128, NST, BF16) for _ in range(NH)]
    a_bdt_s = scr.take(128, NF, F32)
    a_alog_s = scr.take(128, NF * NSA, F32)
    a_Dp_s = scr.take(128, NF, F32)
    b_dtb_s = scr.take(128, NFB, F32)
    b_convw_s = scr.take(128, NFB * 4, F32)
    b_convb_s = scr.take(128, NFB, F32)
    b_alog_s = scr.take(128, NFB * NST, F32)
    b_Dp_s = scr.take(128, NFB, F32)
    f_ln1_s = scr.take(128, NF, F32)
    f_ln2_s = scr.take(128, NF, F32)
    f_A_s = [scr.take(128, NST, BF16) for _ in range(NF)]
    rows = scr.take(1, 2 * T, F32)

    def row(i, tt=None):
        if tt is None:
            return rows[:, i * T:(i + 1) * T]
        return rows[:, i * T + tt * CH: i * T + (tt + 1) * CH]

    pb = [nc.alloc_psum_tensor(f"pb{i}", [128, CH], F32) for i in range(8)]

    def pbank(i, rows_=128, cols=CH):
        return pb[i].ap()[0:rows_, 0:cols]

    def pbank_bf(i, sl=None):
        v = pb[i].ap().bitcast(BF16)
        if sl is None:
            return v[0:128, 0:2 * CH]
        return v[0:128, sl * CH:(sl + 1) * CH]

    qs = Queue(bd, "sync")
    qg = Queue(bd, "gpsimd")
    qa = Queue(bd, "scalar")
    qv = Queue(bd, "vector")

    G8 = [list(range(8))]

    ev = bd.sem("ev")
    ea = bd.sem("ea")
    ep = bd.sem("ep")
    eg = bd.sem("eg")
    esy = bd.sem("esy")

    def barrier():
        qg.wait("gpsimd")
        qa.wait("scalar")
        qs.wait("sync")
        bd.wait("tensor", ep, ep.n)
        bd.wait("sync", esy, esy.n)
        bd.op("gpsimd", "nop", sem=eg)
        bd.op("scalar", "nop", sem=ea)
        bd.op("sync", "nop", sem=esy)
        bd.op("vector", "nop", sem=ev)
        bd.op("tensor", "nop", sem=ep)
        marks = [(ev, "vector"), (ea, "scalar"), (ep, "tensor"),
                 (eg, "gpsimd"), (esy, "sync")]
        for s, _ in marks:
            for _, eng in marks:
                bd.wait(eng, s, s.n)

    def _w(eng, sems):
        for s in sems:
            if isinstance(s, Queue):
                s.wait(eng)
            else:
                bd.wait(eng, s)

    def vwait(*sems):
        _w("vector", sems)

    def awaitx(*sems):
        _w("scalar", sems)

    def pwait(*sems):
        _w("tensor", sems)

    def gwait(*sems):
        _w("gpsimd", sems)

    def swait(*sems):
        _w("sync", sems)

    def phase0():
        qg.dma(selx_s, selx_d)
        for n in range(NST):
            qg.dma(selb_s[n], selb_d[n])
            qg.dma(selc_s[n], selc_d[n])
        for f in range(NF):
            qs.dma(a_bdt_s[:, f:f + 1], a_bdt[f])
            qs.dma(a_alog_s[:, NSA * f:NSA * (f + 1)], a_alog[f])
            qs.dma(a_Dp_s[:, f:f + 1], a_Dp[f])
            qs.dma(f_ln1_s[:, f:f + 1], f_ln1[f])
            qs.dma(f_ln2_s[:, f:f + 1], f_ln2[f])
            qg.dma(f_A_s[f], f_A[128 * f:128 * (f + 1), :])
        for f in range(NFB):
            qs.dma(b_dtb_s[:, f:f + 1], b_dtb[f])
            qs.dma(b_convw_s[:, 4 * f:4 * (f + 1)], b_convw[f])
            qs.dma(b_convb_s[:, f:f + 1], b_convb[f])
            qs.dma(b_alog_s[:, NST * f:NST * (f + 1)], b_alog[f])
            qs.dma(b_Dp_s[:, f:f + 1], b_Dp[f])
        for f in range(NF):
            qs.dma(xt(f), x0[128 * f:128 * (f + 1), :])
        scr.push()
        FQ = FF // 4
        tmpB = scr.take(NST, FQ, F32)
        tmpB2 = scr.take(NST, FQ, BF16)
        vwait(qs, qg)
        bd.op("vector", "memset", ones_bf, 1.0)
        bd.op("vector", "memset", ones_f1, 1.0)
        bd.op("vector", "memset", ones_f, 1.0)
        for ci_, cv_ in enumerate(CONSTVALS):
            bd.op("vector", "memset", const_tile[:, ci_:ci_ + 1], cv_)
        bd.op("vector", "engine_nop", sem=ev)
        awaitx(ev)
        for q4 in range(4):
            swait(ev)
            qs.dma(tmpB, f_B[:, FQ * q4:FQ * (q4 + 1)])
            vwait(qs)
            bd.op("vector", "tensor_scalar_mul", tmpB2, tmpB, 0.1, sem=ev)
            swait(ev)
            qs.dma(b01_d.ap()[:, FQ * q4:FQ * (q4 + 1)], tmpB2)
        scr.push()
        atmp = scr.take(128, NFB * NST, F32)
        awaitx(qs)
        bd.op("scalar", "activation", atmp[:, 0:NF * NSA], a_alog_s, AF.Exp, sem=ea)
        vwait(ea)
        bd.op("vector", "tensor_scalar_mul", a_alog_s, atmp[:, 0:NF * NSA], -1.0, sem=ev)
        awaitx(ev)
        bd.op("scalar", "activation", atmp, b_alog_s, AF.Exp, sem=ea)
        vwait(ea)
        bd.op("vector", "tensor_scalar_mul", b_alog_s, atmp, -1.0, sem=ev)
        scr.pop()
        barrier()
        woT = [scr.take(128, 128, BF16) for _ in range(2)]
        nmm = 0
        for h in range(NH):
            for k in range(NF):
                buf = woT[(h * NF + k) % 2]
                if h * NF + k >= 2:
                    gwait(ep)
                qg.dma(buf, f_Wo[128 * k:128 * (k + 1),
                                           128 * h:128 * (h + 1)])
                pwait(qg)
                bd.op("tensor", "matmul", pbank(0, 128, NST), buf, f_A_s[k],
                      start=(k == 0), stop=(k == NF - 1), sem=ep)
                nmm += 1
            awaitx(ep)
            bd.op("scalar", "activation", wc_s[h], pbank(0, 128, NST), AF.Copy,
                  sem=ea)
            pwait(ea)
        scr.pop()
        barrier()

    def ln_stats(tag, with_rms, s1seg=None):
        scr.push()
        xb = [scr.take(128, T, BF16) for _ in range(2)]
        xb2 = [scr.take(128, T, BF16) for _ in range(2)]
        for f in range(NF):
            b = f % 2
            if f >= 2:
                vwait(ep)
            bd.op("vector", "tensor_copy", xb[b], xt(f), sem=ev)
            awaitx(ev)
            bd.op("scalar", "activation", xb2[b], xb[b], AF.Square, sem=ea)
            pwait(ev, ea)
            for tt in range(2):
                bd.op("tensor", "matmul", pbank(0 if tt == 0 else 6, 1, CH),
                      ones_bf[:, 0:1], xb[b][:, tt * CH:(tt + 1) * CH],
                      start=(f == 0), stop=(f == NF - 1), sem=None)
                bd.op("tensor", "matmul", pbank(1 if tt == 0 else 7, 1, CH),
                      ones_bf[:, 0:1], xb2[b][:, tt * CH:(tt + 1) * CH],
                      start=(f == 0), stop=(f == NF - 1), sem=ep)
        scr.pop()
        scr.push()
        tmp = scr.take(1, T, F32)
        t2 = scr.take(1, T, F32)
        vwait(ep)
        for tt in range(2):
            sb, qb = (0, 1) if tt == 0 else (6, 7)
            bd.op("vector", "tensor_scalar_mul", row(0, tt), pbank(sb, 1, CH), 1.0 / D)
            bd.op("vector", "tensor_scalar_mul", row(1, tt), pbank(qb, 1, CH), 1.0 / D)
        bd.op("vector", "tensor_mul", tmp, row(0), row(0))
        bd.op("vector", "tensor_sub", row(1), row(1), tmp, sem=ev)
        awaitx(ev)
        bd.op("scalar", "activation", tmp, row(1), AF.Ln, bias=EPS_LN, sem=ea)
        if with_rms:
            vwait(ea)
            bd.op("vector", "tensor_scalar_add", t2, row(1), EPS_LN)
            bd.op("vector", "reciprocal", t2, t2)
            bd.op("vector", "tensor_mul", t2, row(1), t2, sem=ev)
            awaitx(ev)
        bd.op("scalar", "activation", row(1), tmp, AF.Exp, scale=-0.5, sem=ea)
        if with_rms:
            awaitx(ev)
            bd.op("scalar", "activation", t2, t2, AF.Ln, bias=EPS_RMS)
            bd.op("scalar", "activation", t2, t2, AF.Exp, scale=-0.5, sem=ea)
            vwait(ea)
            bd.op("vector", "tensor_mul", s1seg, row(1), t2, sem=ev)
        scr.pop()

    def bcast_row(bank, rseg):
        bd.op("tensor", "matmul", pbank(bank), ones_f1[:, :], rseg,
              start=True, stop=True, sem=ep)

    def phase1():
        scr.push()
        bigc.push()
        selone_s = [scr.take(16, 128, BF16) for _ in range(16)]
        aw_xproj = [bigc.take(128, 272, BF16) for _ in range(NF)]
        aw_dt = [[bigc.take(128, 128, BF16) for _ in range(NF)] for _ in range(2)]
        for n in range(16):
            qg.dma(selone_s[n], selone_d[n])
        for f in range(NF):
            qg.dma(aw_xproj[f], a_xprojT[128 * f:128 * (f + 1), :])
            qg.dma(aw_dt[0][f], a_dtT[0:128, 128 * f:128 * (f + 1)])
            qg.dma(aw_dt[1][f], a_dtT[128:256, 128 * f:128 * (f + 1)])
        pwait(qg)
        vwait(qg)
        ln_stats("p1", with_rms=False)
        pwait(ev, ea)
        bcast_row(2, row(0, 0))
        bcast_row(3, row(1, 0))
        bcast_row(4, row(0, 1))
        bcast_row(5, row(1, 1))
        MU, RS = [2, 4], [3, 5]
        xc = scr.take(128, CH, F32)
        xa32 = scr.take(128, CH, F32)
        vwait(ep)
        for f in range(NF):
            for tt in range(2):
                sl = slice(tt * CH, (tt + 1) * CH)
                bd.op("vector", "tensor_sub", xc, xt(f)[:, sl], pbank(MU[tt]))
                bd.op("vector", "tensor_mul", xa32, xc, pbank(RS[tt]))
                bd.op("vector", "tensor_copy", b1(f)[:, sl], xa32)
                bd.op("vector", "scalar_tensor_tensor", xt(f)[:, sl], xa32,
                      a_Dp_s[:, f:f + 1], xt(f)[:, sl], ALU.mult, ALU.add, sem=ev)
        dtr = [bigc.take(128, T, BF16) for _ in range(2)]
        bca = scr.take(16, T, BF16)
        for tt in range(2):
            for mb in range(3):
                mrows = 128 if mb < 2 else 16
                pwait(ev, ea)
                for k in range(NF):
                    bd.op("tensor", "matmul", pbank(6, mrows, CH),
                          aw_xproj[k][:, 128 * mb:128 * mb + mrows],
                          b1(k)[:, tt * CH:(tt + 1) * CH],
                          start=(k == 0), stop=(k == NF - 1),
                          sem=(ep if k == NF - 1 else None))
                awaitx(ep)
                dst = (dtr[mb][:, tt * CH:(tt + 1) * CH] if mb < 2
                      else bca[:, tt * CH:(tt + 1) * CH])
                bd.op("scalar", "activation", dst, pbank(6, mrows, CH), AF.Copy,
                      sem=ea)
        delta = bigc.take(128, T, F32)
        sp1 = scr.take(128, CH, F32)
        sp2 = scr.take(128, CH, F32)
        fstage = scr.take(128, 2 * NF * NSA, F32)
        dstage = scr.take(128, 2 * NF, F32)
        bigc.push()
        scr.push()
        ub = bigc.take(128, T, BF16)
        at_ = scr.take(128, T, F32)
        bt_ = scr.take(128, T, F32)
        Ht = bigc.take(128, T, F32)
        at2 = [at_, scr.t.ap().bitcast(F32)[0:128, sp1.offset:sp1.offset + T]]
        for f in range(NF):
            for tt in range(2):
                pwait(ev, ea)
                for k in range(2):
                    bd.op("tensor", "matmul", pbank(0),
                          aw_dt[k][f], dtr[k][:, tt * CH:(tt + 1) * CH],
                          start=(k == 0), stop=(k == 1),
                          sem=(ep if k == 1 else None))
                awaitx(ep, ev)
                bd.op("scalar", "activation", sp1, pbank(0), AF.Abs,
                      bias=a_bdt_s[:, f:f + 1])
                bd.op("scalar", "activation", sp1, sp1, AF.Exp, scale=-1.0)
                bd.op("scalar", "activation", sp1, sp1, AF.Ln, bias=1.0)
                bd.op("scalar", "activation", sp2, pbank(0), AF.Relu,
                      bias=a_bdt_s[:, f:f + 1], sem=ea)
                vwait(ea)
                bd.op("vector", "scalar_tensor_tensor",
                      delta[:, tt * CH:(tt + 1) * CH], sp1, 1.0, sp2,
                      ALU.mult, ALU.add,
                      accum_out=dstage[:, NF * tt + f: NF * tt + f + 1], sem=ev)
                bd.op("vector", "tensor_mul", ub[:, tt * CH:(tt + 1) * CH],
                      delta[:, tt * CH:(tt + 1) * CH],
                      b1(f)[:, tt * CH:(tt + 1) * CH], sem=ev)
            mka, mkB, mkC = {}, {}, {}
            _pv = bd.sems["ev"].n
            _pg = bd.sems["eg"].n
            mkscan = {-2: _pv, -1: _pv}
            mkb = {-2: _pv, -1: _pv}
            mkym = {-2: _pv, -1: _pv}
            mkg = {-2: _pg, -1: _pg}
            pwait(ea)

            def prep1B(n, f=f):
                BP = (1, 2) if n % 2 == 0 else (3, 4)
                bd.wait("scalar", ev, mkscan[n - 2])
                bd.op("scalar", "activation", at2[n % 2], delta, AF.Exp,
                      scale=a_alog_s[:, NSA * f + n: NSA * f + n + 1], sem=ea)
                mka[n] = ea.n
                bd.wait("tensor", ev, mkb[n - 2])
                for tt in range(2):
                    bd.op("tensor", "matmul", pbank(BP[tt]),
                          selone_s[n][0:16, :], bca[0:16, tt * CH:(tt + 1) * CH],
                          start=True, stop=True, sem=ep)
                mkB[n] = ep.n

            def prep1C(n, f=f):
                bd.wait("tensor", ev, mkym[n - 1])
                for tt in range(2):
                    bd.op("tensor", "matmul", pbank(5 + tt),
                          selone_s[8 + n][0:16, :], bca[0:16, tt * CH:(tt + 1) * CH],
                          start=True, stop=True, sem=ep)
                mkC[n] = ep.n

            prep1B(0)
            prep1C(0)
            for n in range(NSA):
                if n + 1 < NSA:
                    prep1B(n + 1)
                BP = (1, 2) if n % 2 == 0 else (3, 4)
                bd.wait("vector", ea, mka[n])
                bd.wait("vector", ep, mkB[n])
                for tt in range(2):
                    bd.op("vector", "tensor_mul", bt_[:, tt * CH:(tt + 1) * CH],
                          ub[:, tt * CH:(tt + 1) * CH], pbank(BP[tt]))
                mkb[n] = bd.sems["ev"].n
                bd.wait("vector", eg, mkg[n - 1])
                for tt in range(2):
                    bd.op("vector", "tensor_tensor_scan",
                          Ht[:, tt * CH:(tt + 1) * CH],
                          at2[n % 2][:, tt * CH:(tt + 1) * CH],
                          bt_[:, tt * CH:(tt + 1) * CH],
                          0.0, ALU.mult, ALU.add)
                    bd.op("vector", "tensor_copy",
                          fstage[:, 128 * tt + NSA * f + n: 128 * tt + NSA * f + n + 1],
                          Ht[:, (tt + 1) * CH - 1:(tt + 1) * CH])
                mkscan[n] = bd.sems["ev"].n
                bd.wait("vector", ep, mkC[n])
                bd.wait("vector", eg, mkg[n - 1])
                for tt in range(2):
                    bd.op("vector", "tensor_mul", Ht[:, tt * CH:(tt + 1) * CH],
                          Ht[:, tt * CH:(tt + 1) * CH], pbank(5 + tt))
                mkym[n] = bd.sems["ev"].n
                bd.wait("gpsimd", ev, mkym[n])
                bd.op("gpsimd", "tensor_add", xt(f), xt(f), Ht)
                mkg[n] = bd.sems["eg"].n
                if n + 1 < NSA:
                    prep1C(n + 1)
        bigc.pop()
        scr.pop()
        barrier()
        awaitx(ev)
        for c in range(2):
            qa.dma(
                   a_pub[c, 0:D].rearrange("(p f) -> p f", p=128),
                   dstage[:, NF * c:NF * (c + 1)])
            qa.dma(
                   a_pub[c, D:].rearrange("(p k) -> p k", p=128),
                   fstage[:, 128 * c:128 * (c + 1)])
        gwait(qa)
        cc1 = bd.collective("cc_a_g", "AllGather", ALU.bypass, G8,
                            [a_pub.ap().opt()], [a_gath.ap().opt()])
        gwait(cc1)
        bigc.push()
        dts_roll = [bigc.take(128, NF, F32) for _ in range(2)]
        F_roll = [bigc.take(128, NF * NSA, F32) for _ in range(2)]
        cprev = bigc.take(128, NF * NSA, F32)
        ccur = bigc.take(128, NF * NSA, F32)
        Ptmp = bigc.take(128, NSA, F32)

        def s16_of(b2, c):
            r, s = _rank_of(b2, c)
            return 2 * r + s

        for b2 in range(2):
            for c in range(NCHK):
                cseg = ccur if c % 2 else cprev
                psg = cprev if c % 2 else ccur
                vwait(qa)
                if c == 0:
                    bd.op("vector", "memset", cseg, 0.0, sem=ev)
                else:
                    ps16 = s16_of(b2, c - 1)
                    gwait(ev)
                    qg.dma(dts_roll[c % 2],
                           a_gath[ps16, 0:D].rearrange("(p f) -> p f", p=128),
                           )
                    qg.dma(F_roll[c % 2],
                           a_gath[ps16, D:].rearrange("(p k) -> p k", p=128),
                           )
                    vwait(qg)
                    awaitx(qg, ev)
                    for f in range(NF):
                        awaitx(ev)
                        bd.op("scalar", "activation", Ptmp,
                              a_alog_s[:, NSA * f:NSA * (f + 1)], AF.Exp,
                              scale=dts_roll[c % 2][:, f:f + 1], sem=ea)
                        vwait(ea)
                        bd.op("vector", "tensor_mul",
                              cseg[:, NSA * f:NSA * (f + 1)],
                              psg[:, NSA * f:NSA * (f + 1)], Ptmp)
                        bd.op("vector", "tensor_add",
                              cseg[:, NSA * f:NSA * (f + 1)],
                              cseg[:, NSA * f:NSA * (f + 1)],
                              F_roll[c % 2][:, NSA * f:NSA * (f + 1)], sem=ev)
                if c <= 3:
                    j, ci = 4 * b2 + c, 0
                else:
                    j, ci = 4 * b2 + 7 - c, 1
                awaitx(ev)
                qa.dma(
                       a_sel_send[j, ci, :].rearrange("(p k) -> p k", p=128),
                       cseg)
                gwait(qa)
        gwait(qa)
        cc2 = bd.collective("cc_a_s", "AllToAll", ALU.bypass, G8,
                            [a_sel_send.ap().opt()], [a_sel_recv.ap().opt()])
        gwait(cc2)
        myc = [bigc.take(128, NF * NSA, F32) for _ in range(2)]
        for ci in range(2):
            qg.dma(myc[ci],
                   a_sel_recv[0, ci, :].rearrange("(p k) -> p k", p=128))
        DT = scr.take(128, T, F32)
        cpA = scr.take(128, T, F32)
        fxt = scr.take(128, T, F32)
        at2 = [cpA, scr.t.ap().bitcast(F32)[0:128, sp1.offset:sp1.offset + T]]
        fxt2 = scr.t.ap().bitcast(F32)[0:128, xc.offset:xc.offset + T]
        fxtb = [fxt, fxt2]
        vwait(qg)
        awaitx(qg)
        for f in range(NF):
            for tt in range(2):
                pwait(ev, ea)
                for k in range(2):
                    bd.op("tensor", "matmul", pbank(0),
                          aw_dt[k][f], dtr[k][:, tt * CH:(tt + 1) * CH],
                          start=(k == 0), stop=(k == 1),
                          sem=(ep if k == 1 else None))
                awaitx(ep, ev)
                bd.op("scalar", "activation", sp1, pbank(0), AF.Abs,
                      bias=a_bdt_s[:, f:f + 1])
                bd.op("scalar", "activation", sp1, sp1, AF.Exp, scale=-1.0)
                bd.op("scalar", "activation", sp1, sp1, AF.Ln, bias=1.0)
                bd.op("scalar", "activation", sp2, pbank(0), AF.Relu,
                      bias=a_bdt_s[:, f:f + 1], sem=ea)
                vwait(ea)
                bd.op("vector", "scalar_tensor_tensor",
                      delta[:, tt * CH:(tt + 1) * CH], sp1, 1.0, sp2,
                      ALU.mult, ALU.add, sem=ev)
                bd.op("vector", "tensor_tensor_scan",
                      DT[:, tt * CH:(tt + 1) * CH], ones_f,
                      delta[:, tt * CH:(tt + 1) * CH], 0.0,
                      ALU.mult, ALU.add, sem=ev)
            mka, mkC = {}, {}
            _pv = bd.sems["ev"].n
            _pg = bd.sems["eg"].n
            mkstt = {-2: _pv, -1: _pv}
            mkg1f = {-2: _pg, -1: _pg}
            pwait(ea)

            def prep1f(n, f=f):
                CP = (1, 2) if n % 2 == 0 else (3, 4)
                bd.wait("scalar", ev, mkstt[n - 2])
                bd.op("scalar", "activation", at2[n % 2], DT, AF.Exp,
                      scale=a_alog_s[:, NSA * f + n: NSA * f + n + 1], sem=ea)
                mka[n] = ea.n
                bd.wait("tensor", ev, mkstt[n - 2])
                for tt in range(2):
                    bd.op("tensor", "matmul", pbank(CP[tt]),
                          selone_s[8 + n][0:16, :], bca[0:16, tt * CH:(tt + 1) * CH],
                          start=True, stop=True, sem=ep)
                mkC[n] = ep.n

            prep1f(0)
            for n in range(NSA):
                if n + 1 < NSA:
                    prep1f(n + 1)
                CP = (1, 2) if n % 2 == 0 else (3, 4)
                bd.wait("vector", ea, mka[n])
                bd.wait("vector", ep, mkC[n])
                bd.wait("vector", eg, mkg1f[n - 2])
                for tt in range(2):
                    bd.op("vector", "scalar_tensor_tensor",
                          fxtb[n % 2][:, tt * CH:(tt + 1) * CH],
                          at2[n % 2][:, tt * CH:(tt + 1) * CH],
                          myc[tt][:, NSA * f + n: NSA * f + n + 1],
                          pbank(CP[tt]), ALU.mult, ALU.mult, sem=ev)
                mkstt[n] = bd.sems["ev"].n
                bd.wait("gpsimd", ev, mkstt[n])
                bd.op("gpsimd", "tensor_add", xt(f), xt(f), fxtb[n % 2])
                mkg1f[n] = bd.sems["eg"].n
        bigc.pop()
        bigc.pop()
        scr.pop()
        barrier()


    def phase_ffn(tag):
        scr.push()
        trimask_s = [scr.take(128, CH, BF16) for _ in range(4)]
        for s in range(4):
            qg.dma(trimask_s[s], trimask_d[s])
        qg.flip()
        vwait(qg)
        s1row = scr.take(1, T, F32)
        brow = scr.take(1, CH, F32)
        lrow = scr.take(1, CH, F32)
        ln_stats(tag, with_rms=True, s1seg=s1row)
        pwait(ev, ea)
        bcast_row(2, row(0, 0))
        bcast_row(3, s1row[:, 0:CH])
        bcast_row(4, row(0, 1))
        bcast_row(5, s1row[:, CH:2 * CH])
        MU, S1 = [2, 4], [3, 5]
        xc = scr.take(128, CH, F32)
        vwait(ep)
        for f in range(NF):
            for tt in range(2):
                sl = slice(tt * CH, (tt + 1) * CH)
                bd.op("vector", "tensor_sub", xc, xt(f)[:, sl], pbank(MU[tt]))
                bd.op("vector", "scalar_tensor_tensor", b1(f)[:, sl], xc,
                      f_ln1_s[:, f:f + 1], pbank(S1[tt]), ALU.mult, ALU.mult,
                      sem=ev)
        wbuf = [scr.take(128, NF * 128, BF16) for _ in range(2)]
        wbuf2 = [scr.take(128, NF * 128, BF16) for _ in range(2)]
        qkvt = [scr.take(128, CH, BF16) for _ in range(4)]
        WT = [f_WqT, f_WkT, f_WvT]
        gwait(ep, ev, ea)
        nblk = 0
        for qkv in range(3):
            for mb in range(NH):
                wb = wbuf[nblk % 2]
                if nblk >= 2:
                    gwait(ep)
                qg.dma(wb.rearrange("p (k m) -> p k m", k=NF),
                       WT[qkv][:, 128 * mb:128 * (mb + 1)]
                       .rearrange("(k p) m -> p k m", p=128))
                qg.flip()
                pwait(qg)
                if nblk == 0:
                    pwait(ev)
                for tt in range(2):
                    bank = tt
                    pwait(ea)
                    for k in range(NF):
                        bd.op("tensor", "matmul", pbank(bank),
                              wb[:, 128 * k:128 * (k + 1)],
                              b1(k)[:, tt * CH:(tt + 1) * CH],
                              start=(k == 0), stop=(k == NF - 1),
                              sem=(ep if k == NF - 1 else None))
                    awaitx(ep)
                    if nblk >= 2:
                        qa.wait("scalar", qa.epoch - 2)
                    qt = qkvt[2 * (nblk % 2) + tt]
                    sc = QSCALE if qkv == 0 else 1.0
                    bd.op("scalar", "activation", qt, pbank(bank), AF.Copy,
                          scale=sc, sem=ea)
                    awaitx(ea)
                    s6 = 2 * qkv + (mb % 2)
                    qa.dma(kvq_send.ap()[mb // 2, s6, :, tt * CH:(tt + 1) * CH],
                           qt)
                qa.flip()
                nblk += 1
        gwait(qa)
        gwait(ev, ea)
        cc_kvq = bd.collective("cc_kvq_" + tag, "AllToAll", ALU.bypass, G8,
                               [kvq_send.ap().opt()], [kvq_recv.ap().opt()])
        gwait(cc_kvq)
        bd.wait("sync", cc_kvq, 1)
        bigc.push()
        qtile = [bigc.take(128, CH, BF16) for _ in range(2)]
        ktile = [bigc.take(128, CH, BF16) for _ in range(3)]
        vtile = [bigc.take(128, CH, BF16) for _ in range(3)]
        ptile = [bigc.take(128, CH, BF16) for _ in range(2)]
        tmpbf = bigc.take(128, CH, BF16)
        linv_sb = bigc.take(128, CH, F32)
        o_sb = [bigc.take(128, CH, BF16) for _ in range(2)]
        SB = [0, 1]
        nq = 0

        def load_kv(buf_i, b2, kc, hh):
            rk2, sl2 = _rank_of(b2, kc)
            qs.dma(ktile[buf_i],
                   kvq_recv.ap()[rk2, 2 + hh, :, sl2 * CH:(sl2 + 1) * CH])
            for s4 in range(4):
                qs.dma(vtile[buf_i][:, 128 * s4:128 * (s4 + 1)],
                       kvq_recv.ap()[rk2, 4 + hh, :,
                                     sl2 * CH + 128 * s4: sl2 * CH + 128 * (s4 + 1)],
                       transpose=True)
            return qs.flip()

        for b2 in range(2):
            for hh in range(2):
                for qrow in range(NCHK):
                    rk, sl_ = _rank_of(b2, qrow)
                    qb = qtile[nq % 2]
                    bd.wait("sync", ev, ev.n)
                    bd.wait("sync", ep, ep.n)
                    qs.dma(qb, kvq_recv.ap()[rk, hh, :, sl_ * CH:(sl_ + 1) * CH])
                    tok = load_kv(qrow % 3, b2, qrow, hh)
                    vwait(qs)
                    bd.op("vector", "tensor_mul", tmpbf, qb,
                          ktile[qrow % 3], sem=ev)
                    pwait(ev, ea)
                    bd.op("tensor", "matmul", pbank(4, 1, CH), ones_bf[:, 0:1],
                          tmpbf, start=True, stop=True, sem=ep)
                    vwait(ep)
                    bd.op("vector", "tensor_scalar", brow, pbank(4, 1, CH),
                          -1.0, -BIAS_OFF, ALU.mult, ALU.add, sem=ev)
                    nsubs = 4 * (qrow + 1)
                    act_mark = {}
                    pv_mark = {-2: 0, -1: 0}
                    toks = None
                    subq = []
                    for kc_ in range(qrow, -1, -1):
                        for s4_ in range(4):
                            subq.append((kc_, s4_))

                    def emit_qk(i):
                        kc, s4 = subq[i]
                        sbank = SB[i % 2]
                        if i >= 2:
                            bd.wait("tensor", ea, act_mark[i - 2])
                        bd.op("tensor", "matmul", pbank(sbank),
                              ones_f1[:, :], brow,
                              start=True, stop=False)
                        bd.op("tensor", "matmul", pbank(sbank),
                              ktile[kc % 3][:, 128 * s4:128 * (s4 + 1)], qb,
                              start=False, stop=True, sem=ep)
                        return ep.n

                    def emit_exp(i, qkm):
                        kc, s4 = subq[i]
                        sbank = SB[i % 2]
                        if kc == qrow:
                            bd.wait("vector", ep, qkm)
                            bd.op("vector", "tensor_add", pbank(sbank),
                                  pbank(sbank), trimask_s[s4], sem=ev)
                            awaitx(ev)
                        else:
                            bd.wait("scalar", ep, qkm)
                        bd.wait("scalar", ep, pv_mark[i - 2])
                        bd.op("scalar", "activation", ptile[i % 2], pbank(sbank),
                              AF.Exp, sem=ea)
                        act_mark[i] = ea.n

                    def emit_lpv(i):
                        kc, s4 = subq[i]
                        first = (i == 0)
                        last = (i == nsubs - 1)
                        bd.wait("tensor", ea, act_mark[i])
                        bd.op("tensor", "matmul", pbank(3, 1, CH),
                              ones_bf[:, 0:1], ptile[i % 2],
                              start=first, stop=last, sem=None)
                        bd.op("tensor", "matmul", pbank(2),
                              vtile[kc % 3][:, 128 * s4:128 * (s4 + 1)],
                              ptile[i % 2],
                              start=first, stop=last, sem=ep)
                        pv_mark[i] = ep.n

                    pwait(ev)
                    prev_kc = None
                    for i in range(nsubs):
                        kc, s4 = subq[i]
                        if i >= 1:
                            emit_lpv(i - 1)
                        if kc != prev_kc:
                            if kc != qrow:
                                qs.wait("tensor", toks)
                                qs.wait("scalar", toks)
                            if kc > 0:
                                bd.wait("sync", ep, ep.n)
                                toks = load_kv((kc - 1) % 3, b2, kc - 1, hh)
                            prev_kc = kc
                        qkm = emit_qk(i)
                        emit_exp(i, qkm)
                    emit_lpv(nsubs - 1)
                    vwait(ep)
                    bd.op("vector", "reciprocal", lrow, pbank(3, 1, CH),
                          sem=ev)
                    pwait(ev)
                    bd.op("tensor", "matmul", pbank(4), ones_f1[:, :],
                          lrow, start=True, stop=True, sem=ep)
                    awaitx(ep)
                    bd.op("scalar", "activation", linv_sb, pbank(4), AF.Copy,
                          sem=ea)
                    vwait(ea)
                    if nq >= 2:
                        qa.wait("vector", qa.epoch - 2)
                    osb = o_sb[nq % 2]
                    bd.op("vector", "tensor_mul", osb, pbank(2), linv_sb,
                          sem=ev)
                    awaitx(ev)
                    qa.dma(o_send.ap()[rk, hh, :, sl_ * CH:(sl_ + 1) * CH], osb)
                    qa.flip()
                    nq += 1
        gwait(qa)
        gwait(ev, ea)
        cc_o = bd.collective("cc_o_" + tag, "AllToAll", ALU.bypass, G8,
                             [o_send.ap().opt()], [o_recv.ap().opt()])
        gwait(cc_o)
        bd.wait("sync", cc_o, 1)
        o2 = [bigc.take(128, CH, BF16) for _ in range(2)]
        tanh_sb = bigc.take(16, T, F32)
        modh_bf = scr.take(16, T, BF16)
        aA_mark = {}
        nh2 = 0
        for tt in range(2):
            for h in range(NH):
                ob = o2[nh2 % 2]
                if nh2 >= 2:
                    bd.wait("sync", ep, aA_mark[nh2 - 2])
                qs.dma(ob, o_recv.ap()[h // 2, h % 2, :, tt * CH:(tt + 1) * CH])
                qs.flip()
                pwait(qs)
                if nh2 == 0:
                    pwait(ea, ev)
                bd.op("tensor", "matmul", pbank(5 + tt, 16, CH), wc_s[h], ob,
                      start=(h == 0), stop=(h == NH - 1), sem=ep)
                aA_mark[nh2] = ep.n
                nh2 += 1
        awaitx(ep)
        for tt in range(2):
            bd.op("scalar", "activation", tanh_sb[:, tt * CH:(tt + 1) * CH],
                  pbank(5 + tt, 16, CH), AF.Tanh, sem=ea)
        pwait(ev, ea)
        bcast_row(2, row(0, 0))
        bcast_row(3, s1row[:, 0:CH])
        bcast_row(4, row(0, 1))
        bcast_row(5, s1row[:, CH:2 * CH])
        vwait(ep)
        for f in range(NF):
            for tt in range(2):
                sl = slice(tt * CH, (tt + 1) * CH)
                bd.op("vector", "tensor_sub", xc, xt(f)[:, sl], pbank(MU[tt]))
                bd.op("vector", "scalar_tensor_tensor", b1(f)[:, sl], xc,
                      f_ln2_s[:, f:f + 1], pbank(S1[tt]), ALU.mult, ALU.mult,
                      sem=ev)
        for tt in range(2):
            pwait(ev)
            pwait(ea)
            for k in range(NF):
                bd.op("tensor", "matmul", pbank(5 + tt, 16, CH), f_A_s[k],
                      b1(k)[:, tt * CH:(tt + 1) * CH],
                      start=(k == 0), stop=(k == NF - 1),
                      sem=(ep if k == NF - 1 else None))
            vwait(ep)
            bd.op("vector", "tensor_mul", modh_bf[:, tt * CH:(tt + 1) * CH],
                  tanh_sb[:, tt * CH:(tt + 1) * CH], pbank(5 + tt, 16, CH),
                  sem=ev)
        barrier()
        bigc.pop()
        bigc.push()
        gu = bigc.take(128, 16 * 2 * CH, BF16)

        def gusl(fb, tt):
            o = (fb * 2 + tt) * CH
            return gu[:, o:o + CH]

        wbufd = scr.take(128, 16 * 128, BF16)
        b01t = [scr.take(16, 128, BF16) for _ in range(2)]
        gate_bf = scr.take(128, CH, BF16)
        nfbq = 0

        def mlp_load(i):
            gwait(ep)
            qg.dma(wbuf[i % 2].rearrange("p (k m) -> p k m", k=NF),
                   f_WinT[:, 128 * i:128 * (i + 1)]
                   .rearrange("(k p) m -> p k m", p=128))
            qg.dma(b01t[i % 2], b01_d.ap()[:, 128 * i:128 * (i + 1)])
            qg.dma(wbuf2[i % 2].rearrange("p (k m) -> p k m", k=NF),
                   f_WupT[:, 128 * i:128 * (i + 1)]
                   .rearrange("(k p) m -> p k m", p=128))
            return qg.flip()

        pend2 = mlp_load(0)
        for quarter in range(4):
            for fbq in range(16):
                fb = 16 * quarter + fbq
                wbi = wbuf[nfbq % 2]
                wbu2 = wbuf2[nfbq % 2]
                tok = pend2
                if fb + 1 < NFF:
                    pend2 = mlp_load(fb + 1)
                qg.wait("tensor", tok)
                for tt in range(2):
                    gb = (2 * nfbq + tt) % 2
                    ub = 2 + (2 * nfbq + tt) % 2
                    pwait(ea, ev)
                    for k in range(NF):
                        bd.op("tensor", "matmul", pbank(gb),
                              wbi[:, 128 * k:128 * (k + 1)],
                              b1(k)[:, tt * CH:(tt + 1) * CH],
                              start=(k == 0), stop=False)
                    bd.op("tensor", "matmul", pbank(gb),
                          b01t[nfbq % 2][0:16, :],
                          modh_bf[0:16, tt * CH:(tt + 1) * CH],
                          start=False, stop=True, sem=ep)
                    for k in range(NF):
                        bd.op("tensor", "matmul", pbank(ub),
                              wbu2[:, 128 * k:128 * (k + 1)],
                              b1(k)[:, tt * CH:(tt + 1) * CH],
                              start=(k == 0), stop=(k == NF - 1),
                              sem=(ep if k == NF - 1 else None))
                    awaitx(ep)
                    bd.op("scalar", "activation", gate_bf, pbank(gb), AF.Silu,
                          sem=ea)
                    vwait(ea)
                    bd.op("vector", "tensor_mul", gusl(fbq, tt), gate_bf,
                          pbank(ub), sem=ev)
                nfbq += 1
            for m in range(NF):
                gwait(ep)
                qg.dma(wbufd.rearrange("p (k mm) -> p k mm", k=16),
                       f_WdownT[2048 * quarter:2048 * (quarter + 1),
                                128 * m:128 * (m + 1)]
                       .rearrange("(k p) mm -> p k mm", p=128))
                qg.flip()
                pwait(qg)
                for tt in range(2):
                    dbank = 4 + tt
                    sl = slice(tt * CH, (tt + 1) * CH)
                    pwait(ev)
                    for k in range(16):
                        bd.op("tensor", "matmul", pbank(dbank),
                              wbufd[:, 128 * k:128 * (k + 1)], gusl(k, tt),
                              start=(k == 0), stop=(k == 15),
                              sem=(ep if k == 15 else None))
                    if quarter == 0:
                        bcast_row(6, row(0, tt))
                        bcast_row(7, row(1, tt))
                    vwait(ep)
                    if quarter == 0:
                        bd.op("vector", "tensor_sub", xc, xt(m)[:, sl],
                              pbank(6))
                        bd.op("vector", "tensor_mul", xc, xc, pbank(7))
                        bd.op("vector", "tensor_add", xc, xc, pbank(dbank))
                        bd.op("vector", "tensor_add", xt(m)[:, sl],
                              xt(m)[:, sl], xc, sem=ev)
                    else:
                        bd.op("vector", "tensor_add", xt(m)[:, sl],
                              xt(m)[:, sl], pbank(dbank), sem=ev)
        bigc.pop()
        scr.pop()
        barrier()


    def phase_b():
        scr.push()
        ln_stats("pb", with_rms=False)
        pwait(ev, ea)
        bcast_row(2, row(0, 0))
        bcast_row(3, row(1, 0))
        bcast_row(4, row(0, 1))
        bcast_row(5, row(1, 1))
        MU, RS = [2, 4], [3, 5]
        xc = scr.take(128, CH, F32)
        vwait(ep)
        for f in range(NF):
            for tt in range(2):
                sl = slice(tt * CH, (tt + 1) * CH)
                bd.op("vector", "tensor_sub", xc, xt(f)[:, sl], pbank(MU[tt]))
                bd.op("vector", "tensor_mul", b1(f)[:, sl], xc, pbank(RS[tt]),
                      sem=ev)
        wbuf = [scr.take(128, NF * 128, BF16) for _ in range(2)]
        evt = [scr.take(128, CH, BF16) for _ in range(2)]
        dtin_sb = scr.take(128, T, BF16)
        gwait(ep, ev, ea)
        qg.dma(wbuf[0].rearrange("p (k m) -> p k m", k=NF),
               b_dtinT.rearrange("(k p) m -> p k m", p=128))
        qg.flip()
        pwait(qg)
        pwait(ev)
        for tt in range(2):
            pwait(ea)
            for k in range(NF):
                bd.op("tensor", "matmul", pbank(0),
                      wbuf[0][:, 128 * k:128 * (k + 1)],
                      b1(k)[:, tt * CH:(tt + 1) * CH],
                      start=(k == 0), stop=(k == NF - 1),
                      sem=(ep if k == NF - 1 else None))
            awaitx(ep)
            bd.op("scalar", "activation", dtin_sb[:, tt * CH:(tt + 1) * CH],
                  pbank(0), AF.Copy, sem=ea)
        nblk = 0
        for mb in range(16):
            wb = wbuf[nblk % 2]
            gwait(ep)
            qg.dma(wb.rearrange("p (k m) -> p k m", k=NF),
                   b_inprojT[:, 128 * (32 + mb):128 * (33 + mb)]
                   .rearrange("(k p) m -> p k m", p=128))
            qg.flip()
            pwait(qg)
            for tt in range(2):
                bank = tt
                pwait(ea)
                for k in range(NF):
                    bd.op("tensor", "matmul", pbank(bank),
                          wb[:, 128 * k:128 * (k + 1)],
                          b1(k)[:, tt * CH:(tt + 1) * CH],
                          start=(k == 0), stop=(k == NF - 1),
                          sem=(ep if k == NF - 1 else None))
                awaitx(ep)
                if nblk >= 1:
                    qa.wait("scalar", qa.epoch - 1)
                et = evt[tt]
                bd.op("scalar", "activation", et, pbank(bank), AF.Copy, sem=ea)
                awaitx(ea)
                qa.dma(xrb_d.ap()[128 * mb:128 * (mb + 1),
                                  tt * CH:(tt + 1) * CH], et)
            qa.flip()
            nblk += 1
        zz_sb = scr.take(128, 32, BF16)
        bd.op("vector", "memset", zz_sb, 0.0, sem=ev)
        zz_d = nc.dram_tensor("zz_" + "b", [1024, 4], BF16)
        vwait(ev)
        gwait(qa)
        gwait(ev)
        qg.dma(zz_d.ap().rearrange("(a p) c -> p a c", p=128), 
               zz_sb.rearrange("p (a c) -> p a c", c=4))
        for c in range(2):
            qg.dma(bnd_pub.ap()[c], xrb_d.ap()[0:1024, c * CH + 508:c * CH + 512])
        qg.flip()
        gwait(qg)
        ccb1 = bd.collective("cc_bnd_g", "AllGather", ALU.bypass, G8,
                             [bnd_pub.ap().opt()], [bnd_gath.ap().opt()])
        gwait(ccb1)
        for j in range(8):
            b2 = j // 4
            for ci, c in enumerate((j % 4, 7 - j % 4)):
                if c == 0:
                    qg.dma(bnd_send.ap()[j, ci], zz_d.ap())
                else:
                    rk, sl2 = _rank_of(b2, c - 1)
                    qg.dma(bnd_send.ap()[j, ci], bnd_gath.ap()[2 * rk + sl2])
        qg.flip()
        gwait(qg)
        ccb2 = bd.collective("cc_bnd_s", "AllToAll", ALU.bypass, G8,
                             [bnd_send.ap().opt()], [bnd_recv.ap().opt()])
        gwait(ccb2)
        bnd_sb = [scr.take(32, 32 * 4, BF16) for _ in range(2)]
        for ci in range(2):
            qg.dma(bnd_sb[ci].rearrange("p (w c) -> p w c", c=4),
                   bnd_recv.ap()[0, ci].rearrange("(w p) c -> p w c", p=32))
        qg.flip()
        bdtp_sb = scr.take(128, 16 * 128, BF16)
        dt_fb = scr.take(128, T, F32)
        xact_bf = scr.take(128, T, BF16)
        u_bf = scr.take(128, T, BF16)
        _xoff = xact_bf.offset
        fixsum = scr.t.ap().bitcast(F32)[0:128, _xoff // 2:_xoff // 2 + T]
        at_ = scr.take(128, T, F32)
        bt_ = scr.take(128, T, F32)
        Ht = scr.take(128, T, F32)
        C_sb = scr.take(128, T, BF16)
        z_bf = scr.take(128, T, BF16)
        xrw = [scr.take(32, T, BF16)]
        Bw = [scr.take(32, T, BF16)]
        _wo = xrw[0].offset
        bt2_ = scr.t.ap().bitcast(F32)[0:128, _wo // 2:_wo // 2 + T]
        dstage_b = scr.take(128, 2 * 16, F32)
        fstage_b = scr.take(128, 2 * 256, F32)
        F_roll = [scr.take(128, 256, F32) for _ in range(2)]
        myc = F_roll
        cprev = bt_[:, 0:256]
        ccur = bt_[:, 256:512]
        Ptmp = bt_[:, 512:528]
        dts_roll = [bt_[:, 528 + 16 * i:544 + 16 * i] for i in range(2)]

        def zc_mms(fb, dst_bf, rowbase, silu):
            nonlocal_wb = wbuf[fb % 2]
            gwait(ep)
            qg.dma(nonlocal_wb.rearrange("p (k m) -> p k m", k=NF),
                   b_inprojT[:, rowbase + 128 * fb: rowbase + 128 * (fb + 1)]
                   .rearrange("(k p) m -> p k m", p=128))
            qg.flip()
            pwait(qg)
            pwait(ev)
            for tt in range(2):
                pwait(ea)
                for k in range(NF):
                    bd.op("tensor", "matmul", pbank(3),
                          nonlocal_wb[:, 128 * k:128 * (k + 1)],
                          b1(k)[:, tt * CH:(tt + 1) * CH],
                          start=(k == 0), stop=(k == NF - 1),
                          sem=(ep if k == NF - 1 else None))
                awaitx(ep)
                bd.op("scalar", "activation", dst_bf[:, tt * CH:(tt + 1) * CH],
                      pbank(3), AF.Silu if silu else AF.Copy, sem=ea)

        bdtp_tok = [None]

        def dt_softplus(fb, want_dtsum):
            fbq = fb % 16
            qs_ = qg.wait("tensor", bdtp_tok[0])
            for tt in range(2):
                pwait(ea, ev)
                bd.op("tensor", "matmul", pbank(0),
                      bdtp_sb[:, 128 * (fb % 16):128 * (fb % 16 + 1)],
                      dtin_sb[:, tt * CH:(tt + 1) * CH],
                      start=True, stop=True, sem=ep)
                awaitx(ep, ev)
                bd.op("scalar", "activation", sp1b, pbank(0), AF.Abs,
                      bias=b_dtb_s[:, fb:fb + 1])
                bd.op("scalar", "activation", sp1b, sp1b, AF.Exp, scale=-1.0)
                bd.op("scalar", "activation", sp1b, sp1b, AF.Ln, bias=1.0)
                bd.op("scalar", "activation", sp2b, pbank(0), AF.Relu,
                      bias=b_dtb_s[:, fb:fb + 1], sem=ea)
                vwait(ea)
                kw = {}
                if want_dtsum:
                    kw["accum_out"] = dstage_b[:, 16 * tt + fbq: 16 * tt + fbq + 1]
                bd.op("vector", "scalar_tensor_tensor",
                      dt_fb[:, tt * CH:(tt + 1) * CH], sp1b, 1.0, sp2b,
                      ALU.mult, ALU.add, sem=ev, **kw)

        _reg = scr.take(128, 1040, F32)
        sp1b = _reg[:, 0:512]
        sp2b = _reg[:, 512:1024]
        xpad = [_reg[:, 0:520]]
        cacc = _reg[:, 520:1032]
        at2 = [at_, _reg[:, 0:T]]
        def ysl_ap(k, tt):
            o = (2 * k + tt) * CH
            return big2.ap()[:, o:o + CH]

        nfb3 = 0
        qa.wait("sync")
        for half in range(2):
            gwait(ep)
            qg.dma(bdtp_sb, b_dtprojT[:, 2048 * half:2048 * (half + 1)])
            bdtp_tok[0] = qg.flip()
            vwait(qa)
            for fbq in range(16):
                fb = 16 * half + fbq
                xw = xrw[0]
                bw = Bw[0]
                bd.wait("sync", ep, ep.n)
                bd.wait("sync", ev, ev.n)
                qs.dma(xw, xrb_d.ap()[32 * fb:32 * (fb + 1), :])
                qs.dma(bw, xrb_d.ap()[1024 + 32 * fb:1024 + 32 * (fb + 1), :])
                qs.flip()
                dt_softplus(fb, True)
                pwait(qs, qg)
                for c in range(2):
                    pwait(ea)
                    bd.op("tensor", "matmul", pbank(1),
                          selx_s[0:32, :], xw[0:32, c * CH:(c + 1) * CH],
                          start=True, stop=True)
                    bd.op("tensor", "matmul", pbank(2, 128, 4),
                          selx_s[0:32, :], bnd_sb[c][0:32, 4 * fb:4 * (fb + 1)],
                          start=True, stop=True, sem=ep)
                    awaitx(ep, ev)
                    awaitx(ev)
                    bd.op("scalar", "activation", xpad[0][:, 4:516], pbank(1),
                          AF.Copy)
                    bd.op("scalar", "activation", xpad[0][:, 0:4],
                          pbank(2, 128, 4), AF.Copy, sem=ea)
                    vwait(ea)
                    bd.op("vector", "tensor_scalar",
                          cacc, xpad[0][:, 1:513],
                          b_convw_s[:, 4 * fb:4 * fb + 1], None, ALU.mult)
                    for kk in range(1, 4):
                        bd.op("vector", "scalar_tensor_tensor", cacc,
                              xpad[0][:, 1 + kk:513 + kk],
                              b_convw_s[:, 4 * fb + kk:4 * fb + kk + 1],
                              cacc, ALU.mult, ALU.add)
                    bd.op("vector", "engine_nop", sem=ev)
                    awaitx(ev)
                    bd.op("scalar", "activation",
                          xact_bf[:, c * CH:(c + 1) * CH], cacc, AF.Silu,
                          bias=b_convb_s[:, fb:fb + 1], sem=ea)
                vwait(ea, ev)
                bd.op("vector", "tensor_mul", u_bf, dt_fb, xact_bf, sem=ev)
                awaitx(qa)
                zc_mms(fb, C_sb, 128 * 48, False)
                awaitx(ea)
                qa.dma(C_d.ap()[fb], C_sb)
                qa.flip()
                mka, mkB, mkC = {}, {}, {}
                _pv = bd.sems["ev"].n
                mkscan = {-2: _pv, -1: _pv}
                mkb = {-2: _pv, -1: _pv}
                mkym = {-2: _pv, -1: _pv}
                pwait(ea)

                def prep_nB(n, fb=fb):
                    BP = (1, 2) if n % 2 == 0 else (3, 4)
                    bd.wait("scalar", ev, mkscan[n - 2])
                    bd.op("scalar", "activation", at2[n % 2], dt_fb, AF.Exp,
                          scale=b_alog_s[:, NST * fb + n: NST * fb + n + 1],
                          sem=ea)
                    mka[n] = ea.n
                    bd.wait("tensor", ev, mkb[n - 2])
                    for tt in range(2):
                        bd.op("tensor", "matmul", pbank(BP[tt]),
                              selb_s[n][0:32, :],
                              bw[0:32, tt * CH:(tt + 1) * CH],
                              start=True, stop=True, sem=ep)
                    mkB[n] = ep.n

                def prep_nC(n, fb=fb):
                    bd.wait("tensor", ev, mkym[n - 1])
                    for tt in range(2):
                        bd.op("tensor", "matmul", pbank(5 if tt == 0 else 0),
                              selc_s[n], C_sb[:, tt * CH:(tt + 1) * CH],
                              start=True, stop=True, sem=ep)
                    mkC[n] = ep.n

                prep_nB(0)
                prep_nC(0)
                for n in range(NST):
                    if n + 1 < NST:
                        prep_nB(n + 1)
                    BP = (1, 2) if n % 2 == 0 else (3, 4)
                    bd.wait("vector", ea, mka[n])
                    bd.wait("vector", ep, mkB[n])
                    for tt in range(2):
                        bd.op("vector", "tensor_mul", bt_[:, tt * CH:(tt + 1) * CH],
                              u_bf[:, tt * CH:(tt + 1) * CH], pbank(BP[tt]))
                    mkb[n] = bd.sems["ev"].n
                    for tt in range(2):
                        bd.op("vector", "tensor_tensor_scan",
                              Ht[:, tt * CH:(tt + 1) * CH],
                              at2[n % 2][:, tt * CH:(tt + 1) * CH],
                              bt_[:, tt * CH:(tt + 1) * CH],
                              0.0, ALU.mult, ALU.add)
                        bd.op("vector", "tensor_copy",
                              fstage_b[:, 256 * tt + 16 * fbq + n:
                                       256 * tt + 16 * fbq + n + 1],
                              Ht[:, (tt + 1) * CH - 1:(tt + 1) * CH])
                    mkscan[n] = bd.sems["ev"].n
                    bd.wait("vector", ep, mkC[n])
                    for tt in range(2):
                        bd.op("vector", "tensor_mul", Ht[:, tt * CH:(tt + 1) * CH],
                              Ht[:, tt * CH:(tt + 1) * CH],
                              pbank(5 if tt == 0 else 0))
                    if n == 0:
                        for tt in range(2):
                            bd.op("vector", "tensor_copy", pbank(6 + tt),
                                  Ht[:, tt * CH:(tt + 1) * CH], sem=ev)
                    else:
                        for tt in range(2):
                            bd.op("vector", "tensor_add", pbank(6 + tt),
                                  pbank(6 + tt), Ht[:, tt * CH:(tt + 1) * CH],
                                  sem=ev)
                    mkym[n] = bd.sems["ev"].n
                    if n + 1 < NST:
                        prep_nC(n + 1)
                awaitx(qa)
                zc_mms(fb, z_bf, 0, True)
                awaitx(ea)
                qa.dma(sz_d.ap()[fb], z_bf)
                qa.flip()
                vwait(ea)
                for tt in range(2):
                    sl = slice(tt * CH, (tt + 1) * CH)
                    bd.op("vector", "scalar_tensor_tensor", Ht[:, sl],
                          xact_bf[:, sl], b_Dp_s[:, fb:fb + 1], pbank(6 + tt),
                          ALU.mult, ALU.add)
                    bd.op("vector", "tensor_mul", ysl_ap(fbq, tt),
                          Ht[:, sl], z_bf[:, sl], sem=ev)
                nfb3 += 1
            awaitx(ev)
            for c in range(2):
                qa.dma(b_pub[half].ap()[c, 0:2048]
                       .rearrange("(p k) -> p k", p=128),
                       dstage_b[:, 16 * c:16 * (c + 1)])
                qa.dma(b_pub[half].ap()[c, 2048:]
                       .rearrange("(p k) -> p k", p=128),
                       fstage_b[:, 256 * c:256 * (c + 1)])
            qa.flip()
            gwait(qa)
            ccg = bd.collective(f"cc_b_g{half}_", "AllGather", ALU.bypass, G8,
                                [b_pub[half].ap().opt()],
                                [b_gathb[half].ap().opt()])
            gwait(ccg)

            def s16_of(b2, c):
                r, s = _rank_of(b2, c)
                return 2 * r + s

            for b2 in range(2):
                for c in range(NCHK):
                    cseg = ccur if c % 2 else cprev
                    psg = cprev if c % 2 else ccur
                    vwait(qa)
                    if c == 0:
                        bd.op("vector", "memset", cseg, 0.0, sem=ev)
                    else:
                        ps16 = s16_of(b2, c - 1)
                        gwait(ev)
                        qg.dma(dts_roll[c % 2],
                               b_gathb[half].ap()[ps16, 0:2048]
                               .rearrange("(p k) -> p k", p=128))
                        qg.dma(F_roll[c % 2],
                               b_gathb[half].ap()[ps16, 2048:]
                               .rearrange("(p k) -> p k", p=128))
                        qg.flip()
                        vwait(qg)
                        awaitx(qg, ev)
                        for fbq in range(16):
                            fb = 16 * half + fbq
                            awaitx(ev)
                            bd.op("scalar", "activation", Ptmp,
                                  b_alog_s[:, NST * fb:NST * (fb + 1)], AF.Exp,
                                  scale=dts_roll[c % 2][:, fbq:fbq + 1], sem=ea)
                            vwait(ea)
                            bd.op("vector", "tensor_mul",
                                  cseg[:, 16 * fbq:16 * (fbq + 1)],
                                  psg[:, 16 * fbq:16 * (fbq + 1)], Ptmp)
                            bd.op("vector", "tensor_add",
                                  cseg[:, 16 * fbq:16 * (fbq + 1)],
                                  cseg[:, 16 * fbq:16 * (fbq + 1)],
                                  F_roll[c % 2][:, 16 * fbq:16 * (fbq + 1)],
                                  sem=ev)
                    awaitx(ev)
                    if c <= 3:
                        j, ci = 4 * b2 + c, 0
                    else:
                        j, ci = 4 * b2 + 7 - c, 1
                    qa.dma(b_sel_send[half].ap()[j, ci]
                           .rearrange("(p k) -> p k", p=128), cseg)
                    qa.flip()
            gwait(qa)
            ccs = bd.collective(f"cc_b_s{half}_", "AllToAll", ALU.bypass, G8,
                                [b_sel_send[half].ap().opt()],
                                [b_sel_recv[half].ap().opt()])
            gwait(ccs)
            gwait(ev)
            for ci in range(2):
                qg.dma(myc[ci],
                       b_sel_recv[half].ap()[0, ci]
                       .rearrange("(p k) -> p k", p=128))
            qg.flip()
            vwait(qg)
            awaitx(qg)
            swait(qa)
            DTb = dt_fb
            for fbq in range(16):
                fb = 16 * half + fbq
                dt_softplus(fb, False)
                for tt in range(2):
                    vwait(ev)
                    bd.op("vector", "tensor_tensor_scan",
                          Ht[:, tt * CH:(tt + 1) * CH], ones_f,
                          dt_fb[:, tt * CH:(tt + 1) * CH], 0.0,
                          ALU.mult, ALU.add, sem=ev)
                bd.wait("sync", ep, ep.n)
                bd.wait("sync", ev, ev.n)
                qs.dma(C_sb, C_d.ap()[fb])
                qs.dma(z_bf, sz_d.ap()[fb])
                qs.flip()
                pwait(qs)
                btb = [bt_, bt2_]
                mka, mkC = {}, {}
                _pv = bd.sems["ev"].n
                _pg = bd.sems["eg"].n
                mkstt = {-2: _pv, -1: _pv}
                mkg3f = {-2: _pg, -1: _pg}
                pwait(ea)

                def prep_fx(n, fb=fb):
                    CP = (1, 2) if n % 2 == 0 else (3, 4)
                    bd.wait("scalar", ev, mkstt[n - 2])
                    bd.op("scalar", "activation", at2[n % 2], Ht, AF.Exp,
                          scale=b_alog_s[:, NST * fb + n: NST * fb + n + 1],
                          sem=ea)
                    mka[n] = ea.n
                    bd.wait("tensor", ev, mkstt[n - 2])
                    for tt in range(2):
                        bd.op("tensor", "matmul", pbank(CP[tt]),
                              selc_s[n], C_sb[:, tt * CH:(tt + 1) * CH],
                              start=True, stop=True, sem=ep)
                    mkC[n] = ep.n

                prep_fx(0)
                for n in range(NST):
                    if n + 1 < NST:
                        prep_fx(n + 1)
                    CP = (1, 2) if n % 2 == 0 else (3, 4)
                    bd.wait("vector", ea, mka[n])
                    bd.wait("vector", ep, mkC[n])
                    bd.wait("vector", eg, mkg3f[n - 2])
                    for tt in range(2):
                        bd.op("vector", "scalar_tensor_tensor",
                              btb[n % 2][:, tt * CH:(tt + 1) * CH],
                              at2[n % 2][:, tt * CH:(tt + 1) * CH],
                              myc[tt][:, 16 * fbq + n: 16 * fbq + n + 1],
                              pbank(CP[tt]), ALU.mult, ALU.mult, sem=ev)
                    mkstt[n] = bd.sems["ev"].n
                    bd.wait("gpsimd", ev, mkstt[n])
                    if n == 0:
                        bd.op("gpsimd", "tensor_copy", fixsum, btb[0])
                    else:
                        bd.op("gpsimd", "tensor_add", fixsum, fixsum,
                              btb[n % 2])
                    mkg3f[n] = bd.sems["eg"].n
                vwait(qs)
                vwait(eg)
                for tt in range(2):
                    sl = slice(tt * CH, (tt + 1) * CH)
                    ysl = ysl_ap(fbq, tt)
                    bd.op("vector", "tensor_mul", bt_[:, sl], fixsum[:, sl],
                          z_bf[:, sl])
                    bd.op("vector", "tensor_add", ysl, ysl, bt_[:, sl], sem=ev)
            for m in range(NF):
                wb = wbuf[m % 2]
                gwait(ep)
                qg.dma(wb.rearrange("p (k mm) -> p k mm", k=NF),
                       b_outprojT[2048 * half:2048 * (half + 1),
                                  128 * m:128 * (m + 1)]
                       .rearrange("(k p) mm -> p k mm", p=128))
                qg.flip()
                pwait(qg)
                pwait(ev)
                for tt in range(2):
                    dbank = 4 + tt
                    sl = slice(tt * CH, (tt + 1) * CH)
                    pwait(ev)
                    for k in range(16):
                        bd.op("tensor", "matmul", pbank(dbank),
                              wb[:, 128 * k:128 * (k + 1)], ysl_ap(k, tt),
                              start=(k == 0), stop=(k == 15),
                              sem=(ep if k == 15 else None))
                    vwait(ep)
                    bd.op("vector", "tensor_add", xt(m)[:, sl], xt(m)[:, sl],
                          pbank(dbank), sem=ev)
        scr.pop()
        barrier()

    def phase_out():
        swait(ev, ea, ep, eg)
        for f in range(NF):
            qs.dma(y0[128 * f:128 * (f + 1), :], xt(f))
        qs.wait("sync")

    phases = [phase0, phase1, lambda: phase_ffn('f1'), phase_b,
              lambda: phase_ffn('f2')]
    for i, ph in enumerate(phases):
        if i < n_phases:
            ph()
    phase_out()

    block = es.enter_context(nc.Block())

    @block.sync
    def _(e):
        for fn in bd.ops["sync"]:
            fn(e)

    @block.tensor
    def _(e):
        for fn in bd.ops["tensor"]:
            fn(e)

    @block.vector
    def _(e):
        for fn in bd.ops["vector"]:
            fn(e)

    @block.scalar
    def _(e):
        for fn in bd.ops["scalar"]:
            fn(e)

    @block.gpsimd
    def _(e):
        for fn in bd.ops["gpsimd"]:
            fn(e)

    es.close()
    return nc



def _host_constants():
    selx = np.zeros((32, 128), np.float32)
    for ch in range(128):
        selx[(ch // 64) * 16 + (ch % 16), ch] = 1.0
    selb = np.zeros((NST, 32, 128), np.float32)
    for n in range(NST):
        for ch in range(128):
            selb[n, (ch // 64) * 16 + n, ch] = 1.0
    selc = np.zeros((NST, 128, 128), np.float32)
    for n in range(NST):
        for ch in range(128):
            selc[n, (ch // 16) * 16 + n, ch] = 1.0
    selone = np.zeros((16, 16, 128), np.float32)
    for n in range(16):
        selone[n, n, :] = 1.0
    trimask = np.zeros((4, 128, CH), np.float32)
    for s in range(4):
        for k in range(128):
            kg = 128 * s + k
            trimask[s, k, :kg] = BIG_NEG
    return selx, selb, selc, selone, trimask


def prepare_inputs(full):
    f32 = np.float32
    cT = np.ascontiguousarray
    x = np.asarray(full["L0_attn_a_x"], f32)
    selx, selb, selc, selone, trimask = _host_constants()
    shared = {
        "a_xprojT": cT(np.asarray(full["L0_attn_a_W_xproj"], f32).T),
        "a_dtT": cT(np.asarray(full["L0_attn_a_W_dt"], f32).T),
        "a_bdt": cT(np.asarray(full["L0_attn_a_b_dt"], f32).reshape(NF, 128, 1)),
        "a_alog": cT(np.asarray(full["L0_attn_a_A_log"], f32).reshape(NF, 128, NSA)),
        "a_Dp": cT(np.asarray(full["L0_attn_a_D_param"], f32).reshape(NF, 128, 1)),
        "b_inprojT": cT(np.asarray(full["L0_attn_b_in_proj_w"], f32).T),
        "b_dtinT": cT(np.asarray(full["L0_attn_b_dt_in_proj_w"], f32).T),
        "b_dtprojT": cT(np.asarray(full["L0_attn_b_dt_proj_w"], f32).T),
        "b_dtb": cT(np.asarray(full["L0_attn_b_dt_proj_b"], f32).reshape(NFB, 128, 1)),
        "b_convw": cT(np.asarray(full["L0_attn_b_conv_w"], f32).reshape(NFB, 128, 4)),
        "b_convb": cT(np.asarray(full["L0_attn_b_conv_b"], f32).reshape(NFB, 128, 1)),
        "b_alog": cT(np.asarray(full["L0_attn_b_A_log"], f32).reshape(NFB, 128, NST)),
        "b_Dp": cT(np.asarray(full["L0_attn_b_D"], f32).reshape(NFB, 128, 1)),
        "b_outprojT": cT(np.asarray(full["L0_attn_b_out_proj_w"], f32).T),
        "f_WqT": cT(np.asarray(full["L0_ffn_Wq"], f32).T),
        "f_WkT": cT(np.asarray(full["L0_ffn_Wk"], f32).T),
        "f_WvT": cT(np.asarray(full["L0_ffn_Wv"], f32).T),
        "f_Wo": cT(np.asarray(full["L0_ffn_Wo"], f32)),
        "f_A": cT(np.asarray(full["L0_ffn_A"], f32)),
        "f_B": cT(np.asarray(full["L0_ffn_B"], f32)),
        "f_ln1": cT(np.asarray(full["L0_ffn_ln1_w"], f32).reshape(NF, 128, 1)),
        "f_ln2": cT(np.asarray(full["L0_ffn_ln2_w"], f32).reshape(NF, 128, 1)),
        "f_WinT": cT(np.asarray(full["L0_ffn_W_in_base"], f32).T),
        "f_WupT": cT(np.asarray(full["L0_ffn_W_up_base"], f32).T),
        "f_WdownT": cT(np.asarray(full["L0_ffn_W_down"], f32).T),
        "selx": selx, "selb": selb, "selc": selc, "selone": selone,
        "trimask": trimask,
    }
    in_maps = []
    for c in range(8):
        b = c // 4
        jA, jB = c % 4, 7 - c % 4
        xc = np.concatenate([x[b, CH * jA:CH * (jA + 1)],
                             x[b, CH * jB:CH * (jB + 1)]], axis=0)
        m = {"x0": np.ascontiguousarray(xc.T)}
        m.update(shared)
        in_maps.append(m)
    return in_maps


def gather_output(results):
    y = np.empty((NB, 4096, D), np.float32)
    for c in range(8):
        b = c // 4
        jA, jB = c % 4, 7 - c % 4
        out = results[c]["y0"]
        y[b, CH * jA:CH * (jA + 1)] = out[:, :CH].T
        y[b, CH * jB:CH * (jB + 1)] = out[:, CH:].T
    return y


_NC_CACHE = {}


def kernel(**inputs):
    key = N_PHASES
    if key not in _NC_CACHE:
        _NC_CACHE[key] = build_program(N_PHASES)
    nc = _NC_CACHE[key]
    in_maps = prepare_inputs(inputs)
    res = run_bass_kernel_spmd(nc, in_maps, core_ids=list(range(8)))
    return gather_output(res.results)
